# version 02
# baseline (speedup 1.0000x reference)
import math
from contextlib import ExitStack

import numpy as np
import ml_dtypes

import concourse.bass as bass
import concourse.mybir as mybir
from concourse.bass_utils import run_bass_kernel_spmd

F32 = mybir.dt.float32
BF16 = mybir.dt.bfloat16
I32 = mybir.dt.int32
AF = mybir.ActivationFunctionType
ALU = mybir.AluOpType
AX = mybir.AxisListType.X
BF = ml_dtypes.bfloat16

B, T, C, NH, HD = 2, 4096, 4096, 32, 128
NCORE = 8
TPC = 1024
NCH = C // 128
SOFT_B = 20.0
SCALE = 1.0 / math.sqrt(HD)
EPS = 1e-5


def _tok_idx(j):
    a0 = np.arange(j * 512, (j + 1) * 512)
    a1 = np.arange((7 - j) * 512, (8 - j) * 512)
    return np.concatenate([a0, a1])


def _wr_blocks(wT):
    return np.ascontiguousarray(
        wT.reshape(32, 128, 32, 128).transpose(2, 0, 1, 3))


def _rope_tables():
    pos = np.arange(T, dtype=np.float32)
    thetas = (1.0 / 10000.0 ** (np.arange(0, HD, 2, dtype=np.float32) / HD))
    pt = np.outer(pos, thetas)
    ptc = np.concatenate([pt, pt], axis=1)
    cosT = np.ascontiguousarray(np.cos(ptc).T)
    sinT = np.sin(ptc).T
    sign = np.where(np.arange(HD) < HD // 2, -1.0, 1.0).astype(np.float32)
    sin2 = np.ascontiguousarray(sinT * sign[:, None])
    return cosT.astype(np.float32), sin2.astype(np.float32)


def _consts():
    iota = np.broadcast_to(np.arange(512, dtype=np.float32), (128, 512)).copy()
    ident = np.eye(128, dtype=np.float32)
    ones_col = np.ones((128, 1), dtype=BF)
    ones_row_f32 = np.ones((1, 128), dtype=np.float32)
    f = np.arange(512)[None, :]
    p = np.arange(128)[:, None]
    tri = np.stack([(f >= 128 * i + p) for i in range(4)]).astype(BF)
    exp8 = np.zeros((8, 128), dtype=BF)
    for e in range(8):
        exp8[e, e * 16:(e + 1) * 16] = 1.0
    return dict(iota=iota, ident=ident, ones_col=ones_col,
                ones_row_f32=ones_row_f32, tri=tri, exp8=exp8)


def _tthr(j):
    out = np.zeros((128, 48), dtype=np.float32)
    p = np.arange(128, dtype=np.float32)
    col = 0
    for a, pa in ((0, 16), (1, 32)):
        gq = j if a == 0 else 7 - j
        for b in range(pa):
            out[:, col] = b * 128 + p - gq * 512
            col += 1
    return out



class P:

    def __init__(self, nc, name):
        self.nc = nc
        self.name = name
        self.es = ExitStack()
        self._sems = {}

    def sem(self, n):
        if n not in self._sems:
            self._sems[n] = self.nc.alloc_semaphore(f"{self.name}_{n}")
        return self._sems[n]

    def sb(self, n, shape, dtype):
        return self.es.enter_context(
            self.nc.sbuf_tensor(f"{self.name}_{n}", shape, dtype))

    def ps(self, n, shape, dtype=F32):
        return self.es.enter_context(
            self.nc.psum_tensor(f"{self.name}_{n}", shape, dtype))

    def close(self):
        self.es.close()



def emit_p0(nc, io):
    ph = P(nc, "p0")
    xTb, wq_r, wk_r = io["xTb"], io["wq_r"], io["wk_r"]
    ropec, ropes2 = io["ropec"], io["ropes2"]
    QT0, KT0, xbd = io["QT0"], io["KT0"], io["xbd"]

    land = [ph.sb(f"land{i}", [128, 8, 512], F32) for i in range(2)]
    xa = [ph.sb(f"xa{i}", [128, 8, 512], BF16) for i in range(2)]
    landw = ph.sb("landw", [128, 16, 128], F32)
    wqb = [ph.sb(f"wqb{i}", [128, 8192], BF16) for i in range(2)]
    wkb = [ph.sb(f"wkb{i}", [128, 8192], BF16) for i in range(2)]
    xb = [ph.sb(f"xb{i}", [128, 32, 512], BF16) for i in range(2)]
    scc = [ph.sb(f"scc{i}", [128, 512], F32) for i in range(2)]
    ssn = [ph.sb(f"ssn{i}", [128, 512], F32) for i in range(2)]
    qs = [ph.sb(f"qs{i}", [128, 512], F32) for i in range(2)]
    t1 = ph.sb("t1", [128, 512], F32)
    t2 = ph.sb("t2", [128, 512], F32)
    stg = [ph.sb(f"stg{i}", [128, 512], BF16) for i in range(4)]

    bqk = [ph.ps(f"bqk{i}", [128, 512]) for i in range(4)]

    a_ld = [ph.sem("a_ld0"), ph.sem("a_ld1")]
    a_st = [ph.sem("a_st0"), ph.sem("a_st1")]
    w_ld = ph.sem("w_ld")
    w_cv = ph.sem("w_cv")
    r_ld = [ph.sem("r_ld0"), ph.sem("r_ld1")]
    x_ld = [ph.sem("x_ld0"), ph.sem("x_ld1")]
    mm = ph.sem("mm")
    rp = ph.sem("rp")
    rpd = ph.sem("rpd")
    ev = ph.sem("ev")
    outq = [ph.sem("outq0"), ph.sem("outq1")]
    acv = ph.sem("acv")
    ch_ve = ph.sem("ch_ve")

    NSUB = 32
    NTRG = io.get("p0_ntrg", 32)

    with nc.Block("p0") as blk:

        SKIP_A = io.get("p0_skip_a", False)
        NO_ROPE = io.get("p0_no_rope", False)
        NO_STORE = io.get("p0_no_store", False)
        NO_ACT = io.get("p0_no_act", False)

        @blk.sync
        def _(sy):
            if SKIP_A:
                return
            for sub in range(NSUB):
                tr, q = sub // 4, sub % 4
                if sub >= 2:
                    sy.wait_ge(acv, sub - 1)
                sy.dma_start(
                    out=land[sub % 2][:, :, :],
                    in_=xTb[q * 1024:(q + 1) * 1024,
                            tr * 512:(tr + 1) * 512].rearrange(
                        "(i p) c -> p i c", p=128)).then_inc(a_ld[sub % 2], 16)

        @blk.scalar
        def _(sc):
            def load_w(hp):
                for w, src in enumerate((wq_r, wk_r)):
                    for pc in range(4):
                        idx = hp * 8 + w * 4 + pc
                        if idx > 0:
                            sc.wait_ge(w_cv, idx)
                        h, ir = pc // 2, (pc % 2) * 16
                        sc.dma_start(
                            out=landw[:, :, :],
                            in_=src[2 * hp + h, ir:ir + 16, :, :].rearrange(
                                "i p c -> p i c")).then_inc(w_ld, 16)

            def load_xb(trg):
                hp, tr = trg // 8, trg % 8
                if not SKIP_A:
                    sc.wait_ge(a_st[0], 16 * (2 * tr + 2))
                    sc.wait_ge(a_st[1], 16 * (2 * tr + 2))
                if trg >= 2:
                    sc.wait_ge(mm, trg - 1)
                sc.dma_start(
                    out=xb[trg % 2][:, :, :],
                    in_=xbd[:, tr * 512:(tr + 1) * 512].rearrange(
                        "(i p) c -> p i c", p=128)).then_inc(x_ld[trg % 2], 16)

            load_w(0)
            load_xb(0)
            load_xb(1)
            if not NO_ROPE:
                sc.dma_start(out=scc[0][:, :],
                             in_=ropec[:, 0:512]).then_inc(r_ld[0], 16)
                sc.dma_start(out=ssn[0][:, :],
                             in_=ropes2[:, 0:512]).then_inc(r_ld[0], 16)
                sc.dma_start(out=scc[1][:, :],
                             in_=ropec[:, 512:1024]).then_inc(r_ld[1], 16)
                sc.dma_start(out=ssn[1][:, :],
                             in_=ropes2[:, 512:1024]).then_inc(r_ld[1], 16)
            for trg in range(NTRG):
                hp, tr = trg // 8, trg % 8
                if trg < NTRG - 2:
                    load_xb(trg + 2)
                if tr == 4 and hp < 3 and (hp + 1) * 8 < NTRG:
                    load_w(hp + 1)
                if not NO_ROPE and trg + 2 < NTRG:
                    tr2 = (trg + 2) % 8
                    sc.wait_ge(ev, 4 * (trg + 1))
                    sc.dma_start(out=scc[trg % 2][:, :],
                                 in_=ropec[:, tr2 * 512:(tr2 + 1) * 512]
                                 ).then_inc(r_ld[trg % 2], 16)
                    sc.dma_start(out=ssn[trg % 2][:, :],
                                 in_=ropes2[:, tr2 * 512:(tr2 + 1) * 512]
                                 ).then_inc(r_ld[trg % 2], 16)

        @blk.gpsimd
        def _(gp):
            def p0a(sub):
                gp.wait_ge(a_ld[sub % 2], 16 * (sub // 2 + 1))
                if sub >= 2:
                    gp.wait_ge(a_st[sub % 2], 16 * (sub // 2))
                tr, q = sub // 4, sub % 4
                gp.tensor_copy(out=xa[sub % 2][:, :, :],
                               in_=land[sub % 2][:, :, :]).then_inc(acv, 1)
                gp.wait_ge(acv, sub + 1)
                gp.dma_start(
                    out=xbd[q * 1024:(q + 1) * 1024,
                            tr * 512:(tr + 1) * 512].rearrange(
                        "(i p) c -> p i c", p=128),
                    in_=xa[sub % 2][:, :, :]).then_inc(a_st[sub % 2], 16)

            if not SKIP_A:
                for sub in range(12):
                    p0a(sub)
            for trg in range(NTRG):
                if not SKIP_A and trg < 5:
                    for sub in range(12 + 4 * trg, 16 + 4 * trg):
                        p0a(sub)
                hp, tr = trg // 8, trg % 8
                if NO_STORE:
                    continue
                for bk in range(4):
                    head = 2 * hp + (bk % 2)
                    dst = QT0 if bk < 2 else KT0
                    gp.wait_ge(ev, trg * 4 + bk + 1)
                    gp.dma_start(out=dst[head, :, tr * 512:(tr + 1) * 512],
                                 in_=stg[bk][:, :]).then_inc(outq[trg % 2], 16)
            if not NO_STORE:
                gp.wait_ge(outq[0], 64 * ((NTRG + 1) // 2))
                gp.wait_ge(outq[1], 64 * (NTRG // 2))
            gp.wait_ge(ev, 4 * NTRG)
            if not SKIP_A:
                gp.wait_ge(a_st[0], 16 * 16)
                gp.wait_ge(a_st[1], 16 * 16)

        @blk.vector
        def _(ve):
            def conv_w(hp):
                for pc in range(8):
                    idx = hp * 8 + pc
                    ve.wait_ge(w_ld, 16 * (idx + 1))
                    if hp >= 2:
                        ve.wait_ge(mm, 8 * (hp - 1))
                    dst = wqb[hp % 2] if pc < 4 else wkb[hp % 2]
                    off = (pc % 4) * 2048
                    ve.tensor_copy(
                        out=dst[:, off:off + 2048].rearrange(
                            "p (i c) -> p i c", i=16),
                        in_=landw[:, :, :]).then_inc(w_cv, 1)

            conv_w(0)
            for trg in range(NTRG):
                hp, tr = trg // 8, trg % 8
                if tr == 4 and hp < 3 and (hp + 1) * 8 < NTRG:
                    conv_w(hp + 1)
                ve.wait_ge(mm, trg + 1)
                if not NO_ROPE:
                    ve.wait_ge(r_ld[trg % 2], 32 * (trg // 2 + 1))
                if not NO_STORE and trg >= 1:
                    ve.wait_ge(outq[(trg - 1) % 2], 64 * ((trg - 1) // 2 + 1))
                for bk in range(4):
                    g = trg * 4 + bk
                    if NO_ROPE:
                        ve.tensor_copy(out=stg[bk][:, :],
                                       in_=bqk[bk][:, :]).then_inc(ev, 1)
                        continue
                    if g >= 1:
                        ve.wait_ge(ev, g)
                    if g >= 2:
                        ve.wait_ge(rpd, g - 1)
                    ve.tensor_copy(out=qs[bk % 2][0:64, :],
                                   in_=bqk[bk][64:128, :])
                    ve.tensor_copy(out=qs[bk % 2][64:128, :],
                                   in_=bqk[bk][0:64, :]).then_inc(rp, 1)
                    ve.tensor_tensor(out=t1[:, :], in0=bqk[bk][:, :],
                                     in1=scc[trg % 2][:, :],
                                     op=ALU.mult).then_inc(ch_ve, 1)
                    ve.wait_ge(rp, g + 1)
                    ve.tensor_tensor(out=t2[:, :], in0=qs[bk % 2][:, :],
                                     in1=ssn[trg % 2][:, :],
                                     op=ALU.mult).then_inc(rpd, 1)
                    ve.wait_ge(ch_ve, g + 1)
                    ve.wait_ge(rpd, g + 1)
                    ve.tensor_tensor(out=stg[bk][:, :], in0=t1[:, :], in1=t2[:, :],
                                     op=ALU.add).then_inc(ev, 1)

        @blk.tensor
        def _(te):
            for trg in range(NTRG):
                hp, tr = trg // 8, trg % 8
                te.wait_ge(w_cv, 8 * hp + 8)
                te.wait_ge(x_ld[trg % 2], 16 * (trg // 2 + 1))
                if trg >= 1:
                    te.wait_ge(ev, 4 * trg)
                for i in range(32):
                    st = (i == 0)
                    sp = (i == 31)
                    for hh in range(2):
                        te.matmul(out=bqk[hh][:, :],
                                  lhsT=wqb[hp % 2][:, (hh * 32 + i) * 128:
                                                   (hh * 32 + i) * 128 + 128],
                                  rhs=xb[trg % 2][:, i, :], start=st, stop=sp)
                        ins = te.matmul(out=bqk[2 + hh][:, :],
                                        lhsT=wkb[hp % 2][:, (hh * 32 + i) * 128:
                                                         (hh * 32 + i) * 128 + 128],
                                        rhs=xb[trg % 2][:, i, :], start=st, stop=sp)
                    if i == 31:
                        ins.then_inc(mm, 1)

    ph.close()



def emit_p0v(nc, io):
    ph = P(nc, "p0v")
    wv_r, V0, xbd = io["wv_r"], io["V0"], io["xbd"]

    landw = ph.sb("landw", [128, 4, 512], F32)
    swv = ph.sb("swv", [128, 16384], BF16)
    xb = [ph.sb(f"xb{i}", [128, 32, 512], BF16) for i in range(2)]
    vst = [ph.sb(f"vst{i}", [128, 512], BF16) for i in range(4)]

    bv = [ph.ps(f"bv{i}", [128, 512]) for i in range(4)]

    w_ld = ph.sem("w_ld")
    w_cv = ph.sem("w_cv")
    x_ld = [ph.sem("x_ld0"), ph.sem("x_ld1")]
    mm = ph.sem("mm")
    ev = ph.sem("ev")
    outq = [ph.sem("outq0"), ph.sem("outq1")]

    with nc.Block("p0v") as blk:

        @blk.scalar
        def _(sc):
            for op in range(2):
                for pc in range(8):
                    idx = op * 8 + pc
                    if idx > 0:
                        sc.wait_ge(w_cv, idx)
                    if op == 1 and pc == 0:
                        sc.wait_ge(mm, 8)
                    sc.dma_start(
                        out=landw[:, :, :],
                        in_=wv_r[pc * 512:(pc + 1) * 512,
                                 op * 512:(op + 1) * 512].rearrange(
                            "(i p) c -> p i c", p=128)).then_inc(w_ld, 16)

        @blk.vector
        def _(ve):
            def conv_v(op):
                for pc in range(8):
                    idx = op * 8 + pc
                    ve.wait_ge(w_ld, 16 * (idx + 1))
                    if op == 1 and pc == 0:
                        ve.wait_ge(mm, 8)
                    ve.tensor_copy(
                        out=swv[:, pc * 2048:(pc + 1) * 2048].rearrange(
                            "p (i c) -> p i c", i=4),
                        in_=landw[:, :, :]).then_inc(w_cv, 1)

            conv_v(0)
            for trg in range(16):
                ve.wait_ge(mm, trg + 1)
                if trg >= 1:
                    ve.wait_ge(outq[(trg - 1) % 2], 256 * ((trg - 1) // 2 + 1))
                for tt in range(4):
                    ve.tensor_copy(out=vst[tt][:, :],
                                   in_=bv[tt][:, :]).then_inc(ev, 1)
                if trg == 7:
                    conv_v(1)

        @blk.sync
        def _(sy):
            for trg in range(16):
                tr = trg % 8
                if trg >= 2:
                    sy.wait_ge(mm, trg - 1)
                sy.dma_start(
                    out=xb[trg % 2][:, :, :],
                    in_=xbd[:, tr * 512:(tr + 1) * 512].rearrange(
                        "(i p) c -> p i c", p=128)).then_inc(x_ld[trg % 2], 16)

        @blk.tensor
        def _(te):
            for trg in range(16):
                op, tr = trg // 8, trg % 8
                te.wait_ge(w_cv, 8 * (op + 1))
                te.wait_ge(x_ld[trg % 2], 16 * (trg // 2 + 1))
                if trg >= 1:
                    te.wait_ge(ev, 4 * trg)
                for i in range(32):
                    st = (i == 0)
                    sp = (i == 31)
                    for tt in range(4):
                        ins = te.matmul(out=bv[tt][:, :],
                                        lhsT=xb[trg % 2][:, i, tt * 128:(tt + 1) * 128],
                                        rhs=swv[:, i * 512:(i + 1) * 512],
                                        start=st, stop=sp)
                    if i == 31:
                        ins.then_inc(mm, 1)

        @blk.gpsimd
        def _(gp):
            for trg in range(16):
                op, tr = trg // 8, trg % 8
                for tt in range(4):
                    gp.wait_ge(ev, trg * 4 + tt + 1)
                    for hh in range(4):
                        gp.dma_start(
                            out=V0[op * 4 + hh,
                                   tr * 512 + tt * 128:tr * 512 + (tt + 1) * 128, :],
                            in_=vst[tt][:, hh * 128:(hh + 1) * 128]
                            ).then_inc(outq[trg % 2], 16)
            gp.wait_ge(outq[0], 256 * 8)
            gp.wait_ge(outq[1], 256 * 8)

    ph.close()



def emit_p1(nc, io):
    ph = P(nc, "p1")
    QT0, KT0, V0, OTh = io["QT0"], io["KT0"], io["V0"], io["OTh"]
    tri, ones_col, ones_row = io["tri"], io["ones_col"], io["ones_row"]

    sk = [ph.sb(f"sk{i}", [128, T], BF16) for i in range(2)]
    sv = [ph.sb(f"sv{i}", [128, 32, 128], BF16) for i in range(2)]
    sq = [ph.sb(f"sq{i}", [128, 512], BF16) for i in range(2)]
    sp = [ph.sb(f"sp{i}", [128, 512], BF16) for i in range(2)]
    spm = [ph.sb(f"spm{i}", [128, 512], BF16) for i in range(2)]
    stri = ph.sb("stri", [128, 4, 512], BF16)
    sones = ph.sb("sones", [128, 1], BF16)
    sonesr = ph.sb("sonesr", [1, 128], F32)
    srcp = ph.sb("srcp", [1, 512], F32)
    srb = ph.sb("srb", [128, 512], F32)
    ostg = [ph.sb(f"ostg{i}", [128, 512], BF16) for i in range(2)]
    sbias = ph.sb("sbias", [128, 1], F32)

    bO = ph.ps("bO", [128, 512])
    bS = [ph.ps(f"bS{i}", [128, 512]) for i in range(2)]
    bD = ph.ps("bD", [1, 512])
    bRB = ph.ps("bRB", [128, 512])

    kv_ld = [ph.sem("kv_ld0"), ph.sem("kv_ld1")]
    c_ld = ph.sem("c_ld")
    q_ld = [ph.sem("q_ld0"), ph.sem("q_ld1")]
    s_mm = ph.sem("s_mm")
    s_pv = ph.sem("s_pv")
    s_dn = ph.sem("s_dn")
    s_bc = ph.sem("s_bc")
    a_ex = ph.sem("a_ex")
    v_mk = ph.sem("v_mk")
    v_rcp = ph.sem("v_rcp")
    v_ev = ph.sem("v_ev")
    p_st = [ph.sem("p_st0"), ph.sem("p_st1")]
    v_ini = ph.sem("v_ini")
    ch_ve = ph.sem("ch_ve")

    NHD = 8
    NQ = 8
    KM = [4 * (a + 1) for a in range(NQ)]
    GPH = sum(KM)

    def g_of(h, a, b):
        return h * GPH + sum(KM[:a]) + b

    def m_of(h, a, i):
        return (h * NQ + a) * 4 + i

    with nc.Block("p1") as blk:

        @blk.sync
        def _(sy):
            sy.dma_start(out=stri[:, :, :],
                         in_=tri[:, :, :].rearrange("t p c -> p t c")).then_inc(c_ld, 16)
            sy.dma_start(out=sones[:, :], in_=ones_col[:, :]).then_inc(c_ld, 16)
            sy.dma_start(out=sonesr[:, :], in_=ones_row[:, :]).then_inc(c_ld, 16)
            for h in range(NHD):
                if h >= 2:
                    sy.wait_ge(s_dn, NQ * (h - 1))
                sy.dma_start(out=sk[h % 2][:, :],
                             in_=KT0[h, :, :]).then_inc(kv_ld[h % 2], 16)
                for part in range(4):
                    sy.dma_start(
                        out=sv[h % 2][:, part * 8:(part + 1) * 8, :],
                        in_=V0[h, part * 1024:(part + 1) * 1024, :].rearrange(
                            "(kc p) d -> p kc d", p=128)).then_inc(kv_ld[h % 2], 16)
                for a in range(NQ):
                    ha = h * NQ + a
                    if ha >= 2:
                        sy.wait_ge(s_dn, ha - 1)
                    sy.dma_start(out=sq[ha % 2][:, :],
                                 in_=QT0[h, :, a * 512:(a + 1) * 512]
                                 ).then_inc(q_ld[ha % 2], 16)

        @blk.tensor
        def _(te):
            for h in range(NHD):
                for a in range(NQ):
                    ha = h * NQ + a
                    km = KM[a]
                    for b in range(km):
                        g = g_of(h, a, b)
                        if b == 0:
                            te.wait_ge(c_ld, 48)
                            te.wait_ge(kv_ld[h % 2], 80 * (h // 2 + 1))
                            te.wait_ge(q_ld[ha % 2], 16 * (ha // 2 + 1))
                            te.wait_ge(v_ev, ha)
                            te.wait_ge(v_rcp, ha)
                        if g >= 2:
                            te.wait_ge(a_ex, g - 1)
                        te.matmul(out=bS[g % 2][:, :],
                                  lhsT=sk[h % 2][:, b * 128:(b + 1) * 128],
                                  rhs=sq[ha % 2][:, :], start=True, stop=True
                                  ).then_inc(s_mm, 1)
                        diag = b >= km - 4
                        if diag:
                            te.wait_ge(v_mk, m_of(h, a, b - (km - 4)) + 1)
                            rhs_p = spm[g % 2][:, :]
                        else:
                            te.wait_ge(a_ex, g + 1)
                            rhs_p = sp[g % 2][:, :]
                        te.matmul(out=bO[:, :],
                                  lhsT=sv[h % 2][:, b, :], rhs=rhs_p,
                                  start=(b == 0), stop=(b == km - 1)
                                  ).then_inc(s_pv, 1)
                        ins_d = te.matmul(out=bD[:, :], lhsT=sones[:, :], rhs=rhs_p,
                                          start=(b == 0), stop=(b == km - 1),
                                          skip_group_check=True)
                        if b == km - 1:
                            ins_d.then_inc(s_dn, 1)
                    te.wait_ge(v_rcp, ha + 1)
                    te.wait_ge(v_ev, ha)
                    te.matmul(out=bRB[:, :], lhsT=sonesr[:, :], rhs=srcp[:, :],
                              start=True, stop=True).then_inc(s_bc, 1)

        @blk.scalar
        def _(sc):
            sc.wait_ge(v_ini, 1)
            for h in range(NHD):
                for a in range(NQ):
                    km = KM[a]
                    for b in range(km):
                        g = g_of(h, a, b)
                        sc.wait_ge(s_mm, g + 1)
                        if g >= 2:
                            sc.wait_ge(s_pv, g - 1)
                        sc.activation(out=sp[g % 2][:, :], in_=bS[g % 2][:, :],
                                      func=AF.Exp, scale=SCALE,
                                      bias=sbias[:, 0:1]).then_inc(a_ex, 1)

        @blk.vector
        def _(ve):
            ve.memset(sbias[:, :], -SOFT_B).then_inc(v_ini, 1)
            for h in range(NHD):
                for a in range(NQ):
                    ha = h * NQ + a
                    km = KM[a]
                    for b in range(km - 4, km):
                        g = g_of(h, a, b)
                        i = b - (km - 4)
                        ve.wait_ge(a_ex, g + 1)
                        if g >= 2:
                            ve.wait_ge(s_pv, g - 1)
                        ve.tensor_tensor(out=spm[g % 2][:, :], in0=sp[g % 2][:, :],
                                         in1=stri[:, i, :],
                                         op=ALU.mult).then_inc(v_mk, 1)
                    ve.wait_ge(s_dn, ha + 1)
                    ve.reciprocal(out=srcp[:, :], in_=bD[0:1, :]).then_inc(v_rcp, 1)
                    ve.wait_ge(s_bc, ha + 1)
                    ve.tensor_copy(out=srb[:, :], in_=bRB[:, :]).then_inc(ch_ve, 1)
                    if ha >= 2:
                        ve.wait_ge(p_st[ha % 2], 16 * (ha // 2))
                    ve.wait_ge(ch_ve, ha + 1)
                    ve.tensor_tensor(out=ostg[ha % 2][:, :], in0=bO[:, :],
                                     in1=srb[:, :], op=ALU.mult).then_inc(v_ev, 1)

        @blk.gpsimd
        def _(gp):
            for h in range(NHD):
                for a in range(NQ):
                    ha = h * NQ + a
                    gp.wait_ge(v_ev, ha + 1)
                    gp.dma_start(out=OTh[h * 128:(h + 1) * 128,
                                         a * 512:(a + 1) * 512],
                                 in_=ostg[ha % 2][:, :]).then_inc(p_st[ha % 2], 16)
            gp.wait_ge(p_st[0], 16 * 32)
            gp.wait_ge(p_st[1], 16 * 32)

    ph.close()


def _dram(nc, name, shape, dtype, kind):
    return nc.dram_tensor(name, shape, dtype, kind=kind).ap()


def build_launch1(nphases=3):
    nc = bass.Bass("TRN2", target_bir_lowering=False)
    io = {}
    io["xTb"] = _dram(nc, "xTb", [C, T], F32, "ExternalInput")
    io["wq_r"] = _dram(nc, "wq_r", [8, 32, 128, 128], F32, "ExternalInput")
    io["wk_r"] = _dram(nc, "wk_r", [8, 32, 128, 128], F32, "ExternalInput")
    io["wv_r"] = _dram(nc, "wv_r", [C, 1024], F32, "ExternalInput")
    io["ropec"] = _dram(nc, "ropec", [128, T], F32, "ExternalInput")
    io["ropes2"] = _dram(nc, "ropes2", [128, T], F32, "ExternalInput")
    io["tri"] = _dram(nc, "tri", [4, 128, 512], BF16, "ExternalInput")
    io["ones_col"] = _dram(nc, "ones_col", [128, 1], BF16, "ExternalInput")
    io["ones_row"] = _dram(nc, "ones_row", [1, 128], F32, "ExternalInput")
    io["xbd"] = _dram(nc, "xbd", [C, T], BF16, "Internal")
    io["QT0"] = _dram(nc, "QT0", [8, 128, T], BF16, "Internal")
    io["KT0"] = _dram(nc, "KT0", [8, 128, T], BF16, "Internal")
    io["V0"] = _dram(nc, "V0", [8, T, 128], BF16, "Internal")
    io["OTh"] = _dram(nc, "OTh", [1024, T], BF16, "ExternalOutput")
    emit_p0(nc, io)
    if nphases >= 2:
        nc.all_engine_barrier(sem_only=True)
        emit_p0v(nc, io)
    if nphases >= 3:
        nc.all_engine_barrier(sem_only=True)
        emit_p1(nc, io)
    return nc


def prep_launch1(inputs):
    x = np.asarray(inputs["L0_attn_x"])
    wq = np.asarray(inputs["L0_attn_wq"])
    wk = np.asarray(inputs["L0_attn_wk"])
    wv = np.asarray(inputs["L0_attn_wv"])
    cosT, sin2 = _rope_tables()
    cst = _consts()
    xT = [np.ascontiguousarray(x[b].T) for b in range(B)]
    wq_all = _wr_blocks(np.ascontiguousarray(wq.T))
    wk_all = _wr_blocks(np.ascontiguousarray(wk.T))
    wvT = np.ascontiguousarray(wv.T)
    in_maps = []
    for c in range(NCORE):
        b, g = c // 4, c % 4
        in_maps.append({
            "xTb": xT[b],
            "wq_r": np.ascontiguousarray(wq_all[g * 8:(g + 1) * 8]),
            "wk_r": np.ascontiguousarray(wk_all[g * 8:(g + 1) * 8]),
            "wv_r": np.ascontiguousarray(wvT[:, g * 1024:(g + 1) * 1024]),
            "ropec": cosT, "ropes2": sin2,
            "tri": cst["tri"], "ones_col": cst["ones_col"],
            "ones_row": cst["ones_row_f32"],
        })
    return in_maps


def emit_woln(nc, io, name, OT, W_r, XRES, XOUT, XPRE):
    ph = P(nc, name)
    ones_col, ones_row = io["ones_col"], io["ones_row"]

    so = ph.sb("so", [128, 32, 1024], BF16)
    landw = [ph.sb(f"landw{i}", [128, 32, 128], F32) for i in range(2)]
    swb = [ph.sb(f"swb{i}", [128, 4096], BF16) for i in range(2)]
    sx = [ph.sb(f"sx{i}", [128, 1024], F32) for i in range(2)]
    st_t = [ph.sb(f"stt{i}", [128, 1024], F32) for i in range(2)]
    stb = [ph.sb(f"stb{i}", [128, 1024], BF16) for i in range(2)]
    stsq = [ph.sb(f"stsq{i}", [128, 1024], BF16) for i in range(2)]
    sonesc = ph.sb("sonesc", [128, 1], BF16)
    sonesr = ph.sb("sonesr", [1, 128], F32)
    srow = ph.sb("srow", [1, 4096], F32)
    srow2 = ph.sb("srow2", [1, 4096], F32)
    sra = ph.sb("sra", [128, 1024], F32)
    srb2 = ph.sb("srb2", [128, 1024], F32)
    land2 = [ph.sb(f"land2{i}", [128, 1024], F32) for i in range(2)]
    tmp2 = ph.sb("tmp2", [128, 1024], F32)
    stage2 = [ph.sb(f"stage2{i}", [128, 1024], F32) for i in range(2)]
    eps_t = ph.sb("eps_t", [1, 1], F32)

    yb = [ph.ps(f"yb{i}", [128, 512]) for i in range(4)]
    ssum = [ph.ps(f"ssum{i}", [1, 512]) for i in range(2)]
    ssq = [ph.ps(f"ssq{i}", [1, 512]) for i in range(2)]

    c_ld = ph.sem("c_ld")
    w_ld = [ph.sem("w_ld0"), ph.sem("w_ld1")]
    w_cv = ph.sem("w_cv")
    x_ld = [ph.sem("x_ld0"), ph.sem("x_ld1")]
    p_mm = ph.sem("p_mm")
    p_sd = ph.sem("p_sd")
    p_bc = ph.sem("p_bc")
    v_t = ph.sem("v_t")
    v_st = ph.sem("v_st")
    a_sq = ph.sem("a_sq")
    p_pre = [ph.sem("p_pre0"), ph.sem("p_pre1")]
    v_row = ph.sem("v_row")
    a_row = ph.sem("a_row")
    v_bc = ph.sem("v_bc")
    x2_ld = [ph.sem("x2_ld0"), ph.sem("x2_ld1")]
    v_p2 = ph.sem("v_p2")
    v_p3 = ph.sem("v_p3")
    p_out = [ph.sem("p_out0"), ph.sem("p_out1")]
    ini = ph.sem("ini")
    cch = ph.sem("cch")

    NOB = 32

    with nc.Block(name) as blk:

        @blk.sync
        def _(sy):
            for pc in range(4):
                sy.dma_start(
                    out=so[:, pc * 8:(pc + 1) * 8, :],
                    in_=OT[pc * 1024:(pc + 1) * 1024, :].rearrange(
                        "(cc p) t -> p cc t", p=128)).then_inc(c_ld, 16)
            sy.dma_start(out=sonesc[:, :], in_=ones_col[:, :]).then_inc(c_ld, 16)
            sy.dma_start(out=sonesr[:, :], in_=ones_row[:, :]).then_inc(c_ld, 16)
            for ob in range(NOB):
                if ob >= 2:
                    sy.wait_ge(w_cv, ob - 1)
                for hh in range(2):
                    sy.dma_start(
                        out=landw[ob % 2][:, hh * 16:(hh + 1) * 16, :],
                        in_=W_r[ob, hh * 16:(hh + 1) * 16, :, :].rearrange(
                            "i p c -> p i c")).then_inc(w_ld[ob % 2], 16)
                if ob >= 2:
                    sy.wait_ge(v_t, 2 * (ob - 1))
                sy.dma_start(out=sx[ob % 2][:, :],
                             in_=XRES[ob * 128:(ob + 1) * 128, :]
                             ).then_inc(x_ld[ob % 2], 16)
            for ch in range(NOB):
                sy.wait_ge(p_pre[ch % 2], 16 * (ch // 2 + 1))
                if ch >= 2:
                    sy.wait_ge(v_p2, ch - 1)
                sy.dma_start(out=land2[ch % 2][:, :],
                             in_=XPRE[ch * 128:(ch + 1) * 128, :]
                             ).then_inc(x2_ld[ch % 2], 16)

        @blk.vector
        def _(ve):
            ve.memset(eps_t[:, :], EPS).then_inc(ini, 1)
            for ob in range(NOB):
                ve.wait_ge(w_ld[ob % 2], 32 * (ob // 2 + 1))
                if ob >= 2:
                    ve.wait_ge(p_mm, 2 * (ob - 1))
                ve.tensor_copy(
                    out=swb[ob % 2][:, :].rearrange("p (i c) -> p i c", i=32),
                    in_=landw[ob % 2][:, :, :]).then_inc(w_cv, 1)
                for half in range(2):
                    hs = slice(half * 512, (half + 1) * 512)
                    ve.wait_ge(p_mm, 2 * ob + half + 1)
                    ve.wait_ge(x_ld[ob % 2], 16 * (ob // 2 + 1))
                    if ob >= 2:
                        ve.wait_ge(p_pre[ob % 2], 16 * (ob // 2))
                        ve.wait_ge(a_sq, ob - 1)
                        ve.wait_ge(p_sd, ob - 1)
                    ve.tensor_tensor(out=st_t[ob % 2][:, hs],
                                     in0=yb[(ob % 2) * 2 + half][:, :],
                                     in1=sx[ob % 2][:, hs],
                                     op=ALU.add).then_inc(v_t, 1)
                    ve.wait_ge(v_t, 2 * ob + half + 1)
                    ins = ve.tensor_copy(out=stb[ob % 2][:, hs],
                                         in_=st_t[ob % 2][:, hs])
                    if half == 1:
                        ins.then_inc(v_st, 1)
            ve.wait_ge(p_sd, NOB)
            ve.tensor_copy(out=srow[:, 0:512], in_=ssum[0][:, :]).then_inc(cch, 1)
            ve.tensor_copy(out=srow[:, 512:1024], in_=ssum[1][:, :]).then_inc(cch, 1)
            ve.tensor_copy(out=srow[:, 1024:1536], in_=ssq[0][:, :]).then_inc(cch, 1)
            ve.tensor_copy(out=srow[:, 1536:2048], in_=ssq[1][:, :]).then_inc(cch, 1)
            ve.wait_ge(cch, 4)
            ve.tensor_scalar(out=srow2[:, 0:1024], in0=srow[:, 0:1024],
                             scalar1=1.0 / C, scalar2=None,
                             op0=ALU.mult).then_inc(cch, 1)
            ve.tensor_scalar(out=srow2[:, 1024:2048], in0=srow[:, 1024:2048],
                             scalar1=1.0 / C, scalar2=None,
                             op0=ALU.mult).then_inc(cch, 1)
            ve.wait_ge(cch, 6)
            ve.tensor_tensor(out=srow2[:, 2048:3072], in0=srow2[:, 0:1024],
                             in1=srow2[:, 0:1024], op=ALU.mult).then_inc(cch, 1)
            ve.wait_ge(cch, 7)
            ve.tensor_tensor(out=srow2[:, 2048:3072], in0=srow2[:, 1024:2048],
                             in1=srow2[:, 2048:3072],
                             op=ALU.subtract).then_inc(v_row, 1)
            ve.wait_ge(a_row, 1)
            ve.reciprocal(out=srow[:, 2048:3072],
                          in_=srow2[:, 3072:4096]).then_inc(cch, 1)
            ve.wait_ge(cch, 8)
            ve.tensor_tensor(out=srow[:, 3072:4096], in0=srow2[:, 0:1024],
                             in1=srow[:, 2048:3072], op=ALU.mult).then_inc(cch, 1)
            ve.wait_ge(cch, 9)
            ve.tensor_scalar(out=srow[:, 3072:4096], in0=srow[:, 3072:4096],
                             scalar1=-1.0, scalar2=None,
                             op0=ALU.mult).then_inc(v_row, 1)
            ve.wait_ge(p_bc, 1)
            ve.tensor_copy(out=sra[:, 0:512], in_=yb[0][:, :]).then_inc(cch, 1)
            ve.tensor_copy(out=sra[:, 512:1024], in_=yb[1][:, :]).then_inc(cch, 1)
            ve.tensor_copy(out=srb2[:, 0:512], in_=yb[2][:, :]).then_inc(cch, 1)
            ve.tensor_copy(out=srb2[:, 512:1024],
                           in_=yb[3][:, :]).then_inc(v_bc, 1)
            for ch in range(NOB):
                ve.wait_ge(x2_ld[ch % 2], 16 * (ch // 2 + 1))
                ve.wait_ge(v_bc, 1)
                ve.wait_ge(cch, 12)
                if ch >= 2:
                    ve.wait_ge(p_out[ch % 2], 16 * (ch // 2))
                if ch >= 1:
                    ve.wait_ge(v_p3, ch)
                ve.tensor_tensor(out=tmp2[:, :], in0=land2[ch % 2][:, :],
                                 in1=sra[:, :], op=ALU.mult).then_inc(v_p2, 1)
                ve.wait_ge(v_p2, ch + 1)
                ve.tensor_tensor(out=stage2[ch % 2][:, :], in0=tmp2[:, :],
                                 in1=srb2[:, :], op=ALU.add).then_inc(v_p3, 1)

        @blk.scalar
        def _(sc):
            for ob in range(NOB):
                for half in range(2):
                    hs = slice(half * 512, (half + 1) * 512)
                    sc.wait_ge(v_t, 2 * ob + half + 1)
                    if ob >= 2:
                        sc.wait_ge(p_sd, ob - 1)
                    ins = sc.activation(out=stsq[ob % 2][:, hs],
                                        in_=st_t[ob % 2][:, hs], func=AF.Square)
                    if half == 1:
                        ins.then_inc(a_sq, 1)
            sc.wait_ge(v_row, 1)
            sc.wait_ge(ini, 1)
            sc.activation(out=srow2[:, 3072:4096], in_=srow2[:, 2048:3072],
                          func=AF.Sqrt, bias=eps_t[:, 0:1]).then_inc(a_row, 1)

        def _stats(te, o):
            te.wait_ge(v_st, o + 1)
            te.wait_ge(a_sq, o + 1)
            for half in range(2):
                hs = slice(half * 512, (half + 1) * 512)
                te.matmul(out=ssum[half][:, :], lhsT=sonesc[:, :],
                          rhs=stb[o % 2][:, hs], start=(o == 0), stop=(o == NOB - 1),
                          skip_group_check=True)
                ins = te.matmul(out=ssq[half][:, :], lhsT=sonesc[:, :],
                                rhs=stsq[o % 2][:, hs], start=(o == 0),
                                stop=(o == NOB - 1), skip_group_check=True)
            ins.then_inc(p_sd, 1)

        @blk.tensor
        def _(te):
            te.wait_ge(c_ld, 96)
            for ob in range(NOB):
                te.wait_ge(w_cv, ob + 1)
                if ob >= 2:
                    te.wait_ge(v_t, 2 * (ob - 1))
                for cc in range(32):
                    for half in range(2):
                        ins = te.matmul(
                            out=yb[(ob % 2) * 2 + half][:, :],
                            lhsT=swb[ob % 2][:, cc * 128:(cc + 1) * 128],
                            rhs=so[:, cc, half * 512:(half + 1) * 512],
                            start=(cc == 0), stop=(cc == 31))
                        if cc == 31:
                            ins.then_inc(p_mm, 1)
                if ob >= 1:
                    _stats(te, ob - 1)
            _stats(te, NOB - 1)
            te.wait_ge(v_row, 2)
            te.wait_ge(v_t, 2 * NOB)
            te.matmul(out=yb[0][:, :], lhsT=sonesr[:, :],
                      rhs=srow[:, 2048:2560], start=True, stop=True)
            te.matmul(out=yb[1][:, :], lhsT=sonesr[:, :],
                      rhs=srow[:, 2560:3072], start=True, stop=True)
            te.matmul(out=yb[2][:, :], lhsT=sonesr[:, :],
                      rhs=srow[:, 3072:3584], start=True, stop=True)
            te.matmul(out=yb[3][:, :], lhsT=sonesr[:, :],
                      rhs=srow[:, 3584:4096], start=True, stop=True).then_inc(p_bc, 1)

        @blk.gpsimd
        def _(gp):
            for ob in range(NOB):
                gp.wait_ge(v_t, 2 * ob + 2)
                gp.dma_start(out=XPRE[ob * 128:(ob + 1) * 128, :],
                             in_=st_t[ob % 2][:, :]).then_inc(p_pre[ob % 2], 16)
            for ch in range(NOB):
                gp.wait_ge(v_p3, ch + 1)
                gp.dma_start(out=XOUT[ch * 128:(ch + 1) * 128, :],
                             in_=stage2[ch % 2][:, :]).then_inc(p_out[ch % 2], 16)
            gp.wait_ge(p_out[0], 16 * 16)
            gp.wait_ge(p_out[1], 16 * 16)

    ph.close()


def emit_ffn0(nc, io, name, X1, QW_r, QB_r, WS, WZP, BS, BZP, XOUT, XPRE):
    ph = P(nc, name)
    ones_col, ones_row = io["ones_col"], io["ones_row"]

    sxb = ph.sb("sxb", [128, 32, 1024], BF16)
    land = [ph.sb(f"land{i}", [128, 1024], F32) for i in range(2)]
    landw = [ph.sb(f"landw{i}", [128, 32, 128], I32) for i in range(2)]
    swb = [ph.sb(f"swb{i}", [128, 4096], BF16) for i in range(2)]
    sx = [ph.sb(f"sx{i}", [128, 1024], F32) for i in range(2)]
    sy1 = [ph.sb(f"sy1{i}", [128, 1024], F32) for i in range(2)]
    st_t = [ph.sb(f"stt{i}", [128, 1024], F32) for i in range(2)]
    stb = [ph.sb(f"stb{i}", [128, 1024], BF16) for i in range(2)]
    stsq = [ph.sb(f"stsq{i}", [128, 1024], BF16) for i in range(2)]
    sonesc = ph.sb("sonesc", [128, 1], BF16)
    sonesr = ph.sb("sonesr", [1, 128], F32)
    s_ws = ph.sb("s_ws", [1, 1], F32)
    s_wzp = ph.sb("s_wzp", [1, 1], I32)
    s_bs = ph.sb("s_bs", [1, 1], F32)
    s_bzp = ph.sb("s_bzp", [1, 1], I32)
    s_wzpf = ph.sb("s_wzpf", [1, 1], F32)
    s_bzpf = ph.sb("s_bzpf", [1, 1], F32)
    sqb = ph.sb("sqb", [128, 32], I32)
    sqbf = ph.sb("sqbf", [128, 32], F32)
    sbias = ph.sb("sbias", [128, 32], F32)
    wsb = ph.sb("wsb", [128, 1], F32)
    bzpb = ph.sb("bzpb", [128, 1], F32)
    bsb = ph.sb("bsb", [128, 1], F32)
    wzrow = ph.sb("wzrow", [1, 128], F32)
    wzrow_b = ph.sb("wzrow_b", [1, 128], BF16)
    rsx_b = ph.sb("rsx_b", [1, 1024], BF16)
    srow = ph.sb("srow", [1, 4096], F32)
    sra = ph.sb("sra", [128, 1024], F32)
    srb2 = ph.sb("srb2", [128, 1024], F32)
    land2 = [ph.sb(f"land2{i}", [128, 1024], F32) for i in range(2)]
    tmp2 = ph.sb("tmp2", [128, 1024], F32)
    stage2 = [ph.sb(f"stage2{i}", [128, 1024], F32) for i in range(2)]
    eps_t = ph.sb("eps_t", [1, 1], F32)

    yb = [ph.ps(f"yb{i}", [128, 512]) for i in range(4)]
    ssum = [ph.ps(f"ssum{i}", [1, 512]) for i in range(2)]
    ssq = [ph.ps(f"ssq{i}", [1, 512]) for i in range(2)]

    c_ld = ph.sem("c_ld")
    px_ld = [ph.sem("px_ld0"), ph.sem("px_ld1")]
    pr_cv = ph.sem("pr_cv")
    pr_pe = ph.sem("pr_pe")
    cch = ph.sem("cch")
    w_ld = [ph.sem("w_ld0"), ph.sem("w_ld1")]
    w_cv = ph.sem("w_cv")
    x_ld = [ph.sem("x_ld0"), ph.sem("x_ld1")]
    p_mm = ph.sem("p_mm")
    p_sd = ph.sem("p_sd")
    p_bc = ph.sem("p_bc")
    a_ev = ph.sem("a_ev")
    v_t = ph.sem("v_t")
    v_st = ph.sem("v_st")
    a_sq = ph.sem("a_sq")
    p_pre = [ph.sem("p_pre0"), ph.sem("p_pre1")]
    v_row = ph.sem("v_row")
    a_row = ph.sem("a_row")
    v_bc = ph.sem("v_bc")
    x2_ld = [ph.sem("x2_ld0"), ph.sem("x2_ld1")]
    v_p2 = ph.sem("v_p2")
    v_p3 = ph.sem("v_p3")
    p_out = [ph.sem("p_out0"), ph.sem("p_out1")]
    ini = ph.sem("ini")

    NOB = 32

    with nc.Block(name) as blk:

        @blk.sync
        def _(sy):
            sy.dma_start(out=sonesc[:, :], in_=ones_col[:, :]).then_inc(c_ld, 16)
            sy.dma_start(out=sonesr[:, :], in_=ones_row[:, :]).then_inc(c_ld, 16)
            sy.dma_start(out=s_ws[:, :], in_=WS[:, :]).then_inc(c_ld, 16)
            sy.dma_start(out=s_wzp[:, :], in_=WZP[:, :]).then_inc(c_ld, 16)
            sy.dma_start(out=s_bs[:, :], in_=BS[:, :]).then_inc(c_ld, 16)
            sy.dma_start(out=s_bzp[:, :], in_=BZP[:, :]).then_inc(c_ld, 16)
            sy.dma_start(out=sqb[:, :], in_=QB_r[:, :]).then_inc(c_ld, 16)
            for ch in range(32):
                if ch >= 2:
                    sy.wait_ge(pr_cv, ch - 1)
                sy.dma_start(out=land[ch % 2][:, :],
                             in_=X1[ch * 128:(ch + 1) * 128, :]
                             ).then_inc(px_ld[ch % 2], 16)
            for ob in range(NOB):
                if ob >= 2:
                    sy.wait_ge(w_cv, ob - 1)
                for hh in range(2):
                    sy.dma_start(
                        out=landw[ob % 2][:, hh * 16:(hh + 1) * 16, :],
                        in_=QW_r[ob, hh * 16:(hh + 1) * 16, :, :].rearrange(
                            "i p c -> p i c")).then_inc(w_ld[ob % 2], 16)
                if ob >= 2:
                    sy.wait_ge(v_t, 2 * (ob - 1))
                sy.dma_start(out=sx[ob % 2][:, :],
                             in_=X1[ob * 128:(ob + 1) * 128, :]
                             ).then_inc(x_ld[ob % 2], 16)
            for ch in range(NOB):
                sy.wait_ge(p_pre[ch % 2], 16 * (ch // 2 + 1))
                if ch >= 2:
                    sy.wait_ge(v_p2, ch - 1)
                sy.dma_start(out=land2[ch % 2][:, :],
                             in_=XPRE[ch * 128:(ch + 1) * 128, :]
                             ).then_inc(x2_ld[ch % 2], 16)

        @blk.vector
        def _(ve):
            ve.memset(eps_t[:, :], EPS).then_inc(ini, 1)
            for ch in range(32):
                ve.wait_ge(px_ld[ch % 2], 16 * (ch // 2 + 1))
                ve.tensor_copy(out=sxb[:, ch, :],
                               in_=land[ch % 2][:, :]).then_inc(pr_cv, 1)
            ve.wait_ge(c_ld, 112)
            ve.tensor_copy(out=s_wzpf[:, :], in_=s_wzp[:, :]).then_inc(cch, 1)
            ve.tensor_copy(out=s_bzpf[:, :], in_=s_bzp[:, :]).then_inc(cch, 1)
            ve.tensor_copy(out=sqbf[:, :], in_=sqb[:, :]).then_inc(cch, 1)
            ve.wait_ge(cch, 3)
            ve.tensor_scalar(out=wzrow[:, :], in0=sonesr[:, :],
                             scalar1=s_wzpf[0:1, 0:1], scalar2=-1.0,
                             op0=ALU.mult, op1=ALU.mult).then_inc(cch, 1)
            ve.wait_ge(cch, 4)
            ve.tensor_copy(out=wzrow_b[:, :], in_=wzrow[:, :]).then_inc(v_row, 1)
            ve.wait_ge(pr_pe, 1)
            ve.tensor_copy(out=wsb[:, :], in_=yb[0][:, 0:1]).then_inc(cch, 1)
            ve.tensor_copy(out=bzpb[:, :], in_=yb[1][:, 0:1]).then_inc(cch, 1)
            ve.tensor_copy(out=bsb[:, :], in_=yb[2][:, 0:1]).then_inc(cch, 1)
            ve.wait_ge(cch, 7)
            ve.tensor_scalar(out=sbias[:, :], in0=sqbf[:, :],
                             scalar1=bzpb[:, 0:1], scalar2=bsb[:, 0:1],
                             op0=ALU.subtract, op1=ALU.mult).then_inc(v_row, 1)
            ve.wait_ge(pr_pe, 2)
            ve.tensor_copy(out=rsx_b[:, 0:512], in_=ssum[0][:, :]).then_inc(cch, 1)
            ve.tensor_copy(out=rsx_b[:, 512:1024],
                           in_=ssum[1][:, :]).then_inc(v_row, 1)
            for ob in range(NOB):
                ve.wait_ge(w_ld[ob % 2], 32 * (ob // 2 + 1))
                if ob >= 2:
                    ve.wait_ge(p_mm, 2 * (ob - 1))
                ve.tensor_copy(
                    out=swb[ob % 2][:, :].rearrange("p (i c) -> p i c", i=32),
                    in_=landw[ob % 2][:, :, :]).then_inc(w_cv, 1)
                for half in range(2):
                    hs = slice(half * 512, (half + 1) * 512)
                    ve.wait_ge(a_ev, 2 * ob + half + 1)
                    ve.wait_ge(x_ld[ob % 2], 16 * (ob // 2 + 1))
                    if ob >= 2:
                        ve.wait_ge(p_pre[ob % 2], 16 * (ob // 2))
                        ve.wait_ge(a_sq, ob - 1)
                        ve.wait_ge(p_sd, ob - 1)
                    ve.tensor_tensor(out=st_t[ob % 2][:, hs],
                                     in0=sy1[ob % 2][:, hs],
                                     in1=sx[ob % 2][:, hs],
                                     op=ALU.add).then_inc(v_t, 1)
                    ve.wait_ge(v_t, 2 * ob + half + 1)
                    ins = ve.tensor_copy(out=stb[ob % 2][:, hs],
                                         in_=st_t[ob % 2][:, hs])
                    if half == 1:
                        ins.then_inc(v_st, 1)
            ve.wait_ge(p_sd, NOB)
            ve.tensor_copy(out=srow[:, 0:512], in_=ssum[0][:, :]).then_inc(cch, 1)
            ve.tensor_copy(out=srow[:, 512:1024], in_=ssum[1][:, :]).then_inc(cch, 1)
            ve.tensor_copy(out=srow[:, 1024:1536], in_=ssq[0][:, :]).then_inc(cch, 1)
            ve.tensor_copy(out=srow[:, 1536:2048], in_=ssq[1][:, :]).then_inc(cch, 1)
            ve.wait_ge(cch, 12)
            ve.tensor_scalar(out=srow[:, 2048:3072], in0=srow[:, 0:1024],
                             scalar1=1.0 / C, scalar2=None,
                             op0=ALU.mult).then_inc(cch, 1)
            ve.tensor_scalar(out=srow[:, 3072:4096], in0=srow[:, 1024:2048],
                             scalar1=1.0 / C, scalar2=None,
                             op0=ALU.mult).then_inc(cch, 1)
            ve.wait_ge(cch, 14)
            ve.tensor_tensor(out=srow[:, 0:1024], in0=srow[:, 2048:3072],
                             in1=srow[:, 2048:3072],
                             op=ALU.mult).then_inc(cch, 1)
            ve.wait_ge(cch, 15)
            ve.tensor_tensor(out=srow[:, 1024:2048], in0=srow[:, 3072:4096],
                             in1=srow[:, 0:1024],
                             op=ALU.subtract).then_inc(v_row, 1)
            ve.wait_ge(a_row, 1)
            ve.reciprocal(out=srow[:, 3072:4096],
                          in_=srow[:, 0:1024]).then_inc(cch, 1)
            ve.wait_ge(cch, 16)
            ve.tensor_tensor(out=srow[:, 1024:2048], in0=srow[:, 2048:3072],
                             in1=srow[:, 3072:4096],
                             op=ALU.mult).then_inc(cch, 1)
            ve.wait_ge(cch, 17)
            ve.tensor_scalar(out=srow[:, 1024:2048], in0=srow[:, 1024:2048],
                             scalar1=-1.0, scalar2=None,
                             op0=ALU.mult).then_inc(v_row, 1)
            ve.wait_ge(p_bc, 1)
            ve.tensor_copy(out=sra[:, 0:512], in_=yb[0][:, :]).then_inc(cch, 1)
            ve.tensor_copy(out=sra[:, 512:1024], in_=yb[1][:, :]).then_inc(cch, 1)
            ve.tensor_copy(out=srb2[:, 0:512], in_=yb[2][:, :]).then_inc(cch, 1)
            ve.tensor_copy(out=srb2[:, 512:1024],
                           in_=yb[3][:, :]).then_inc(v_bc, 1)
            for ch in range(NOB):
                ve.wait_ge(x2_ld[ch % 2], 16 * (ch // 2 + 1))
                ve.wait_ge(v_bc, 1)
                ve.wait_ge(cch, 20)
                if ch >= 2:
                    ve.wait_ge(p_out[ch % 2], 16 * (ch // 2))
                if ch >= 1:
                    ve.wait_ge(v_p3, ch)
                ve.tensor_tensor(out=tmp2[:, :], in0=land2[ch % 2][:, :],
                                 in1=sra[:, :], op=ALU.mult).then_inc(v_p2, 1)
                ve.wait_ge(v_p2, ch + 1)
                ve.tensor_tensor(out=stage2[ch % 2][:, :], in0=tmp2[:, :],
                                 in1=srb2[:, :], op=ALU.add).then_inc(v_p3, 1)

        @blk.scalar
        def _(sc):
            for ob in range(NOB):
                for half in range(2):
                    hs = slice(half * 512, (half + 1) * 512)
                    sc.wait_ge(p_mm, 2 * ob + half + 1)
                    if ob >= 2:
                        sc.wait_ge(v_t, 2 * (ob - 1))
                    sc.wait_ge(v_row, 2)
                    sc.activation(out=sy1[ob % 2][:, hs],
                                  in_=yb[(ob % 2) * 2 + half][:, :],
                                  func=AF.Identity, scale=wsb[:, 0:1],
                                  bias=sbias[:, ob:ob + 1]).then_inc(a_ev, 1)
                    sc.wait_ge(v_t, 2 * ob + half + 1)
                    if ob >= 2:
                        sc.wait_ge(p_sd, ob - 1)
                    ins = sc.activation(out=stsq[ob % 2][:, hs],
                                        in_=st_t[ob % 2][:, hs], func=AF.Square)
                    if half == 1:
                        ins.then_inc(a_sq, 1)
            sc.wait_ge(v_row, 4)
            sc.wait_ge(ini, 1)
            sc.activation(out=srow[:, 0:1024], in_=srow[:, 1024:2048],
                          func=AF.Sqrt, bias=eps_t[:, 0:1]).then_inc(a_row, 1)

        def _stats3(te, o):
            te.wait_ge(v_st, o + 1)
            te.wait_ge(a_sq, o + 1)
            for half in range(2):
                hs = slice(half * 512, (half + 1) * 512)
                te.matmul(out=ssum[half][:, :], lhsT=sonesc[:, :],
                          rhs=stb[o % 2][:, hs], start=(o == 0), stop=(o == NOB - 1),
                          skip_group_check=True)
                ins = te.matmul(out=ssq[half][:, :], lhsT=sonesc[:, :],
                                rhs=stsq[o % 2][:, hs], start=(o == 0),
                                stop=(o == NOB - 1), skip_group_check=True)
            ins.then_inc(p_sd, 1)

        @blk.tensor
        def _(te):
            te.wait_ge(c_ld, 112)
            te.wait_ge(cch, 3)
            te.matmul(out=yb[0][:, 0:1], lhsT=sonesr[:, :], rhs=s_ws[:, :],
                      start=True, stop=True)
            te.matmul(out=yb[1][:, 0:1], lhsT=sonesr[:, :], rhs=s_bzpf[:, :],
                      start=True, stop=True)
            te.matmul(out=yb[2][:, 0:1], lhsT=sonesr[:, :], rhs=s_bs[:, :],
                      start=True, stop=True).then_inc(pr_pe, 1)
            te.wait_ge(pr_cv, 32)
            for cc in range(32):
                for half in range(2):
                    ins = te.matmul(out=ssum[half][:, :], lhsT=sonesc[:, :],
                                    rhs=sxb[:, cc, half * 512:(half + 1) * 512],
                                    start=(cc == 0), stop=(cc == 31),
                                    skip_group_check=True)
            ins.then_inc(pr_pe, 1)
            for ob in range(NOB):
                te.wait_ge(w_cv, ob + 1)
                te.wait_ge(v_row, 3)
                if ob >= 2:
                    te.wait_ge(a_ev, 2 * (ob - 1))
                for half in range(2):
                    hs = slice(half * 512, (half + 1) * 512)
                    for cc in range(32):
                        te.matmul(
                            out=yb[(ob % 2) * 2 + half][:, :],
                            lhsT=swb[ob % 2][:, cc * 128:(cc + 1) * 128],
                            rhs=sxb[:, cc, hs], start=(cc == 0), stop=False)
                    te.matmul(out=yb[(ob % 2) * 2 + half][:, :],
                              lhsT=wzrow_b[:, :], rhs=rsx_b[0:1, hs],
                              start=False, stop=True).then_inc(p_mm, 1)
                if ob >= 1:
                    _stats3(te, ob - 1)
            _stats3(te, NOB - 1)
            te.wait_ge(v_row, 5)
            te.wait_ge(a_ev, 2 * NOB)
            te.matmul(out=yb[0][:, :], lhsT=sonesr[:, :],
                      rhs=srow[:, 3072:3584], start=True, stop=True)
            te.matmul(out=yb[1][:, :], lhsT=sonesr[:, :],
                      rhs=srow[:, 3584:4096], start=True, stop=True)
            te.matmul(out=yb[2][:, :], lhsT=sonesr[:, :],
                      rhs=srow[:, 1024:1536], start=True, stop=True)
            te.matmul(out=yb[3][:, :], lhsT=sonesr[:, :],
                      rhs=srow[:, 1536:2048], start=True, stop=True).then_inc(p_bc, 1)

        @blk.gpsimd
        def _(gp):
            for ob in range(NOB):
                gp.wait_ge(v_t, 2 * ob + 2)
                gp.wait_ge(a_sq, ob + 1)
                gp.dma_start(out=XPRE[ob * 128:(ob + 1) * 128, :],
                             in_=st_t[ob % 2][:, :]).then_inc(p_pre[ob % 2], 16)
            for ch in range(NOB):
                gp.wait_ge(v_p3, ch + 1)
                gp.dma_start(out=XOUT[ch * 128:(ch + 1) * 128, :],
                             in_=stage2[ch % 2][:, :]).then_inc(p_out[ch % 2], 16)
            gp.wait_ge(p_out[0], 16 * 16)
            gp.wait_ge(p_out[1], 16 * 16)

    ph.close()


def emit_p4(nc, io, name, X2, WQ1_r, WK1, WV1, QT1, KT1c, V1c):
    ph = P(nc, name)

    sxb = ph.sb("sxb", [128, 32, 1024], BF16)
    land = [ph.sb(f"land{i}", [128, 1024], F32) for i in range(2)]
    landw = [ph.sb(f"landw{i}", [128, 32, 128], F32) for i in range(2)]
    kland = ph.sb("kland", [128, 16, 128], F32)
    swk = ph.sb("swk", [128, 4096], BF16)
    swv = ph.sb("swv", [128, 4096], BF16)
    swq = [ph.sb(f"swq{i}", [128, 4096], BF16) for i in range(2)]
    kstg = [ph.sb(f"kstg{i}", [128, 512], BF16) for i in range(2)]
    vstg = [ph.sb(f"vstg{i}", [128, 128], BF16) for i in range(2)]
    qstg = [ph.sb(f"qstg{i}", [128, 1024], BF16) for i in range(2)]

    kb = [ph.ps(f"kb{i}", [128, 512]) for i in range(2)]
    vb = [ph.ps(f"vb{i}", [128, 128]) for i in range(2)]
    qb = [ph.ps(f"qb{i}", [128, 512]) for i in range(4)]

    px_ld = [ph.sem("px_ld0"), ph.sem("px_ld1")]
    pr_cv = ph.sem("pr_cv")
    kv_ld = ph.sem("kv_ld")
    kv_cv = ph.sem("kv_cv")
    w_ld = [ph.sem("w_ld0"), ph.sem("w_ld1")]
    w_cv = ph.sem("w_cv")
    p_k = ph.sem("p_k")
    p_v = ph.sem("p_v")
    p_q = ph.sem("p_q")
    v_k = ph.sem("v_k")
    v_v = ph.sem("v_v")
    v_q = ph.sem("v_q")
    s_k = [ph.sem("s_k0"), ph.sem("s_k1")]
    s_v = [ph.sem("s_v0"), ph.sem("s_v1")]
    s_q = [ph.sem("s_q0"), ph.sem("s_q1")]

    with nc.Block(name) as blk:

        @blk.sync
        def _(sy):
            for w in range(4):
                src = WK1 if w < 2 else WV1
                hh = w % 2
                if w > 0:
                    sy.wait_ge(kv_cv, w)
                sy.dma_start(out=kland[:, :, :],
                             in_=src[hh * 2048:(hh + 1) * 2048, :]
                             .rearrange("(i p) d -> p i d", p=128)).then_inc(kv_ld, 16)
            for ch in range(32):
                if ch >= 2:
                    sy.wait_ge(pr_cv, ch - 1)
                sy.dma_start(out=land[ch % 2][:, :],
                             in_=X2[ch * 128:(ch + 1) * 128, :]
                             ).then_inc(px_ld[ch % 2], 16)
            for hd in range(32):
                if hd >= 2:
                    sy.wait_ge(w_cv, hd - 1)
                for hh in range(2):
                    sy.dma_start(
                        out=landw[hd % 2][:, hh * 16:(hh + 1) * 16, :],
                        in_=WQ1_r[hd, hh * 16:(hh + 1) * 16, :, :].rearrange(
                            "i p c -> p i c")).then_inc(w_ld[hd % 2], 16)

        @blk.vector
        def _(ve):
            for w in range(4):
                ve.wait_ge(kv_ld, 16 * (w + 1))
                dst = swk if w < 2 else swv
                off = (w % 2) * 2048
                ve.tensor_copy(
                    out=dst[:, off:off + 2048].rearrange("p (i c) -> p i c", i=16),
                    in_=kland[:, :, :]).then_inc(kv_cv, 1)
            for ch in range(32):
                ve.wait_ge(px_ld[ch % 2], 16 * (ch // 2 + 1))
                ve.tensor_copy(out=sxb[:, ch, :],
                               in_=land[ch % 2][:, :]).then_inc(pr_cv, 1)
            for half in range(2):
                ve.wait_ge(p_k, half + 1)
                if half >= 2:
                    ve.wait_ge(s_k[half % 2], 16 * (half // 2))
                ve.tensor_copy(out=kstg[half % 2][:, :],
                               in_=kb[half][:, :]).then_inc(v_k, 1)
            for tt in range(8):
                ve.wait_ge(p_v, tt + 1)
                if tt >= 2:
                    ve.wait_ge(s_v[tt % 2], 16 * (tt // 2))
                ve.tensor_copy(out=vstg[tt % 2][:, :],
                               in_=vb[tt % 2][:, :]).then_inc(v_v, 1)
            for hd in range(32):
                for half in range(2):
                    g = hd * 2 + half
                    ve.wait_ge(p_q, g + 1)
                    if hd >= 2:
                        ve.wait_ge(s_q[hd % 2], 16 * (hd // 2))
                    ve.tensor_copy(out=qstg[hd % 2][:, half * 512:(half + 1) * 512],
                                   in_=qb[(hd % 2) * 2 + half][:, :]).then_inc(v_q, 1)

        @blk.tensor
        def _(te):
            te.wait_ge(kv_cv, 4)
            te.wait_ge(pr_cv, 32)
            for half in range(2):
                for i in range(32):
                    ins = te.matmul(out=kb[half][:, :],
                                    lhsT=swk[:, i * 128:(i + 1) * 128],
                                    rhs=sxb[:, i, half * 512:(half + 1) * 512],
                                    start=(i == 0), stop=(i == 31))
                ins.then_inc(p_k, 1)
            for tt in range(8):
                if tt >= 2:
                    te.wait_ge(v_v, tt - 1)
                for i in range(32):
                    ins = te.matmul(out=vb[tt % 2][:, :],
                                    lhsT=sxb[:, i, tt * 128:(tt + 1) * 128],
                                    rhs=swv[:, i * 128:(i + 1) * 128],
                                    start=(i == 0), stop=(i == 31))
                ins.then_inc(p_v, 1)
            for hd in range(32):
                te.wait_ge(w_cv, hd + 1)
                if hd >= 2:
                    te.wait_ge(v_q, 2 * (hd - 1))
                for half in range(2):
                    for i in range(32):
                        ins = te.matmul(
                            out=qb[(hd % 2) * 2 + half][:, :],
                            lhsT=swq[hd % 2][:, i * 128:(i + 1) * 128],
                            rhs=sxb[:, i, half * 512:(half + 1) * 512],
                            start=(i == 0), stop=(i == 31))
                    ins.then_inc(p_q, 1)

        @blk.scalar
        def _(sc):
            for hd in range(32):
                sc.wait_ge(w_ld[hd % 2], 32 * (hd // 2 + 1))
                if hd >= 2:
                    sc.wait_ge(p_q, 2 * (hd - 1))
                sc.activation(out=swq[hd % 2][:, :].rearrange(
                                  "p (i c) -> p i c", i=32),
                              in_=landw[hd % 2][:, :, :],
                              func=AF.Copy).then_inc(w_cv, 1)

        @blk.gpsimd
        def _(gp):
            for half in range(2):
                gp.wait_ge(v_k, half + 1)
                gp.dma_start(out=KT1c[:, half * 512:(half + 1) * 512],
                             in_=kstg[half % 2][:, :]).then_inc(s_k[half % 2], 16)
            for tt in range(8):
                gp.wait_ge(v_v, tt + 1)
                gp.dma_start(out=V1c[tt * 128:(tt + 1) * 128, :],
                             in_=vstg[tt % 2][:, :]).then_inc(s_v[tt % 2], 16)
            for hd in range(32):
                gp.wait_ge(v_q, 2 * hd + 2)
                gp.dma_start(out=QT1[hd * 128:(hd + 1) * 128, :],
                             in_=qstg[hd % 2][:, :]).then_inc(s_q[hd % 2], 16)
            gp.wait_ge(s_k[0], 16)
            gp.wait_ge(s_k[1], 16)
            gp.wait_ge(s_v[0], 16 * 4)
            gp.wait_ge(s_v[1], 16 * 4)
            gp.wait_ge(s_q[0], 16 * 16)
            gp.wait_ge(s_q[1], 16 * 16)

    ph.close()


def emit_p5(nc, io, name, QT1, KG1, VG1, TTHR, OT1):
    ph = P(nc, name)
    iota_d, ones_col, ones_row = io["iota"], io["ones_col"], io["ones_row"]

    sk = ph.sb("sk", [128, T], BF16)
    sv = ph.sb("sv", [128, 32, 128], BF16)
    siota = ph.sb("siota", [128, 512], F32)
    sthr = ph.sb("sthr", [128, 48], F32)
    sq = [ph.sb(f"sq{i}", [128, 512], BF16) for i in range(2)]
    sp = [ph.sb(f"sp{i}", [128, 512], BF16) for i in range(2)]
    spm = [ph.sb(f"spm{i}", [128, 512], BF16) for i in range(2)]
    sm01 = ph.sb("sm01", [128, 32, 512], BF16)
    sones = ph.sb("sones", [128, 1], BF16)
    sonesr = ph.sb("sonesr", [1, 128], F32)
    srcp = ph.sb("srcp", [1, 512], F32)
    srb = ph.sb("srb", [128, 512], F32)
    ostg = [ph.sb(f"ostg{i}", [128, 512], BF16) for i in range(2)]
    sbias = ph.sb("sbias", [128, 1], F32)

    bO = ph.ps("bO", [128, 512])
    bS = [ph.ps(f"bS{i}", [128, 512]) for i in range(2)]
    bD = ph.ps("bD", [1, 512])
    bRB = ph.ps("bRB", [128, 512])

    c_ld = ph.sem("c_ld")
    q_ld = [ph.sem("q_ld0"), ph.sem("q_ld1")]
    s_mm = ph.sem("s_mm")
    s_pv = ph.sem("s_pv")
    s_dn = ph.sem("s_dn")
    s_bc = ph.sem("s_bc")
    a_ex = ph.sem("a_ex")
    v_mk = ph.sem("v_mk")
    v_m01 = ph.sem("v_m01")
    v_rcp = ph.sem("v_rcp")
    v_ev = ph.sem("v_ev")
    p_st = [ph.sem("p_st0"), ph.sem("p_st1")]
    v_ini = ph.sem("v_ini")
    ch_ve = ph.sem("ch_ve")

    PA = [16, 32]
    NHD = 32

    def g_of(a, h, b):
        return (a * NHD * PA[0] if a else 0) + h * PA[a] + b

    G_TOT = NHD * (PA[0] + PA[1])

    with nc.Block(name) as blk:

        @blk.sync
        def _(sy):
            sy.dma_start(out=sk[:, :], in_=KG1[:, :]).then_inc(c_ld, 16)
            for part in range(4):
                sy.dma_start(
                    out=sv[:, part * 8:(part + 1) * 8, :],
                    in_=VG1[part * 1024:(part + 1) * 1024, :].rearrange(
                        "(kc p) d -> p kc d", p=128)).then_inc(c_ld, 16)
            sy.dma_start(out=siota[:, :], in_=iota_d[:, :]).then_inc(c_ld, 16)
            sy.dma_start(out=sthr[:, :], in_=TTHR[:, :]).then_inc(c_ld, 16)
            sy.dma_start(out=sones[:, :], in_=ones_col[:, :]).then_inc(c_ld, 16)
            sy.dma_start(out=sonesr[:, :], in_=ones_row[:, :]).then_inc(c_ld, 16)
            for a in range(2):
                for h in range(NHD):
                    ha = a * NHD + h
                    if ha >= 2:
                        sy.wait_ge(s_dn, ha - 1)
                    sy.dma_start(out=sq[ha % 2][:, :],
                                 in_=QT1[h * 128:(h + 1) * 128,
                                         a * 512:(a + 1) * 512]
                                 ).then_inc(q_ld[ha % 2], 16)

        @blk.vector
        def _(ve):
            ve.memset(sbias[:, :], -SOFT_B).then_inc(v_ini, 1)
            ve.wait_ge(c_ld, 16 * 9)
            for a in range(2):
                if a == 1:
                    ve.wait_ge(v_mk, NHD * PA[0])
                for b in range(PA[a]):
                    ve.tensor_scalar(out=sm01[:, b, :], in0=siota[:, :],
                                     scalar1=sthr[:, 16 * a + b:16 * a + b + 1],
                                     scalar2=None,
                                     op0=ALU.is_ge).then_inc(v_m01, 1)
                for h in range(NHD):
                    ha = a * NHD + h
                    km = PA[a]
                    for b in range(km):
                        g = g_of(a, h, b)
                        ve.wait_ge(a_ex, g + 1)
                        if g >= 2:
                            ve.wait_ge(s_pv, g - 1)
                        ve.tensor_tensor(out=spm[g % 2][:, :], in0=sp[g % 2][:, :],
                                         in1=sm01[:, b, :],
                                         op=ALU.mult).then_inc(v_mk, 1)
                    ve.wait_ge(s_dn, ha + 1)
                    ve.reciprocal(out=srcp[:, :], in_=bD[0:1, :]).then_inc(v_rcp, 1)
                    ve.wait_ge(s_bc, ha + 1)
                    ve.tensor_copy(out=srb[:, :], in_=bRB[:, :]).then_inc(ch_ve, 1)
                    if ha >= 2:
                        ve.wait_ge(p_st[ha % 2], 16 * (ha // 2))
                    ve.wait_ge(ch_ve, ha + 1)
                    ve.tensor_tensor(out=ostg[ha % 2][:, :], in0=bO[:, :],
                                     in1=srb[:, :], op=ALU.mult).then_inc(v_ev, 1)

        @blk.tensor
        def _(te):
            te.wait_ge(c_ld, 16 * 9)
            for a in range(2):
                for h in range(NHD):
                    ha = a * NHD + h
                    km = PA[a]
                    for b in range(km):
                        g = g_of(a, h, b)
                        if b == 0:
                            te.wait_ge(q_ld[ha % 2], 16 * (ha // 2 + 1))
                            te.wait_ge(v_ev, ha)
                            te.wait_ge(v_rcp, ha)
                            te.wait_ge(v_m01,
                                       PA[0] if a == 0 else PA[0] + PA[1])
                        if g >= 2:
                            te.wait_ge(a_ex, g - 1)
                        te.matmul(out=bS[g % 2][:, :],
                                  lhsT=sk[:, b * 128:(b + 1) * 128],
                                  rhs=sq[ha % 2][:, :], start=True, stop=True
                                  ).then_inc(s_mm, 1)
                        te.wait_ge(v_mk, g + 1)
                        te.matmul(out=bO[:, :],
                                  lhsT=sv[:, b, :], rhs=spm[g % 2][:, :],
                                  start=(b == 0), stop=(b == km - 1)
                                  ).then_inc(s_pv, 1)
                        ins_d = te.matmul(out=bD[:, :], lhsT=sones[:, :],
                                          rhs=spm[g % 2][:, :],
                                          start=(b == 0), stop=(b == km - 1),
                                          skip_group_check=True)
                        if b == km - 1:
                            ins_d.then_inc(s_dn, 1)
                    te.wait_ge(v_rcp, ha + 1)
                    te.wait_ge(v_ev, ha)
                    te.matmul(out=bRB[:, :], lhsT=sonesr[:, :], rhs=srcp[:, :],
                              start=True, stop=True).then_inc(s_bc, 1)

        @blk.scalar
        def _(sc):
            sc.wait_ge(v_ini, 1)
            for a in range(2):
                for h in range(NHD):
                    km = PA[a]
                    for b in range(km):
                        g = g_of(a, h, b)
                        sc.wait_ge(s_mm, g + 1)
                        if g >= 2:
                            sc.wait_ge(v_mk, g - 1)
                        sc.activation(out=sp[g % 2][:, :], in_=bS[g % 2][:, :],
                                      func=AF.Exp, scale=SCALE,
                                      bias=sbias[:, 0:1]).then_inc(a_ex, 1)

        @blk.gpsimd
        def _(gp):
            for a in range(2):
                for h in range(NHD):
                    ha = a * NHD + h
                    gp.wait_ge(v_ev, ha + 1)
                    gp.dma_start(out=OT1[h * 128:(h + 1) * 128,
                                         a * 512:(a + 1) * 512],
                                 in_=ostg[ha % 2][:, :]).then_inc(p_st[ha % 2], 16)
            gp.wait_ge(p_st[0], 16 * 32)
            gp.wait_ge(p_st[1], 16 * 32)

    ph.close()


def emit_ffn1(nc, io, name, X3, BW_r, GWT, ATALL, BTALL, BB_r, OUT, OUTPRE):
    ph = P(nc, name)
    ones_col, ones_row = io["ones_col"], io["ones_row"]
    ident_d, exp8_d = io["ident"], io["exp8"]

    sxb = ph.sb("sxb", [128, 32, 1024], BF16)
    land = [ph.sb(f"land{i}", [128, 1024], F32) for i in range(2)]
    landw = [ph.sb(f"landw{i}", [128, 16, 128], F32) for i in range(2)]
    swb = [ph.sb(f"swb{i}", [128, 4096], BF16) for i in range(2)]
    sx = [ph.sb(f"sx{i}", [128, 1024], F32) for i in range(2)]
    sy1 = [ph.sb(f"sy1{i}", [128, 1024], F32) for i in range(2)]
    st_t = [ph.sb(f"stt{i}", [128, 1024], F32) for i in range(2)]
    stb = [ph.sb(f"stb{i}", [128, 1024], BF16) for i in range(2)]
    stsq = [ph.sb(f"stsq{i}", [128, 1024], BF16) for i in range(2)]
    sonesc = ph.sb("sonesc", [128, 1], BF16)
    sonesr = ph.sb("sonesr", [1, 128], F32)
    sid = ph.sb("sid", [128, 128], F32)
    sexp8 = ph.sb("sexp8", [8, 128], BF16)
    sgw = ph.sb("sgw", [128, 32, 8], BF16)
    sat = ph.sb("sat", [128, 32, 128], BF16)
    sbt = ph.sb("sbt", [128, 4096], BF16)
    sbb = ph.sb("sbb", [128, 32], F32)
    sgrow = ph.sb("sgrow", [8, 1024], F32)
    sgt = ph.sb("sgt", [128, 64], F32)
    swork = ph.sb("swork", [128, 32], F32)
    svec = ph.sb("svec", [128, 8], F32)
    swf = ph.sb("swf", [128, 64], F32)
    swfT = ph.sb("swfT", [8, 1024], BF16)
    swfe = ph.sb("swfe", [128, 1024], BF16)
    slw = ph.sb("slw", [128, 1024], BF16)
    srow = ph.sb("srow", [1, 4096], F32)
    sra = ph.sb("sra", [128, 1024], F32)
    srb2 = ph.sb("srb2", [128, 1024], F32)
    land2 = [ph.sb(f"land2{i}", [128, 1024], F32) for i in range(2)]
    stage2 = [ph.sb(f"stage2{i}", [128, 1024], F32) for i in range(2)]
    eps_t = ph.sb("eps_t", [1, 1], F32)

    yb = [ph.ps(f"yb{i}", [128, 512]) for i in range(4)]
    ssum = [ph.ps(f"ssum{i}", [8, 512]) for i in range(2)]
    ssq = [ph.ps(f"ssq{i}", [1, 512]) for i in range(2)]

    c_ld = ph.sem("c_ld")
    px_ld = [ph.sem("px_ld0"), ph.sem("px_ld1")]
    pr_cv = ph.sem("pr_cv")
    g_pe = ph.sem("g_pe")
    g_ve = ph.sem("g_ve")
    g_ac = ph.sem("g_ac")
    w_ld = [ph.sem("w_ld0"), ph.sem("w_ld1")]
    w_cv = ph.sem("w_cv")
    x_ld = [ph.sem("x_ld0"), ph.sem("x_ld1")]
    p_mm = ph.sem("p_mm")
    p_sd = ph.sem("p_sd")
    p_bc = ph.sem("p_bc")
    a_ev = ph.sem("a_ev")
    v_t = ph.sem("v_t")
    v_st = ph.sem("v_st")
    a_sq = ph.sem("a_sq")
    p_pre = [ph.sem("p_pre0"), ph.sem("p_pre1")]
    v_row = ph.sem("v_row")
    a_row = ph.sem("a_row")
    v_bc = ph.sem("v_bc")
    x2_ld = [ph.sem("x2_ld0"), ph.sem("x2_ld1")]
    v_p2 = ph.sem("v_p2")
    v_p3 = ph.sem("v_p3")
    p_out = [ph.sem("p_out0"), ph.sem("p_out1")]
    ini = ph.sem("ini")
    cch = ph.sem("cch")

    NOB = 32

    with nc.Block(name) as blk:

        @blk.sync
        def _(sy):
            sy.dma_start(out=sonesc[:, :], in_=ones_col[:, :]).then_inc(c_ld, 16)
            sy.dma_start(out=sonesr[:, :], in_=ones_row[:, :]).then_inc(c_ld, 16)
            sy.dma_start(out=sid[:, :], in_=ident_d[:, :]).then_inc(c_ld, 16)
            sy.dma_start(out=sexp8[:, :], in_=exp8_d[:, :]).then_inc(c_ld, 16)
            sy.dma_start(out=sbb[:, :], in_=BB_r[:, :]).then_inc(c_ld, 16)
            for ch in range(32):
                if ch >= 2:
                    sy.wait_ge(pr_cv, ch - 1)
                sy.dma_start(out=land[ch % 2][:, :],
                             in_=X3[ch * 128:(ch + 1) * 128, :]
                             ).then_inc(px_ld[ch % 2], 16)
            sy.wait_ge(pr_cv, 31)
            sy.dma_start(out=land[0][:, 0:256].rearrange("p (i e) -> p i e", i=32),
                         in_=GWT[:, :].rearrange("(i p) e -> p i e", p=128)
                         ).then_inc(c_ld, 16)
            for j in range(4):
                sy.wait_ge(pr_cv, 33 + j - 1 if j >= 2 else (33 if j == 0 else 32))
                sy.dma_start(
                    out=land[j % 2][:, :].rearrange("p (i e) -> p i e", i=8),
                    in_=ATALL[j * 1024:(j + 1) * 1024, :].rearrange(
                        "(i p) e -> p i e", p=128)).then_inc(c_ld, 16)
            for k in range(4):
                sy.wait_ge(pr_cv, 36 + k)
                sy.dma_start(out=land[k % 2][:, :],
                             in_=BTALL[:, k * 1024:(k + 1) * 1024]
                             ).then_inc(c_ld, 16)
            for ob in range(NOB):
                for hh in range(2):
                    pcn = 2 * ob + hh
                    if hh == 1:
                        sy.wait_ge(w_cv, pcn)
                    elif ob >= 2:
                        sy.wait_ge(w_cv, pcn - 2)
                    sy.dma_start(
                        out=landw[ob % 2][:, :, :],
                        in_=BW_r[ob, hh * 16:(hh + 1) * 16, :, :].rearrange(
                            "i p c -> p i c")).then_inc(w_ld[ob % 2], 16)
                if ob >= 2:
                    sy.wait_ge(v_t, 2 * (ob - 1))
                sy.dma_start(out=sx[ob % 2][:, :],
                             in_=X3[ob * 128:(ob + 1) * 128, :]
                             ).then_inc(x_ld[ob % 2], 16)
            for ch in range(NOB):
                sy.wait_ge(p_pre[ch % 2], 16 * (ch // 2 + 1))
                if ch >= 2:
                    sy.wait_ge(v_p2, ch - 1)
                sy.dma_start(out=land2[ch % 2][:, :],
                             in_=OUTPRE[ch * 128:(ch + 1) * 128, :]
                             ).then_inc(x2_ld[ch % 2], 16)

        @blk.vector
        def _(ve):
            ve.memset(eps_t[:, :], EPS).then_inc(ini, 1)
            for ch in range(32):
                ve.wait_ge(px_ld[ch % 2], 16 * (ch // 2 + 1))
                ve.tensor_copy(out=sxb[:, ch, :],
                               in_=land[ch % 2][:, :]).then_inc(pr_cv, 1)
            ve.wait_ge(c_ld, 96)
            ve.tensor_copy(out=sgw[:, :, :],
                           in_=land[0][:, 0:256].rearrange(
                               "p (i e) -> p i e", i=32)).then_inc(pr_cv, 1)
            for j in range(4):
                ve.wait_ge(c_ld, 128 if j < 2 else 160)
                ve.tensor_copy(out=sat[:, j * 8:(j + 1) * 8, :],
                               in_=land[j % 2][:, :].rearrange(
                                   "p (i e) -> p i e", i=8)).then_inc(pr_cv, 1)
            for k in range(4):
                ve.wait_ge(c_ld, 192 if k < 2 else 224)
                ve.tensor_copy(out=sbt[:, k * 1024:(k + 1) * 1024],
                               in_=land[k % 2][:, :]).then_inc(pr_cv, 1)
            ve.wait_ge(g_pe, 2)
            ve.tensor_copy(out=sgrow[:, 0:512], in_=ssum[0][0:8, :])
            ve.tensor_copy(out=sgrow[:, 512:1024],
                           in_=ssum[1][0:8, :]).then_inc(g_ve, 1)
            gv = 1
            io["_p7_a"] = a_marks = []
            io["_p7_b"] = b_marks = []
            io["_p7_h"] = h_marks = []
            io["_p7_l"] = l_marks = []
            for tt in range(8):
                ve.wait_ge(g_pe, 2 + tt + 1)
                ve.tensor_copy(out=sgt[:, tt * 8:(tt + 1) * 8],
                               in_=yb[0][:, 0:8]).then_inc(g_ve, 1)
                gv += 1; a_marks.append(gv)
                gts = sgt[:, tt * 8:(tt + 1) * 8]
                ve.wait_ge(g_ve, gv)
                if tt >= 1:
                    ve.wait_ge(g_ac, tt)
                ve.reduce_max(out=svec[:, 0:1], in_=gts, axis=AX,
                              negate=True).then_inc(g_ve, 1)
                gv += 1; b_marks.append(gv)
                ve.wait_ge(g_ve, gv)
                ve.tensor_scalar(out=svec[:, 1:2], in0=svec[:, 0:1],
                                 scalar1=-1.0, scalar2=None,
                                 op0=ALU.mult).then_inc(g_ve, 1)
                gv += 1
                ve.wait_ge(g_ve, gv)
                ve.tensor_scalar(out=swork[:, 0:8], in0=gts,
                                 scalar1=svec[:, 1:2], scalar2=None,
                                 op0=ALU.is_ge).then_inc(g_ve, 1)
                gv += 1
                ve.wait_ge(g_ve, gv)
                ve.scalar_tensor_tensor(out=swork[:, 8:16], in0=swork[:, 0:8],
                                        scalar=1e30, in1=gts,
                                        op0=ALU.mult,
                                        op1=ALU.subtract).then_inc(g_ve, 1)
                gv += 1
                ve.wait_ge(g_ve, gv)
                ve.tensor_reduce(out=svec[:, 2:3], in_=swork[:, 8:16], axis=AX,
                                 op=ALU.min, negate=True).then_inc(g_ve, 1)
                gv += 1
                ve.wait_ge(g_ve, gv)
                ve.tensor_scalar(out=swork[:, 24:32], in0=gts,
                                 scalar1=svec[:, 2:3], scalar2=None,
                                 op0=ALU.is_ge).then_inc(g_ve, 1)
                gv += 1
                ve.wait_ge(g_ac, tt + 1)
                ve.wait_ge(g_ve, gv)
                ve.tensor_tensor(out=swork[:, 16:24], in0=swork[:, 16:24],
                                 in1=swork[:, 24:32],
                                 op=ALU.mult).then_inc(g_ve, 1)
                gv += 1; h_marks.append(gv)
                ve.wait_ge(g_ve, gv)
                ve.reduce_sum(out=svec[:, 3:4], in_=swork[:, 16:24],
                              axis=AX).then_inc(g_ve, 1)
                gv += 1
                ve.wait_ge(g_ve, gv)
                ve.reciprocal(out=svec[:, 4:5], in_=svec[:, 3:4]).then_inc(g_ve, 1)
                gv += 1
                ve.wait_ge(g_ve, gv)
                ve.tensor_scalar(out=svec[:, 5:6], in0=svec[:, 4:5],
                                 scalar1=2.0, scalar2=None,
                                 op0=ALU.mult).then_inc(g_ve, 1)
                gv += 1
                ve.wait_ge(g_ve, gv)
                ve.tensor_scalar(out=swf[:, tt * 8:(tt + 1) * 8],
                                 in0=swork[:, 16:24],
                                 scalar1=svec[:, 5:6], scalar2=None,
                                 op0=ALU.mult).then_inc(g_ve, 1)
                gv += 1; l_marks.append(gv)
            t_marks = []
            for tt in range(8):
                ve.wait_ge(g_pe, 10 + 2 * (tt + 1))
                ve.tensor_copy(out=swfT[:, tt * 128:(tt + 1) * 128],
                               in_=yb[1][0:8, 0:128]).then_inc(g_ve, 1)
                gv += 1; t_marks.append(gv)
            io["_p7_t"] = t_marks
            ve.wait_ge(g_pe, 28)
            ve.tensor_copy(out=swfe[:, 0:512], in_=yb[2][:, :]).then_inc(g_ve, 1)
            gv += 1
            ve.tensor_copy(out=swfe[:, 512:1024],
                           in_=yb[3][:, :]).then_inc(g_ve, 1)
            gv += 1
            swfe_mark = gv
            ve.wait_ge(g_pe, 30)
            ve.wait_ge(g_ve, swfe_mark)
            ve.tensor_tensor(out=slw[:, 0:512], in0=yb[0][:, :],
                             in1=swfe[:, 0:512], op=ALU.mult).then_inc(g_ve, 1)
            gv += 1
            ve.tensor_tensor(out=slw[:, 512:1024], in0=yb[1][:, :],
                             in1=swfe[:, 512:1024], op=ALU.mult).then_inc(g_ve, 1)
            gv += 1
            io["_p7_done"] = gv
            for ob in range(NOB):
                for hh in range(2):
                    ve.wait_ge(w_ld[ob % 2], 32 * (ob // 2) + 16 * (hh + 1))
                    if ob >= 2 and hh == 0:
                        ve.wait_ge(p_mm, 2 * (ob - 1))
                    ve.tensor_copy(
                        out=swb[ob % 2][:, hh * 2048:(hh + 1) * 2048].rearrange(
                            "p (i c) -> p i c", i=16),
                        in_=landw[ob % 2][:, :, :]).then_inc(w_cv, 1)
                for half in range(2):
                    hs = slice(half * 512, (half + 1) * 512)
                    ve.wait_ge(a_ev, 2 * ob + half + 1)
                    ve.wait_ge(x_ld[ob % 2], 16 * (ob // 2 + 1))
                    if ob >= 2:
                        ve.wait_ge(p_pre[ob % 2], 16 * (ob // 2))
                        ve.wait_ge(a_sq, ob - 1)
                        ve.wait_ge(p_sd, ob - 1)
                    ve.tensor_tensor(out=st_t[ob % 2][:, hs],
                                     in0=sy1[ob % 2][:, hs],
                                     in1=sx[ob % 2][:, hs],
                                     op=ALU.add).then_inc(v_t, 1)
                    ve.wait_ge(v_t, 2 * ob + half + 1)
                    ins = ve.tensor_copy(out=stb[ob % 2][:, hs],
                                         in_=st_t[ob % 2][:, hs])
                    if half == 1:
                        ins.then_inc(v_st, 1)
            ve.wait_ge(p_sd, NOB)
            ve.tensor_copy(out=srow[:, 0:512], in_=ssum[0][0:1, :]).then_inc(cch, 1)
            ve.tensor_copy(out=srow[:, 512:1024],
                           in_=ssum[1][0:1, :]).then_inc(cch, 1)
            ve.tensor_copy(out=srow[:, 1024:1536], in_=ssq[0][:, :]).then_inc(cch, 1)
            ve.tensor_copy(out=srow[:, 1536:2048], in_=ssq[1][:, :]).then_inc(cch, 1)
            ve.wait_ge(cch, 4)
            ve.tensor_scalar(out=srow[:, 2048:3072], in0=srow[:, 0:1024],
                             scalar1=1.0 / C, scalar2=None,
                             op0=ALU.mult).then_inc(cch, 1)
            ve.tensor_scalar(out=srow[:, 3072:4096], in0=srow[:, 1024:2048],
                             scalar1=1.0 / C, scalar2=None,
                             op0=ALU.mult).then_inc(cch, 1)
            ve.wait_ge(cch, 6)
            ve.tensor_tensor(out=srow[:, 0:1024], in0=srow[:, 2048:3072],
                             in1=srow[:, 2048:3072], op=ALU.mult).then_inc(cch, 1)
            ve.wait_ge(cch, 7)
            ve.tensor_tensor(out=srow[:, 1024:2048], in0=srow[:, 3072:4096],
                             in1=srow[:, 0:1024],
                             op=ALU.subtract).then_inc(v_row, 1)
            ve.wait_ge(a_row, 1)
            ve.reciprocal(out=srow[:, 3072:4096],
                          in_=srow[:, 0:1024]).then_inc(cch, 1)
            ve.wait_ge(cch, 8)
            ve.tensor_tensor(out=srow[:, 1024:2048], in0=srow[:, 2048:3072],
                             in1=srow[:, 3072:4096], op=ALU.mult).then_inc(cch, 1)
            ve.wait_ge(cch, 9)
            ve.tensor_scalar(out=srow[:, 1024:2048], in0=srow[:, 1024:2048],
                             scalar1=-1.0, scalar2=None,
                             op0=ALU.mult).then_inc(v_row, 1)
            ve.wait_ge(p_bc, 1)
            ve.tensor_copy(out=sra[:, 0:512], in_=yb[0][:, :]).then_inc(cch, 1)
            ve.tensor_copy(out=sra[:, 512:1024], in_=yb[1][:, :]).then_inc(cch, 1)
            ve.tensor_copy(out=srb2[:, 0:512], in_=yb[2][:, :]).then_inc(cch, 1)
            ve.tensor_copy(out=srb2[:, 512:1024],
                           in_=yb[3][:, :]).then_inc(v_bc, 1)
            for ch in range(NOB):
                ve.wait_ge(x2_ld[ch % 2], 16 * (ch // 2 + 1))
                ve.wait_ge(v_bc, 1)
                ve.wait_ge(cch, 12)
                if ch >= 2:
                    ve.wait_ge(p_out[ch % 2], 16 * (ch // 2))
                ve.tensor_tensor(out=stage2[ch % 2][:, :],
                                 in0=land2[ch % 2][:, :],
                                 in1=sra[:, :], op=ALU.mult).then_inc(v_p2, 1)
                ve.wait_ge(v_p2, ch + 1)
                ve.tensor_tensor(out=stage2[ch % 2][:, :],
                                 in0=stage2[ch % 2][:, :],
                                 in1=srb2[:, :], op=ALU.add).then_inc(v_p3, 1)

        @blk.scalar
        def _(sc):
            for tt in range(8):
                sc.wait_ge(g_ve, io["_p7_b"][tt])
                if tt >= 1:
                    sc.wait_ge(g_ve, io["_p7_h"][tt - 1])
                sc.activation(out=swork[:, 16:24],
                              in_=sgt[:, tt * 8:(tt + 1) * 8], func=AF.Exp,
                              bias=svec[:, 0:1]).then_inc(g_ac, 1)
            for ob in range(NOB):
                for half in range(2):
                    hs = slice(half * 512, (half + 1) * 512)
                    sc.wait_ge(p_mm, 2 * ob + half + 1)
                    if ob >= 2:
                        sc.wait_ge(v_t, 2 * (ob - 1))
                    sc.activation(out=sy1[ob % 2][:, hs],
                                  in_=yb[(ob % 2) * 2 + half][:, :],
                                  func=AF.Identity, scale=1.0,
                                  bias=sbb[:, ob:ob + 1]).then_inc(a_ev, 1)
                    sc.wait_ge(v_t, 2 * ob + half + 1)
                    if ob >= 2:
                        sc.wait_ge(p_sd, ob - 1)
                    ins = sc.activation(out=stsq[ob % 2][:, hs],
                                        in_=st_t[ob % 2][:, hs], func=AF.Square)
                    if half == 1:
                        ins.then_inc(a_sq, 1)
            sc.wait_ge(v_row, 1)
            sc.wait_ge(ini, 1)
            sc.activation(out=srow[:, 0:1024], in_=srow[:, 1024:2048],
                          func=AF.Sqrt, bias=eps_t[:, 0:1]).then_inc(a_row, 1)

        def _stats7(te, o):
            te.wait_ge(v_st, o + 1)
            te.wait_ge(a_sq, o + 1)
            for half in range(2):
                hs = slice(half * 512, (half + 1) * 512)
                te.matmul(out=ssum[half][0:1, :], lhsT=sonesc[:, :],
                          rhs=stb[o % 2][:, hs], start=(o == 0), stop=(o == NOB - 1),
                          skip_group_check=True)
                ins = te.matmul(out=ssq[half][:, :], lhsT=sonesc[:, :],
                                rhs=stsq[o % 2][:, hs], start=(o == 0),
                                stop=(o == NOB - 1), skip_group_check=True)
            ins.then_inc(p_sd, 1)

        @blk.tensor
        def _(te):
            te.wait_ge(c_ld, 96)
            te.wait_ge(pr_cv, 37)
            for half in range(2):
                for i in range(32):
                    ins = te.matmul(out=ssum[half][0:8, :],
                                    lhsT=sgw[:, i, :],
                                    rhs=sxb[:, i, half * 512:(half + 1) * 512],
                                    start=(i == 0), stop=(i == 31),
                                    skip_group_check=True)
            ins.then_inc(g_pe, 2)
            for tt in range(8):
                te.wait_ge(g_ve, 1)
                if tt >= 1:
                    te.wait_ge(g_ve, io["_p7_a"][tt - 1])
                te.transpose(out=yb[0][:, 0:8],
                             in_=sgrow[0:8, tt * 128:(tt + 1) * 128],
                             identity=sid[0:8, 0:8]).then_inc(g_pe, 1)
            for tt in range(8):
                te.wait_ge(g_ve, io["_p7_l"][tt])
                if tt >= 1:
                    te.wait_ge(g_ve, io["_p7_t"][tt - 1])
                te.transpose(out=yb[1][0:8, 0:128],
                             in_=swf[:, tt * 8:(tt + 1) * 8],
                             identity=sid[:, :]).then_inc(g_pe, 2)
            te.wait_ge(g_ve, io["_p7_t"][7])
            te.matmul(out=yb[2][:, :], lhsT=sexp8[:, :], rhs=swfT[0:8, 0:512],
                      start=True, stop=True)
            te.matmul(out=yb[3][:, :], lhsT=sexp8[:, :], rhs=swfT[0:8, 512:1024],
                      start=True, stop=True).then_inc(g_pe, 2)
            for half in range(2):
                for i in range(32):
                    ins = te.matmul(out=yb[half][:, :], lhsT=sat[:, i, :],
                                    rhs=sxb[:, i, half * 512:(half + 1) * 512],
                                    start=(i == 0), stop=(i == 31))
            ins.then_inc(g_pe, 2)
            te.wait_ge(pr_cv, 41)
            te.wait_ge(g_ve, io["_p7_done"])
            for ob in range(NOB):
                te.wait_ge(w_cv, 2 * ob + 2)
                if ob >= 2:
                    te.wait_ge(a_ev, 2 * (ob - 1))
                for half in range(2):
                    hs = slice(half * 512, (half + 1) * 512)
                    for cc in range(32):
                        te.matmul(
                            out=yb[(ob % 2) * 2 + half][:, :],
                            lhsT=swb[ob % 2][:, cc * 128:(cc + 1) * 128],
                            rhs=sxb[:, cc, hs], start=(cc == 0), stop=False)
                    te.matmul(out=yb[(ob % 2) * 2 + half][:, :],
                              lhsT=sbt[:, ob * 128:(ob + 1) * 128],
                              rhs=slw[:, hs],
                              start=False, stop=True).then_inc(p_mm, 1)
                if ob >= 1:
                    _stats7(te, ob - 1)
            _stats7(te, NOB - 1)
            te.wait_ge(v_row, 2)
            te.wait_ge(a_ev, 2 * NOB)
            te.matmul(out=yb[0][:, :], lhsT=sonesr[:, :],
                      rhs=srow[:, 3072:3584], start=True, stop=True)
            te.matmul(out=yb[1][:, :], lhsT=sonesr[:, :],
                      rhs=srow[:, 3584:4096], start=True, stop=True)
            te.matmul(out=yb[2][:, :], lhsT=sonesr[:, :],
                      rhs=srow[:, 1024:1536], start=True, stop=True)
            te.matmul(out=yb[3][:, :], lhsT=sonesr[:, :],
                      rhs=srow[:, 1536:2048], start=True, stop=True).then_inc(p_bc, 1)

        @blk.gpsimd
        def _(gp):
            for ob in range(NOB):
                gp.wait_ge(v_t, 2 * ob + 2)
                gp.wait_ge(a_sq, ob + 1)
                gp.dma_start(out=OUTPRE[ob * 128:(ob + 1) * 128, :],
                             in_=st_t[ob % 2][:, :]).then_inc(p_pre[ob % 2], 16)
            for ch in range(NOB):
                gp.wait_ge(v_p3, ch + 1)
                gp.dma_start(out=OUT[ch * 128:(ch + 1) * 128, :],
                             in_=stage2[ch % 2][:, :]).then_inc(p_out[ch % 2], 16)
            gp.wait_ge(p_out[0], 16 * 16)
            gp.wait_ge(p_out[1], 16 * 16)

    ph.close()


def build_launch2():
    nc = bass.Bass("TRN2", target_bir_lowering=False)
    io = {}
    io["ones_col"] = _dram(nc, "ones_col", [128, 1], BF16, "ExternalInput")
    io["ones_row"] = _dram(nc, "ones_row", [1, 128], F32, "ExternalInput")
    OT = _dram(nc, "OT", [C, TPC], BF16, "ExternalInput")
    wo_r = _dram(nc, "wo_r", [32, 32, 128, 128], F32, "ExternalInput")
    xres = _dram(nc, "xres", [C, TPC], F32, "ExternalInput")
    qw_r = _dram(nc, "qw_r", [32, 32, 128, 128], I32, "ExternalInput")
    qb_r = _dram(nc, "qb_r", [128, 32], I32, "ExternalInput")
    ws = _dram(nc, "ws", [1, 1], F32, "ExternalInput")
    wzp = _dram(nc, "wzp", [1, 1], I32, "ExternalInput")
    bs = _dram(nc, "bs", [1, 1], F32, "ExternalInput")
    bzp = _dram(nc, "bzp", [1, 1], I32, "ExternalInput")
    wq1_r = _dram(nc, "wq1_r", [32, 32, 128, 128], F32, "ExternalInput")
    wk1 = _dram(nc, "wk1", [C, 128], F32, "ExternalInput")
    wv1 = _dram(nc, "wv1", [C, 128], F32, "ExternalInput")
    X1 = _dram(nc, "X1", [C, TPC], F32, "Internal")
    X1p = _dram(nc, "X1p", [C, TPC], F32, "Internal")
    X2p = _dram(nc, "X2p", [C, TPC], F32, "Internal")
    X2 = _dram(nc, "X2", [C, TPC], F32, "ExternalOutput")
    QT1 = _dram(nc, "QT1", [C, TPC], BF16, "ExternalOutput")
    KT1c = _dram(nc, "KT1c", [128, TPC], BF16, "ExternalOutput")
    V1c = _dram(nc, "V1c", [TPC, 128], BF16, "ExternalOutput")
    emit_woln(nc, io, "p2", OT, wo_r, xres, X1, X1p)
    nc.all_engine_barrier(sem_only=True)
    emit_ffn0(nc, io, "p3", X1, qw_r, qb_r, ws, wzp, bs, bzp, X2, X2p)
    nc.all_engine_barrier(sem_only=True)
    emit_p4(nc, io, "p4", X2, wq1_r, wk1, wv1, QT1, KT1c, V1c)
    return nc


def build_launch3():
    nc = bass.Bass("TRN2", target_bir_lowering=False)
    io = {}
    io["ones_col"] = _dram(nc, "ones_col", [128, 1], BF16, "ExternalInput")
    io["ones_row"] = _dram(nc, "ones_row", [1, 128], F32, "ExternalInput")
    io["iota"] = _dram(nc, "iota", [128, 512], F32, "ExternalInput")
    io["ident"] = _dram(nc, "ident", [128, 128], F32, "ExternalInput")
    io["exp8"] = _dram(nc, "exp8", [8, 128], BF16, "ExternalInput")
    X2 = _dram(nc, "X2", [C, TPC], F32, "ExternalInput")
    QT1 = _dram(nc, "QT1", [C, TPC], BF16, "ExternalInput")
    KG1 = _dram(nc, "KG1", [128, T], BF16, "ExternalInput")
    VG1 = _dram(nc, "VG1", [T, 128], BF16, "ExternalInput")
    tthr = _dram(nc, "tthr", [128, 48], F32, "ExternalInput")
    wo1_r = _dram(nc, "wo1_r", [32, 32, 128, 128], F32, "ExternalInput")
    bw_r = _dram(nc, "bw_r", [32, 32, 128, 128], F32, "ExternalInput")
    gwT = _dram(nc, "gwT", [C, 8], F32, "ExternalInput")
    atall = _dram(nc, "atall", [C, 128], F32, "ExternalInput")
    btall = _dram(nc, "btall", [128, C], F32, "ExternalInput")
    bb_r = _dram(nc, "bb_r", [128, 32], F32, "ExternalInput")
    OT1 = _dram(nc, "OT1", [C, TPC], BF16, "Internal")
    X3 = _dram(nc, "X3", [C, TPC], F32, "Internal")
    X3p = _dram(nc, "X3p", [C, TPC], F32, "Internal")
    OUTp = _dram(nc, "OUTp", [C, TPC], F32, "Internal")
    OUT = _dram(nc, "OUT", [C, TPC], F32, "ExternalOutput")
    emit_p5(nc, io, "p5", QT1, KG1, VG1, tthr, OT1)
    nc.all_engine_barrier(sem_only=True)
    emit_woln(nc, io, "p6", OT1, wo1_r, X2, X3, X3p)
    nc.all_engine_barrier(sem_only=True)
    emit_ffn1(nc, io, "p7", X3, bw_r, gwT, atall, btall, bb_r, OUT, OUTp)
    return nc


_NC_CACHE = {}


def _get_nc(which):
    if which not in _NC_CACHE:
        _NC_CACHE[which] = {1: build_launch1, 2: build_launch2,
                            3: build_launch3}[which]()
    return _NC_CACHE[which]


def kernel(**inputs):
    cst = _consts()
    cosT, sin2 = _rope_tables()
    x = np.asarray(inputs["L0_attn_x"])

    in_maps1 = prep_launch1(inputs)
    res1 = run_bass_kernel_spmd(_get_nc(1), in_maps1,
                                core_ids=list(range(NCORE))).results

    wo_r = _wr_blocks(np.ascontiguousarray(
        np.asarray(inputs["L0_attn_wo"]).T))
    qw_r = np.ascontiguousarray(np.asarray(
        inputs["L0_ffn_quantized_weight"]).T.reshape(
            32, 128, 32, 128).transpose(2, 0, 1, 3))
    qb_r = np.ascontiguousarray(np.asarray(
        inputs["L0_ffn_quantized_bias"]).reshape(32, 128).T)
    wq1_r = np.ascontiguousarray(np.asarray(
        inputs["L1_attn_wq"]).reshape(32, 128, 32, 128).transpose(2, 0, 1, 3))
    wk1 = np.ascontiguousarray(np.asarray(inputs["L1_attn_wk"]))
    wv1 = np.ascontiguousarray(np.asarray(inputs["L1_attn_wv"]))
    in_maps2 = []
    for c in range(NCORE):
        b, j = c // 4, c % 4
        tok = _tok_idx(j)
        OT = np.concatenate(
            [np.asarray(res1[b * 4 + g]["OTh"])[:, tok] for g in range(4)],
            axis=0)
        in_maps2.append({
            "ones_col": cst["ones_col"], "ones_row": cst["ones_row_f32"],
            "OT": np.ascontiguousarray(OT),
            "wo_r": wo_r,
            "xres": np.ascontiguousarray(x[b][tok, :].T),
            "qw_r": qw_r, "qb_r": qb_r,
            "ws": np.asarray(inputs["L0_ffn_weight_scale"],
                             dtype=np.float32).reshape(1, 1),
            "wzp": np.asarray(inputs["L0_ffn_weight_zero_point"],
                              dtype=np.int32).reshape(1, 1),
            "bs": np.asarray(inputs["L0_ffn_bias_scale"],
                             dtype=np.float32).reshape(1, 1),
            "bzp": np.asarray(inputs["L0_ffn_bias_zero_point"],
                              dtype=np.int32).reshape(1, 1),
            "wq1_r": wq1_r, "wk1": wk1, "wv1": wv1,
        })
    res2 = run_bass_kernel_spmd(_get_nc(2), in_maps2,
                                core_ids=list(range(NCORE))).results

    wo1_r = _wr_blocks(np.ascontiguousarray(np.asarray(inputs["L1_attn_wo"])))
    bw_r = _wr_blocks(np.ascontiguousarray(
        np.asarray(inputs["L1_ffn_base_w"]).T))
    gwT = np.ascontiguousarray(np.asarray(inputs["L1_ffn_gate_w"]).T)
    atall = np.ascontiguousarray(np.asarray(
        inputs["L1_ffn_lora_A"]).transpose(2, 0, 1).reshape(C, 128))
    btall = np.ascontiguousarray(np.asarray(
        inputs["L1_ffn_lora_B"]).transpose(0, 2, 1).reshape(128, C))
    bb_r = np.ascontiguousarray(np.asarray(
        inputs["L1_ffn_base_b"]).reshape(32, 128).T)
    KG = []
    VG = []
    for b in range(B):
        kg = np.empty((128, T), dtype=BF)
        vg = np.empty((T, 128), dtype=BF)
        for g in range(8):
            j = g if g < 4 else 7 - g
            half = 0 if g < 4 else 1
            kt = np.asarray(res2[b * 4 + j]["KT1c"])
            vt = np.asarray(res2[b * 4 + j]["V1c"])
            kg[:, g * 512:(g + 1) * 512] = kt[:, half * 512:(half + 1) * 512]
            vg[g * 512:(g + 1) * 512, :] = vt[half * 512:(half + 1) * 512, :]
        KG.append(kg)
        VG.append(vg)
    in_maps3 = []
    for c in range(NCORE):
        b, j = c // 4, c % 4
        in_maps3.append({
            "ones_col": cst["ones_col"], "ones_row": cst["ones_row_f32"],
            "iota": cst["iota"], "ident": cst["ident"], "exp8": cst["exp8"],
            "X2": np.asarray(res2[c]["X2"]),
            "QT1": np.asarray(res2[c]["QT1"]),
            "KG1": KG[b], "VG1": VG[b],
            "tthr": _tthr(j),
            "wo1_r": wo1_r, "bw_r": bw_r, "gwT": gwT,
            "atall": atall, "btall": btall, "bb_r": bb_r,
        })
    res3 = run_bass_kernel_spmd(_get_nc(3), in_maps3,
                                core_ids=list(range(NCORE))).results

    out = np.empty((B, T, C), dtype=np.float32)
    for c in range(NCORE):
        b, j = c // 4, c % 4
        out[b, _tok_idx(j), :] = np.asarray(res3[c]["OUT"]).T
    return out


# version 05
# speedup vs baseline: 1.0978x; 1.0978x over previous
import math
from contextlib import ExitStack

import numpy as np
import ml_dtypes

import concourse.bass as bass
import concourse.mybir as mybir
from concourse.bass_utils import run_bass_kernel_spmd

F32 = mybir.dt.float32
BF16 = mybir.dt.bfloat16
I32 = mybir.dt.int32
AF = mybir.ActivationFunctionType
ALU = mybir.AluOpType
AX = mybir.AxisListType.X
BF = ml_dtypes.bfloat16

B, T, C, NH, HD = 2, 4096, 4096, 32, 128
NCORE = 8
TPC = 1024
NCH = C // 128
SOFT_B = 20.0
SCALE = 1.0 / math.sqrt(HD)
EPS = 1e-5


def _tok_idx(j):
    a0 = np.arange(j * 512, (j + 1) * 512)
    a1 = np.arange((7 - j) * 512, (8 - j) * 512)
    return np.concatenate([a0, a1])


def _wr_blocks(wT):
    return np.ascontiguousarray(
        wT.reshape(32, 128, 32, 128).transpose(2, 0, 1, 3))


def _rope_tables():
    pos = np.arange(T, dtype=np.float32)
    thetas = (1.0 / 10000.0 ** (np.arange(0, HD, 2, dtype=np.float32) / HD))
    pt = np.outer(pos, thetas)
    ptc = np.concatenate([pt, pt], axis=1)
    cosT = np.ascontiguousarray(np.cos(ptc).T)
    sinT = np.sin(ptc).T
    sign = np.where(np.arange(HD) < HD // 2, -1.0, 1.0).astype(np.float32)
    sin2 = np.ascontiguousarray(sinT * sign[:, None])
    return cosT.astype(np.float32), sin2.astype(np.float32)


def _consts():
    iota = np.broadcast_to(np.arange(512, dtype=np.float32), (128, 512)).copy()
    ident = np.eye(128, dtype=np.float32)
    ones_col = np.ones((128, 1), dtype=BF)
    ones_row_f32 = np.ones((1, 128), dtype=np.float32)
    f = np.arange(512)[None, :]
    p = np.arange(128)[:, None]
    tri = np.stack([(f >= 128 * i + p) for i in range(4)]).astype(BF)
    exp8 = np.zeros((8, 128), dtype=BF)
    for e in range(8):
        exp8[e, e * 16:(e + 1) * 16] = 1.0
    return dict(iota=iota, ident=ident, ones_col=ones_col,
                ones_row_f32=ones_row_f32, tri=tri, exp8=exp8)


def _tthr(j):
    out = np.zeros((128, 48), dtype=np.float32)
    p = np.arange(128, dtype=np.float32)
    col = 0
    for a, pa in ((0, 16), (1, 32)):
        gq = j if a == 0 else 7 - j
        for b in range(pa):
            out[:, col] = b * 128 + p - gq * 512
            col += 1
    return out



class SemPool:

    def __init__(self, nc, n=60, base=192):
        self.sems = [nc.alloc_semaphore(f"pool{i}", num=base + i)
                     for i in range(n)]
        self.acc = {}
        self.idx = 0

    def take(self):
        s = self.sems[self.idx]
        self.idx += 1
        return s

    def reset(self):
        self.idx = 0


class P:

    def __init__(self, nc, name, pool=None):
        self.nc = nc
        self.name = name
        self.pool = pool
        self.es = ExitStack()
        self._sems = {}
        self._base = {}

    def sem(self, n):
        if n not in self._sems:
            if self.pool is not None:
                h = self.pool.take()
                self._base[h.num] = self.pool.acc.get(h.num, 0)
            else:
                h = self.nc.alloc_semaphore(f"{self.name}_{n}")
                self._base[h.num] = 0
            self._sems[n] = h
        return self._sems[n]

    def w(self, eng, sem, v):
        eng.wait_ge(sem, self._base[sem.num] + v)

    def a(self, sem, k):
        if self.pool is not None:
            self.pool.acc[sem.num] = self.pool.acc.get(sem.num, 0) + k

    def sb(self, n, shape, dtype):
        return self.es.enter_context(
            self.nc.sbuf_tensor(f"{self.name}_{n}", shape, dtype))

    def ps(self, n, shape, dtype=F32):
        return self.es.enter_context(
            self.nc.psum_tensor(f"{self.name}_{n}", shape, dtype))

    def close(self):
        self.es.close()



def emit_p0(nc, io):
    ph = P(nc, "p0", io.get("pool"))
    xTb, wq_r, wk_r = io["xTb"], io["wq_r"], io["wk_r"]
    ropec, ropes2 = io["ropec"], io["ropes2"]
    QT0, KT0, xbd = io["QT0"], io["KT0"], io["xbd"]

    land = [ph.sb(f"land{i}", [128, 8, 512], F32) for i in range(2)]
    xa = [ph.sb(f"xa{i}", [128, 8, 512], BF16) for i in range(2)]
    landw = ph.sb("landw", [128, 16, 128], F32)
    wqb = [ph.sb(f"wqb{i}", [128, 8192], BF16) for i in range(2)]
    wkb = [ph.sb(f"wkb{i}", [128, 8192], BF16) for i in range(2)]
    xb = [ph.sb(f"xb{i}", [128, 32, 512], BF16) for i in range(2)]
    scc = [ph.sb(f"scc{i}", [128, 512], F32) for i in range(2)]
    ssn = [ph.sb(f"ssn{i}", [128, 512], F32) for i in range(2)]
    qs = [ph.sb(f"qs{i}", [128, 512], F32) for i in range(2)]
    t1 = ph.sb("t1", [128, 512], F32)
    t2 = ph.sb("t2", [128, 512], F32)
    stg = [ph.sb(f"stg{i}", [128, 512], BF16) for i in range(4)]

    bqk = [ph.ps(f"bqk{i}", [128, 512]) for i in range(4)]

    a_ld = [ph.sem("a_ld0"), ph.sem("a_ld1")]
    a_st = [ph.sem("a_st0"), ph.sem("a_st1")]
    w_ld = ph.sem("w_ld")
    w_cv = ph.sem("w_cv")
    r_ld = [ph.sem("r_ld0"), ph.sem("r_ld1")]
    x_ld = [ph.sem("x_ld0"), ph.sem("x_ld1")]
    mm = ph.sem("mm")
    rp = ph.sem("rp")
    rpd = ph.sem("rpd")
    ev = ph.sem("ev")
    outq = [ph.sem("outq0"), ph.sem("outq1")]
    acv = ph.sem("acv")
    ch_ve = ph.sem("ch_ve")

    NSUB = 32
    NTRG = io.get("p0_ntrg", 32)

    with nc.Block("p0") as blk:

        SKIP_A = io.get("p0_skip_a", False)
        NO_ROPE = io.get("p0_no_rope", False)
        NO_STORE = io.get("p0_no_store", False)
        NO_ACT = io.get("p0_no_act", False)

        @blk.sync
        def _(sy):
            if SKIP_A:
                return
            for sub in range(NSUB):
                tr, q = sub // 4, sub % 4
                if sub >= 2:
                    ph.w(sy, acv, sub - 1)
                sy.dma_start(
                    out=land[sub % 2][:, :, :],
                    in_=xTb[q * 1024:(q + 1) * 1024,
                            tr * 512:(tr + 1) * 512].rearrange(
                        "(i p) c -> p i c", p=128)).then_inc(a_ld[sub % 2], 16); ph.a(a_ld[sub % 2], 16)

        @blk.scalar
        def _(sc):
            def load_w(hp):
                for w, src in enumerate((wq_r, wk_r)):
                    for pc in range(4):
                        idx = hp * 8 + w * 4 + pc
                        if idx > 0:
                            ph.w(sc, w_cv, idx)
                        h, ir = pc // 2, (pc % 2) * 16
                        sc.dma_start(
                            out=landw[:, :, :],
                            in_=src[2 * hp + h, ir:ir + 16, :, :].rearrange(
                                "i p c -> p i c")).then_inc(w_ld, 16); ph.a(w_ld, 16)

            def load_xb(trg):
                hp, tr = trg // 8, trg % 8
                if not SKIP_A:
                    ph.w(sc, a_st[0], 16 * (2 * tr + 2))
                    ph.w(sc, a_st[1], 16 * (2 * tr + 2))
                if trg >= 2:
                    ph.w(sc, mm, trg - 1)
                sc.dma_start(
                    out=xb[trg % 2][:, :, :],
                    in_=xbd[:, tr * 512:(tr + 1) * 512].rearrange(
                        "(i p) c -> p i c", p=128)).then_inc(x_ld[trg % 2], 16); ph.a(x_ld[trg % 2], 16)

            load_w(0)
            load_xb(0)
            load_xb(1)
            if not NO_ROPE:
                sc.dma_start(out=scc[0][:, :],
                             in_=ropec[:, 0:512]).then_inc(r_ld[0], 16); ph.a(r_ld[0], 16)
                sc.dma_start(out=ssn[0][:, :],
                             in_=ropes2[:, 0:512]).then_inc(r_ld[0], 16); ph.a(r_ld[0], 16)
                sc.dma_start(out=scc[1][:, :],
                             in_=ropec[:, 512:1024]).then_inc(r_ld[1], 16); ph.a(r_ld[1], 16)
                sc.dma_start(out=ssn[1][:, :],
                             in_=ropes2[:, 512:1024]).then_inc(r_ld[1], 16); ph.a(r_ld[1], 16)
            for trg in range(NTRG):
                hp, tr = trg // 8, trg % 8
                if trg < NTRG - 2:
                    load_xb(trg + 2)
                if tr == 4 and hp < 3 and (hp + 1) * 8 < NTRG:
                    load_w(hp + 1)
                if not NO_ROPE and trg + 2 < NTRG:
                    tr2 = (trg + 2) % 8
                    ph.w(sc, ev, 4 * (trg + 1))
                    sc.dma_start(out=scc[trg % 2][:, :],
                                 in_=ropec[:, tr2 * 512:(tr2 + 1) * 512]
                                 ).then_inc(r_ld[trg % 2], 16); ph.a(r_ld[trg % 2], 16)
                    sc.dma_start(out=ssn[trg % 2][:, :],
                                 in_=ropes2[:, tr2 * 512:(tr2 + 1) * 512]
                                 ).then_inc(r_ld[trg % 2], 16); ph.a(r_ld[trg % 2], 16)

        @blk.gpsimd
        def _(gp):
            def p0a(sub):
                ph.w(gp, a_ld[sub % 2], 16 * (sub // 2 + 1))
                if sub >= 2:
                    ph.w(gp, a_st[sub % 2], 16 * (sub // 2))
                tr, q = sub // 4, sub % 4
                gp.tensor_copy(out=xa[sub % 2][:, :, :],
                               in_=land[sub % 2][:, :, :]).then_inc(acv, 1); ph.a(acv, 1)
                ph.w(gp, acv, sub + 1)
                gp.dma_start(
                    out=xbd[q * 1024:(q + 1) * 1024,
                            tr * 512:(tr + 1) * 512].rearrange(
                        "(i p) c -> p i c", p=128),
                    in_=xa[sub % 2][:, :, :]).then_inc(a_st[sub % 2], 16); ph.a(a_st[sub % 2], 16)

            if not SKIP_A:
                for sub in range(12):
                    p0a(sub)
            for trg in range(NTRG):
                if not SKIP_A and trg < 5:
                    for sub in range(12 + 4 * trg, 16 + 4 * trg):
                        p0a(sub)
                hp, tr = trg // 8, trg % 8
                if NO_STORE:
                    continue
                for bk in range(4):
                    head = 2 * hp + (bk % 2)
                    dst = QT0 if bk < 2 else KT0
                    ph.w(gp, ev, trg * 4 + bk + 1)
                    gp.dma_start(out=dst[head, :, tr * 512:(tr + 1) * 512],
                                 in_=stg[bk][:, :]).then_inc(outq[trg % 2], 16); ph.a(outq[trg % 2], 16)
            if not NO_STORE:
                ph.w(gp, outq[0], 64 * ((NTRG + 1) // 2))
                ph.w(gp, outq[1], 64 * (NTRG // 2))
            ph.w(gp, ev, 4 * NTRG)
            if not SKIP_A:
                ph.w(gp, a_st[0], 16 * 16)
                ph.w(gp, a_st[1], 16 * 16)

        @blk.vector
        def _(ve):
            def conv_w(hp):
                for pc in range(8):
                    idx = hp * 8 + pc
                    ph.w(ve, w_ld, 16 * (idx + 1))
                    if hp >= 2:
                        ph.w(ve, mm, 8 * (hp - 1))
                    dst = wqb[hp % 2] if pc < 4 else wkb[hp % 2]
                    off = (pc % 4) * 2048
                    ve.tensor_copy(
                        out=dst[:, off:off + 2048].rearrange(
                            "p (i c) -> p i c", i=16),
                        in_=landw[:, :, :]).then_inc(w_cv, 1); ph.a(w_cv, 1)

            conv_w(0)
            for trg in range(NTRG):
                hp, tr = trg // 8, trg % 8
                if tr == 4 and hp < 3 and (hp + 1) * 8 < NTRG:
                    conv_w(hp + 1)
                ph.w(ve, mm, trg + 1)
                if not NO_ROPE:
                    ph.w(ve, r_ld[trg % 2], 32 * (trg // 2 + 1))
                if not NO_STORE and trg >= 1:
                    ph.w(ve, outq[(trg - 1) % 2], 64 * ((trg - 1) // 2 + 1))
                for bk in range(4):
                    g = trg * 4 + bk
                    if NO_ROPE:
                        ve.tensor_copy(out=stg[bk][:, :],
                                       in_=bqk[bk][:, :]).then_inc(ev, 1); ph.a(ev, 1)
                        continue
                    if g >= 1:
                        ph.w(ve, ev, g)
                    if g >= 2:
                        ph.w(ve, rpd, g - 1)
                    ve.tensor_copy(out=qs[bk % 2][0:64, :],
                                   in_=bqk[bk][64:128, :])
                    ve.tensor_copy(out=qs[bk % 2][64:128, :],
                                   in_=bqk[bk][0:64, :]).then_inc(rp, 1); ph.a(rp, 1)
                    ve.tensor_tensor(out=t1[:, :], in0=bqk[bk][:, :],
                                     in1=scc[trg % 2][:, :],
                                     op=ALU.mult).then_inc(ch_ve, 1); ph.a(ch_ve, 1)
                    ph.w(ve, rp, g + 1)
                    ve.tensor_tensor(out=t2[:, :], in0=qs[bk % 2][:, :],
                                     in1=ssn[trg % 2][:, :],
                                     op=ALU.mult).then_inc(rpd, 1); ph.a(rpd, 1)
                    ph.w(ve, ch_ve, g + 1)
                    ph.w(ve, rpd, g + 1)
                    ve.tensor_tensor(out=stg[bk][:, :], in0=t1[:, :], in1=t2[:, :],
                                     op=ALU.add).then_inc(ev, 1); ph.a(ev, 1)

        @blk.tensor
        def _(te):
            for trg in range(NTRG):
                hp, tr = trg // 8, trg % 8
                ph.w(te, w_cv, 8 * hp + 8)
                ph.w(te, x_ld[trg % 2], 16 * (trg // 2 + 1))
                if trg >= 1:
                    ph.w(te, ev, 4 * trg)
                for i in range(32):
                    st = (i == 0)
                    sp = (i == 31)
                    for hh in range(2):
                        te.matmul(out=bqk[hh][:, :],
                                  lhsT=wqb[hp % 2][:, (hh * 32 + i) * 128:
                                                   (hh * 32 + i) * 128 + 128],
                                  rhs=xb[trg % 2][:, i, :], start=st, stop=sp)
                        ins = te.matmul(out=bqk[2 + hh][:, :],
                                        lhsT=wkb[hp % 2][:, (hh * 32 + i) * 128:
                                                         (hh * 32 + i) * 128 + 128],
                                        rhs=xb[trg % 2][:, i, :], start=st, stop=sp)
                    if i == 31:
                        ins.then_inc(mm, 1); ph.a(mm, 1)

    ph.close()



def emit_p0v(nc, io):
    ph = P(nc, "p0v", io.get("pool"))
    wv_r, V0, xbd = io["wv_r"], io["V0"], io["xbd"]

    landw = ph.sb("landw", [128, 4, 512], F32)
    swv = ph.sb("swv", [128, 16384], BF16)
    xb = [ph.sb(f"xb{i}", [128, 32, 512], BF16) for i in range(2)]
    vst = [ph.sb(f"vst{i}", [128, 512], BF16) for i in range(4)]

    bv = [ph.ps(f"bv{i}", [128, 512]) for i in range(4)]

    w_ld = ph.sem("w_ld")
    w_cv = ph.sem("w_cv")
    x_ld = [ph.sem("x_ld0"), ph.sem("x_ld1")]
    mm = ph.sem("mm")
    ev = ph.sem("ev")
    outq = [ph.sem("outq0"), ph.sem("outq1")]

    with nc.Block("p0v") as blk:

        @blk.scalar
        def _(sc):
            for op in range(2):
                for pc in range(8):
                    idx = op * 8 + pc
                    if idx > 0:
                        ph.w(sc, w_cv, idx)
                    if op == 1 and pc == 0:
                        ph.w(sc, mm, 8)
                    sc.dma_start(
                        out=landw[:, :, :],
                        in_=wv_r[pc * 512:(pc + 1) * 512,
                                 op * 512:(op + 1) * 512].rearrange(
                            "(i p) c -> p i c", p=128)).then_inc(w_ld, 16); ph.a(w_ld, 16)

        @blk.vector
        def _(ve):
            def conv_v(op):
                for pc in range(8):
                    idx = op * 8 + pc
                    ph.w(ve, w_ld, 16 * (idx + 1))
                    if op == 1 and pc == 0:
                        ph.w(ve, mm, 8)
                    ve.tensor_copy(
                        out=swv[:, pc * 2048:(pc + 1) * 2048].rearrange(
                            "p (i c) -> p i c", i=4),
                        in_=landw[:, :, :]).then_inc(w_cv, 1); ph.a(w_cv, 1)

            conv_v(0)
            for trg in range(16):
                ph.w(ve, mm, trg + 1)
                if trg >= 1:
                    ph.w(ve, outq[(trg - 1) % 2], 256 * ((trg - 1) // 2 + 1))
                for tt in range(4):
                    ve.tensor_copy(out=vst[tt][:, :],
                                   in_=bv[tt][:, :]).then_inc(ev, 1); ph.a(ev, 1)
                if trg == 7:
                    conv_v(1)

        @blk.sync
        def _(sy):
            for trg in range(16):
                tr = trg % 8
                if trg >= 2:
                    ph.w(sy, mm, trg - 1)
                sy.dma_start(
                    out=xb[trg % 2][:, :, :],
                    in_=xbd[:, tr * 512:(tr + 1) * 512].rearrange(
                        "(i p) c -> p i c", p=128)).then_inc(x_ld[trg % 2], 16); ph.a(x_ld[trg % 2], 16)

        @blk.tensor
        def _(te):
            for trg in range(16):
                op, tr = trg // 8, trg % 8
                ph.w(te, w_cv, 8 * (op + 1))
                ph.w(te, x_ld[trg % 2], 16 * (trg // 2 + 1))
                if trg >= 1:
                    ph.w(te, ev, 4 * trg)
                for i in range(32):
                    st = (i == 0)
                    sp = (i == 31)
                    for tt in range(4):
                        ins = te.matmul(out=bv[tt][:, :],
                                        lhsT=xb[trg % 2][:, i, tt * 128:(tt + 1) * 128],
                                        rhs=swv[:, i * 512:(i + 1) * 512],
                                        start=st, stop=sp)
                    if i == 31:
                        ins.then_inc(mm, 1); ph.a(mm, 1)

        @blk.gpsimd
        def _(gp):
            for trg in range(16):
                op, tr = trg // 8, trg % 8
                for tt in range(4):
                    ph.w(gp, ev, trg * 4 + tt + 1)
                    for hh in range(4):
                        gp.dma_start(
                            out=V0[op * 4 + hh,
                                   tr * 512 + tt * 128:tr * 512 + (tt + 1) * 128, :],
                            in_=vst[tt][:, hh * 128:(hh + 1) * 128]
                            ).then_inc(outq[trg % 2], 16); ph.a(outq[trg % 2], 16)
            ph.w(gp, outq[0], 256 * 8)
            ph.w(gp, outq[1], 256 * 8)

    ph.close()



def emit_p1(nc, io):
    ph = P(nc, "p1", io.get("pool"))
    QT0, KT0, V0, OTh = io["QT0"], io["KT0"], io["V0"], io["OTh"]
    tri, ones_col, ones_row = io["tri"], io["ones_col"], io["ones_row"]

    sk = [ph.sb(f"sk{i}", [128, T], BF16) for i in range(2)]
    sv = [ph.sb(f"sv{i}", [128, 32, 128], BF16) for i in range(2)]
    sq = [ph.sb(f"sq{i}", [128, 512], BF16) for i in range(2)]
    sp = [ph.sb(f"sp{i}", [128, 512], BF16) for i in range(2)]
    spm = [ph.sb(f"spm{i}", [128, 512], BF16) for i in range(2)]
    stri = ph.sb("stri", [128, 4, 512], BF16)
    sones = ph.sb("sones", [128, 1], BF16)
    sonesr = ph.sb("sonesr", [1, 128], F32)
    srcp = ph.sb("srcp", [1, 512], F32)
    srb = ph.sb("srb", [128, 512], F32)
    ostg = [ph.sb(f"ostg{i}", [128, 512], BF16) for i in range(2)]
    sbias = ph.sb("sbias", [128, 1], F32)

    bO = ph.ps("bO", [128, 512])
    bS = [ph.ps(f"bS{i}", [128, 512]) for i in range(2)]
    bD = ph.ps("bD", [1, 512])
    bRB = ph.ps("bRB", [128, 512])

    kv_ld = [ph.sem("kv_ld0"), ph.sem("kv_ld1")]
    c_ld = ph.sem("c_ld")
    q_ld = [ph.sem("q_ld0"), ph.sem("q_ld1")]
    s_mm = ph.sem("s_mm")
    s_pv = ph.sem("s_pv")
    s_dn = ph.sem("s_dn")
    s_bc = ph.sem("s_bc")
    a_ex = ph.sem("a_ex")
    v_mk = ph.sem("v_mk")
    v_rcp = ph.sem("v_rcp")
    v_ev = ph.sem("v_ev")
    p_st = [ph.sem("p_st0"), ph.sem("p_st1")]
    v_ini = ph.sem("v_ini")
    ch_ve = ph.sem("ch_ve")

    NHD = 8
    NQ = 8
    KM = [4 * (a + 1) for a in range(NQ)]
    GPH = sum(KM)

    def g_of(h, a, b):
        return h * GPH + sum(KM[:a]) + b

    def m_of(h, a, i):
        return (h * NQ + a) * 4 + i

    with nc.Block("p1") as blk:

        @blk.sync
        def _(sy):
            sy.dma_start(out=stri[:, :, :],
                         in_=tri[:, :, :].rearrange("t p c -> p t c")).then_inc(c_ld, 16); ph.a(c_ld, 16)
            sy.dma_start(out=sones[:, :], in_=ones_col[:, :]).then_inc(c_ld, 16); ph.a(c_ld, 16)
            sy.dma_start(out=sonesr[:, :], in_=ones_row[:, :]).then_inc(c_ld, 16); ph.a(c_ld, 16)
            for h in range(NHD):
                if h >= 2:
                    ph.w(sy, s_dn, NQ * (h - 1))
                sy.dma_start(out=sk[h % 2][:, :],
                             in_=KT0[h, :, :]).then_inc(kv_ld[h % 2], 16); ph.a(kv_ld[h % 2], 16)
                for part in range(4):
                    sy.dma_start(
                        out=sv[h % 2][:, part * 8:(part + 1) * 8, :],
                        in_=V0[h, part * 1024:(part + 1) * 1024, :].rearrange(
                            "(kc p) d -> p kc d", p=128)).then_inc(kv_ld[h % 2], 16); ph.a(kv_ld[h % 2], 16)
                for a in range(NQ):
                    ha = h * NQ + a
                    if ha >= 2:
                        ph.w(sy, s_dn, ha - 1)
                    sy.dma_start(out=sq[ha % 2][:, :],
                                 in_=QT0[h, :, a * 512:(a + 1) * 512]
                                 ).then_inc(q_ld[ha % 2], 16); ph.a(q_ld[ha % 2], 16)

        @blk.tensor
        def _(te):
            for h in range(NHD):
                for a in range(NQ):
                    ha = h * NQ + a
                    km = KM[a]
                    for b in range(km):
                        g = g_of(h, a, b)
                        if b == 0:
                            ph.w(te, c_ld, 48)
                            ph.w(te, kv_ld[h % 2], 80 * (h // 2 + 1))
                            ph.w(te, q_ld[ha % 2], 16 * (ha // 2 + 1))
                            ph.w(te, v_ev, ha)
                            ph.w(te, v_rcp, ha)
                        if g >= 2:
                            ph.w(te, a_ex, g - 1)
                        te.matmul(out=bS[g % 2][:, :],
                                  lhsT=sk[h % 2][:, b * 128:(b + 1) * 128],
                                  rhs=sq[ha % 2][:, :], start=True, stop=True
                                  ).then_inc(s_mm, 1); ph.a(s_mm, 1)
                        diag = b >= km - 4
                        if diag:
                            ph.w(te, v_mk, m_of(h, a, b - (km - 4)) + 1)
                            rhs_p = spm[g % 2][:, :]
                        else:
                            ph.w(te, a_ex, g + 1)
                            rhs_p = sp[g % 2][:, :]
                        te.matmul(out=bO[:, :],
                                  lhsT=sv[h % 2][:, b, :], rhs=rhs_p,
                                  start=(b == 0), stop=(b == km - 1)
                                  ).then_inc(s_pv, 1); ph.a(s_pv, 1)
                        ins_d = te.matmul(out=bD[:, :], lhsT=sones[:, :], rhs=rhs_p,
                                          start=(b == 0), stop=(b == km - 1),
                                          skip_group_check=True)
                        if b == km - 1:
                            ins_d.then_inc(s_dn, 1); ph.a(s_dn, 1)
                    ph.w(te, v_rcp, ha + 1)
                    ph.w(te, v_ev, ha)
                    te.matmul(out=bRB[:, :], lhsT=sonesr[:, :], rhs=srcp[:, :],
                              start=True, stop=True).then_inc(s_bc, 1); ph.a(s_bc, 1)

        @blk.scalar
        def _(sc):
            ph.w(sc, v_ini, 1)
            for h in range(NHD):
                for a in range(NQ):
                    km = KM[a]
                    for b in range(km):
                        g = g_of(h, a, b)
                        ph.w(sc, s_mm, g + 1)
                        if g >= 2:
                            ph.w(sc, s_pv, g - 1)
                        sc.activation(out=sp[g % 2][:, :], in_=bS[g % 2][:, :],
                                      func=AF.Exp, scale=SCALE,
                                      bias=sbias[:, 0:1]).then_inc(a_ex, 1); ph.a(a_ex, 1)

        @blk.vector
        def _(ve):
            ve.memset(sbias[:, :], -SOFT_B).then_inc(v_ini, 1); ph.a(v_ini, 1)
            for h in range(NHD):
                for a in range(NQ):
                    ha = h * NQ + a
                    km = KM[a]
                    for b in range(km - 4, km):
                        g = g_of(h, a, b)
                        i = b - (km - 4)
                        ph.w(ve, a_ex, g + 1)
                        if g >= 2:
                            ph.w(ve, s_pv, g - 1)
                        ve.tensor_tensor(out=spm[g % 2][:, :], in0=sp[g % 2][:, :],
                                         in1=stri[:, i, :],
                                         op=ALU.mult).then_inc(v_mk, 1); ph.a(v_mk, 1)
                    ph.w(ve, s_dn, ha + 1)
                    ve.reciprocal(out=srcp[:, :], in_=bD[0:1, :]).then_inc(v_rcp, 1); ph.a(v_rcp, 1)
                    ph.w(ve, s_bc, ha + 1)
                    ve.tensor_copy(out=srb[:, :], in_=bRB[:, :]).then_inc(ch_ve, 1); ph.a(ch_ve, 1)
                    if ha >= 2:
                        ph.w(ve, p_st[ha % 2], 16 * (ha // 2))
                    ph.w(ve, ch_ve, ha + 1)
                    ve.tensor_tensor(out=ostg[ha % 2][:, :], in0=bO[:, :],
                                     in1=srb[:, :], op=ALU.mult).then_inc(v_ev, 1); ph.a(v_ev, 1)

        @blk.gpsimd
        def _(gp):
            for h in range(NHD):
                for a in range(NQ):
                    ha = h * NQ + a
                    ph.w(gp, v_ev, ha + 1)
                    if io.get("fused", False):
                        peer = a if a < 4 else 7 - a
                        half = 0 if a < 4 else 1
                        dst = OTh[peer, h * 128:(h + 1) * 128,
                                  half * 512:(half + 1) * 512]
                    else:
                        dst = OTh[h * 128:(h + 1) * 128, a * 512:(a + 1) * 512]
                    gp.dma_start(out=dst,
                                 in_=ostg[ha % 2][:, :]).then_inc(p_st[ha % 2], 16); ph.a(p_st[ha % 2], 16)
            ph.w(gp, p_st[0], 16 * 32)
            ph.w(gp, p_st[1], 16 * 32)

    ph.close()


def _dram(nc, name, shape, dtype, kind):
    return nc.dram_tensor(name, shape, dtype, kind=kind).ap()


def build_launch1(nphases=3):
    nc = bass.Bass("TRN2", target_bir_lowering=False)
    io = {}
    io["xTb"] = _dram(nc, "xTb", [C, T], F32, "ExternalInput")
    io["wq_r"] = _dram(nc, "wq_r", [8, 32, 128, 128], F32, "ExternalInput")
    io["wk_r"] = _dram(nc, "wk_r", [8, 32, 128, 128], F32, "ExternalInput")
    io["wv_r"] = _dram(nc, "wv_r", [C, 1024], F32, "ExternalInput")
    io["ropec"] = _dram(nc, "ropec", [128, T], F32, "ExternalInput")
    io["ropes2"] = _dram(nc, "ropes2", [128, T], F32, "ExternalInput")
    io["tri"] = _dram(nc, "tri", [4, 128, 512], BF16, "ExternalInput")
    io["ones_col"] = _dram(nc, "ones_col", [128, 1], BF16, "ExternalInput")
    io["ones_row"] = _dram(nc, "ones_row", [1, 128], F32, "ExternalInput")
    io["xbd"] = _dram(nc, "xbd", [C, T], BF16, "Internal")
    io["QT0"] = _dram(nc, "QT0", [8, 128, T], BF16, "Internal")
    io["KT0"] = _dram(nc, "KT0", [8, 128, T], BF16, "Internal")
    io["V0"] = _dram(nc, "V0", [8, T, 128], BF16, "Internal")
    io["OTh"] = _dram(nc, "OTh", [1024, T], BF16, "ExternalOutput")
    emit_p0(nc, io)
    if nphases >= 2:
        nc.all_engine_barrier(sem_only=True)
        emit_p0v(nc, io)
    if nphases >= 3:
        nc.all_engine_barrier(sem_only=True)
        emit_p1(nc, io)
    return nc


def prep_launch1(inputs):
    x = np.asarray(inputs["L0_attn_x"])
    wq = np.asarray(inputs["L0_attn_wq"])
    wk = np.asarray(inputs["L0_attn_wk"])
    wv = np.asarray(inputs["L0_attn_wv"])
    cosT, sin2 = _rope_tables()
    cst = _consts()
    xT = [np.ascontiguousarray(x[b].T) for b in range(B)]
    wq_all = _wr_blocks(np.ascontiguousarray(wq.T))
    wk_all = _wr_blocks(np.ascontiguousarray(wk.T))
    wvT = np.ascontiguousarray(wv.T)
    in_maps = []
    for c in range(NCORE):
        b, g = c // 4, c % 4
        in_maps.append({
            "xTb": xT[b],
            "wq_r": np.ascontiguousarray(wq_all[g * 8:(g + 1) * 8]),
            "wk_r": np.ascontiguousarray(wk_all[g * 8:(g + 1) * 8]),
            "wv_r": np.ascontiguousarray(wvT[:, g * 1024:(g + 1) * 1024]),
            "ropec": cosT, "ropes2": sin2,
            "tri": cst["tri"], "ones_col": cst["ones_col"],
            "ones_row": cst["ones_row_f32"],
        })
    return in_maps


def emit_woln(nc, io, name, OT, W_r, XRES, XOUT, XPRE):
    ph = P(nc, name, io.get("pool"))
    ones_col, ones_row = io["ones_col"], io["ones_row"]

    so = ph.sb("so", [128, 32, 1024], BF16)
    landw = [ph.sb(f"landw{i}", [128, 32, 128], F32) for i in range(2)]
    swb = [ph.sb(f"swb{i}", [128, 4096], BF16) for i in range(2)]
    sx = [ph.sb(f"sx{i}", [128, 1024], F32) for i in range(2)]
    st_t = [ph.sb(f"stt{i}", [128, 1024], F32) for i in range(2)]
    stb = [ph.sb(f"stb{i}", [128, 1024], BF16) for i in range(2)]
    stsq = [ph.sb(f"stsq{i}", [128, 1024], BF16) for i in range(2)]
    sonesc = ph.sb("sonesc", [128, 1], BF16)
    sonesr = ph.sb("sonesr", [1, 128], F32)
    srow = ph.sb("srow", [1, 4096], F32)
    srow2 = ph.sb("srow2", [1, 4096], F32)
    sra = ph.sb("sra", [128, 1024], F32)
    srb2 = ph.sb("srb2", [128, 1024], F32)
    land2 = [ph.sb(f"land2{i}", [128, 1024], F32) for i in range(2)]
    tmp2 = ph.sb("tmp2", [128, 1024], F32)
    stage2 = [ph.sb(f"stage2{i}", [128, 1024], F32) for i in range(2)]
    eps_t = ph.sb("eps_t", [1, 1], F32)

    yb = [ph.ps(f"yb{i}", [128, 512]) for i in range(4)]
    ssum = [ph.ps(f"ssum{i}", [1, 512]) for i in range(2)]
    ssq = [ph.ps(f"ssq{i}", [1, 512]) for i in range(2)]

    c_ld = ph.sem("c_ld")
    w_ld = [ph.sem("w_ld0"), ph.sem("w_ld1")]
    w_cv = ph.sem("w_cv")
    x_ld = [ph.sem("x_ld0"), ph.sem("x_ld1")]
    p_mm = ph.sem("p_mm")
    p_sd = ph.sem("p_sd")
    p_bc = ph.sem("p_bc")
    v_t = ph.sem("v_t")
    v_st = ph.sem("v_st")
    a_sq = ph.sem("a_sq")
    p_pre = [ph.sem("p_pre0"), ph.sem("p_pre1")]
    v_row = ph.sem("v_row")
    a_row = ph.sem("a_row")
    v_bc = ph.sem("v_bc")
    x2_ld = [ph.sem("x2_ld0"), ph.sem("x2_ld1")]
    v_p2 = ph.sem("v_p2")
    v_p3 = ph.sem("v_p3")
    p_out = [ph.sem("p_out0"), ph.sem("p_out1")]
    ini = ph.sem("ini")
    cch = ph.sem("cch")

    NOB = 32

    with nc.Block(name) as blk:

        @blk.sync
        def _(sy):
            for pc in range(4):
                sy.dma_start(
                    out=so[:, pc * 8:(pc + 1) * 8, :],
                    in_=OT[pc * 1024:(pc + 1) * 1024, :].rearrange(
                        "(cc p) t -> p cc t", p=128)).then_inc(c_ld, 16); ph.a(c_ld, 16)
            sy.dma_start(out=sonesc[:, :], in_=ones_col[:, :]).then_inc(c_ld, 16); ph.a(c_ld, 16)
            sy.dma_start(out=sonesr[:, :], in_=ones_row[:, :]).then_inc(c_ld, 16); ph.a(c_ld, 16)
            for ob in range(NOB):
                if ob >= 2:
                    ph.w(sy, w_cv, ob - 1)
                for hh in range(2):
                    sy.dma_start(
                        out=landw[ob % 2][:, hh * 16:(hh + 1) * 16, :],
                        in_=W_r[ob, hh * 16:(hh + 1) * 16, :, :].rearrange(
                            "i p c -> p i c")).then_inc(w_ld[ob % 2], 16); ph.a(w_ld[ob % 2], 16)
                if ob >= 2:
                    ph.w(sy, v_t, 2 * (ob - 1))
                sy.dma_start(out=sx[ob % 2][:, :],
                             in_=XRES[ob * 128:(ob + 1) * 128, :]
                             ).then_inc(x_ld[ob % 2], 16); ph.a(x_ld[ob % 2], 16)
            for ch in range(NOB):
                ph.w(sy, p_pre[ch % 2], 16 * (ch // 2 + 1))
                if ch >= 2:
                    ph.w(sy, v_p2, ch - 1)
                sy.dma_start(out=land2[ch % 2][:, :],
                             in_=XPRE[ch * 128:(ch + 1) * 128, :]
                             ).then_inc(x2_ld[ch % 2], 16); ph.a(x2_ld[ch % 2], 16)

        @blk.vector
        def _(ve):
            ve.memset(eps_t[:, :], EPS).then_inc(ini, 1); ph.a(ini, 1)
            for ob in range(NOB):
                ph.w(ve, w_ld[ob % 2], 32 * (ob // 2 + 1))
                if ob >= 2:
                    ph.w(ve, p_mm, 2 * (ob - 1))
                ve.tensor_copy(
                    out=swb[ob % 2][:, :].rearrange("p (i c) -> p i c", i=32),
                    in_=landw[ob % 2][:, :, :]).then_inc(w_cv, 1); ph.a(w_cv, 1)
                for half in range(2):
                    hs = slice(half * 512, (half + 1) * 512)
                    ph.w(ve, p_mm, 2 * ob + half + 1)
                    ph.w(ve, x_ld[ob % 2], 16 * (ob // 2 + 1))
                    if ob >= 2:
                        ph.w(ve, p_pre[ob % 2], 16 * (ob // 2))
                        ph.w(ve, a_sq, ob - 1)
                        ph.w(ve, p_sd, ob - 1)
                    ve.tensor_tensor(out=st_t[ob % 2][:, hs],
                                     in0=yb[(ob % 2) * 2 + half][:, :],
                                     in1=sx[ob % 2][:, hs],
                                     op=ALU.add).then_inc(v_t, 1); ph.a(v_t, 1)
                    ph.w(ve, v_t, 2 * ob + half + 1)
                    ins = ve.tensor_copy(out=stb[ob % 2][:, hs],
                                         in_=st_t[ob % 2][:, hs])
                    if half == 1:
                        ins.then_inc(v_st, 1); ph.a(v_st, 1)
            ph.w(ve, p_sd, NOB)
            ve.tensor_copy(out=srow[:, 0:512], in_=ssum[0][:, :]).then_inc(cch, 1); ph.a(cch, 1)
            ve.tensor_copy(out=srow[:, 512:1024], in_=ssum[1][:, :]).then_inc(cch, 1); ph.a(cch, 1)
            ve.tensor_copy(out=srow[:, 1024:1536], in_=ssq[0][:, :]).then_inc(cch, 1); ph.a(cch, 1)
            ve.tensor_copy(out=srow[:, 1536:2048], in_=ssq[1][:, :]).then_inc(cch, 1); ph.a(cch, 1)
            ph.w(ve, cch, 4)
            ve.tensor_scalar(out=srow2[:, 0:1024], in0=srow[:, 0:1024],
                             scalar1=1.0 / C, scalar2=None,
                             op0=ALU.mult).then_inc(cch, 1); ph.a(cch, 1)
            ve.tensor_scalar(out=srow2[:, 1024:2048], in0=srow[:, 1024:2048],
                             scalar1=1.0 / C, scalar2=None,
                             op0=ALU.mult).then_inc(cch, 1); ph.a(cch, 1)
            ph.w(ve, cch, 6)
            ve.tensor_tensor(out=srow2[:, 2048:3072], in0=srow2[:, 0:1024],
                             in1=srow2[:, 0:1024], op=ALU.mult).then_inc(cch, 1); ph.a(cch, 1)
            ph.w(ve, cch, 7)
            ve.tensor_tensor(out=srow2[:, 2048:3072], in0=srow2[:, 1024:2048],
                             in1=srow2[:, 2048:3072],
                             op=ALU.subtract).then_inc(v_row, 1); ph.a(v_row, 1)
            ph.w(ve, a_row, 1)
            ve.reciprocal(out=srow[:, 2048:3072],
                          in_=srow2[:, 3072:4096]).then_inc(cch, 1); ph.a(cch, 1)
            ph.w(ve, cch, 8)
            ve.tensor_tensor(out=srow[:, 3072:4096], in0=srow2[:, 0:1024],
                             in1=srow[:, 2048:3072], op=ALU.mult).then_inc(cch, 1); ph.a(cch, 1)
            ph.w(ve, cch, 9)
            ve.tensor_scalar(out=srow[:, 3072:4096], in0=srow[:, 3072:4096],
                             scalar1=-1.0, scalar2=None,
                             op0=ALU.mult).then_inc(v_row, 1); ph.a(v_row, 1)
            ph.w(ve, p_bc, 1)
            ve.tensor_copy(out=sra[:, 0:512], in_=yb[0][:, :]).then_inc(cch, 1); ph.a(cch, 1)
            ve.tensor_copy(out=sra[:, 512:1024], in_=yb[1][:, :]).then_inc(cch, 1); ph.a(cch, 1)
            ve.tensor_copy(out=srb2[:, 0:512], in_=yb[2][:, :]).then_inc(cch, 1); ph.a(cch, 1)
            ve.tensor_copy(out=srb2[:, 512:1024],
                           in_=yb[3][:, :]).then_inc(v_bc, 1); ph.a(v_bc, 1)
            for ch in range(NOB):
                ph.w(ve, x2_ld[ch % 2], 16 * (ch // 2 + 1))
                ph.w(ve, v_bc, 1)
                ph.w(ve, cch, 12)
                if ch >= 2:
                    ph.w(ve, p_out[ch % 2], 16 * (ch // 2))
                if ch >= 1:
                    ph.w(ve, v_p3, ch)
                ve.tensor_tensor(out=tmp2[:, :], in0=land2[ch % 2][:, :],
                                 in1=sra[:, :], op=ALU.mult).then_inc(v_p2, 1); ph.a(v_p2, 1)
                ph.w(ve, v_p2, ch + 1)
                ve.tensor_tensor(out=stage2[ch % 2][:, :], in0=tmp2[:, :],
                                 in1=srb2[:, :], op=ALU.add).then_inc(v_p3, 1); ph.a(v_p3, 1)

        @blk.scalar
        def _(sc):
            for ob in range(NOB):
                for half in range(2):
                    hs = slice(half * 512, (half + 1) * 512)
                    ph.w(sc, v_t, 2 * ob + half + 1)
                    if ob >= 2:
                        ph.w(sc, p_sd, ob - 1)
                    ins = sc.activation(out=stsq[ob % 2][:, hs],
                                        in_=st_t[ob % 2][:, hs], func=AF.Square)
                    if half == 1:
                        ins.then_inc(a_sq, 1); ph.a(a_sq, 1)
            ph.w(sc, v_row, 1)
            ph.w(sc, ini, 1)
            sc.activation(out=srow2[:, 3072:4096], in_=srow2[:, 2048:3072],
                          func=AF.Sqrt, bias=eps_t[:, 0:1]).then_inc(a_row, 1); ph.a(a_row, 1)

        def _stats(te, o):
            ph.w(te, v_st, o + 1)
            ph.w(te, a_sq, o + 1)
            for half in range(2):
                hs = slice(half * 512, (half + 1) * 512)
                te.matmul(out=ssum[half][:, :], lhsT=sonesc[:, :],
                          rhs=stb[o % 2][:, hs], start=(o == 0), stop=(o == NOB - 1),
                          skip_group_check=True)
                ins = te.matmul(out=ssq[half][:, :], lhsT=sonesc[:, :],
                                rhs=stsq[o % 2][:, hs], start=(o == 0),
                                stop=(o == NOB - 1), skip_group_check=True)
            ins.then_inc(p_sd, 1); ph.a(p_sd, 1)

        @blk.tensor
        def _(te):
            ph.w(te, c_ld, 96)
            for ob in range(NOB):
                ph.w(te, w_cv, ob + 1)
                if ob >= 2:
                    ph.w(te, v_t, 2 * (ob - 1))
                for cc in range(32):
                    for half in range(2):
                        ins = te.matmul(
                            out=yb[(ob % 2) * 2 + half][:, :],
                            lhsT=swb[ob % 2][:, cc * 128:(cc + 1) * 128],
                            rhs=so[:, cc, half * 512:(half + 1) * 512],
                            start=(cc == 0), stop=(cc == 31))
                        if cc == 31:
                            ins.then_inc(p_mm, 1); ph.a(p_mm, 1)
                if ob >= 1:
                    _stats(te, ob - 1)
            _stats(te, NOB - 1)
            ph.w(te, v_row, 2)
            ph.w(te, v_t, 2 * NOB)
            te.matmul(out=yb[0][:, :], lhsT=sonesr[:, :],
                      rhs=srow[:, 2048:2560], start=True, stop=True)
            te.matmul(out=yb[1][:, :], lhsT=sonesr[:, :],
                      rhs=srow[:, 2560:3072], start=True, stop=True)
            te.matmul(out=yb[2][:, :], lhsT=sonesr[:, :],
                      rhs=srow[:, 3072:3584], start=True, stop=True)
            te.matmul(out=yb[3][:, :], lhsT=sonesr[:, :],
                      rhs=srow[:, 3584:4096], start=True, stop=True).then_inc(p_bc, 1); ph.a(p_bc, 1)

        @blk.gpsimd
        def _(gp):
            for ob in range(NOB):
                ph.w(gp, v_t, 2 * ob + 2)
                gp.dma_start(out=XPRE[ob * 128:(ob + 1) * 128, :],
                             in_=st_t[ob % 2][:, :]).then_inc(p_pre[ob % 2], 16); ph.a(p_pre[ob % 2], 16)
            for ch in range(NOB):
                ph.w(gp, v_p3, ch + 1)
                gp.dma_start(out=XOUT[ch * 128:(ch + 1) * 128, :],
                             in_=stage2[ch % 2][:, :]).then_inc(p_out[ch % 2], 16); ph.a(p_out[ch % 2], 16)
            ph.w(gp, p_out[0], 16 * 16)
            ph.w(gp, p_out[1], 16 * 16)

    ph.close()


def emit_ffn0(nc, io, name, X1, QW_r, QB_r, WS, WZP, BS, BZP, XOUT, XPRE):
    ph = P(nc, name, io.get("pool"))
    ones_col, ones_row = io["ones_col"], io["ones_row"]

    sxb = ph.sb("sxb", [128, 32, 1024], BF16)
    land = [ph.sb(f"land{i}", [128, 1024], F32) for i in range(2)]
    landw = [ph.sb(f"landw{i}", [128, 32, 128], I32) for i in range(2)]
    swb = [ph.sb(f"swb{i}", [128, 4096], BF16) for i in range(2)]
    sx = [ph.sb(f"sx{i}", [128, 1024], F32) for i in range(2)]
    sy1 = [ph.sb(f"sy1{i}", [128, 1024], F32) for i in range(2)]
    st_t = [ph.sb(f"stt{i}", [128, 1024], F32) for i in range(2)]
    stb = [ph.sb(f"stb{i}", [128, 1024], BF16) for i in range(2)]
    stsq = [ph.sb(f"stsq{i}", [128, 1024], BF16) for i in range(2)]
    sonesc = ph.sb("sonesc", [128, 1], BF16)
    sonesr = ph.sb("sonesr", [1, 128], F32)
    s_ws = ph.sb("s_ws", [1, 1], F32)
    s_wzp = ph.sb("s_wzp", [1, 1], I32)
    s_bs = ph.sb("s_bs", [1, 1], F32)
    s_bzp = ph.sb("s_bzp", [1, 1], I32)
    s_wzpf = ph.sb("s_wzpf", [1, 1], F32)
    s_bzpf = ph.sb("s_bzpf", [1, 1], F32)
    sqb = ph.sb("sqb", [128, 32], I32)
    sqbf = ph.sb("sqbf", [128, 32], F32)
    sbias = ph.sb("sbias", [128, 32], F32)
    wsb = ph.sb("wsb", [128, 1], F32)
    bzpb = ph.sb("bzpb", [128, 1], F32)
    bsb = ph.sb("bsb", [128, 1], F32)
    wzrow = ph.sb("wzrow", [1, 128], F32)
    wzrow_b = ph.sb("wzrow_b", [1, 128], BF16)
    rsx_b = ph.sb("rsx_b", [1, 1024], BF16)
    srow = ph.sb("srow", [1, 4096], F32)
    sra = ph.sb("sra", [128, 1024], F32)
    srb2 = ph.sb("srb2", [128, 1024], F32)
    land2 = [ph.sb(f"land2{i}", [128, 1024], F32) for i in range(2)]
    tmp2 = ph.sb("tmp2", [128, 1024], F32)
    stage2 = [ph.sb(f"stage2{i}", [128, 1024], F32) for i in range(2)]
    eps_t = ph.sb("eps_t", [1, 1], F32)

    yb = [ph.ps(f"yb{i}", [128, 512]) for i in range(4)]
    ssum = [ph.ps(f"ssum{i}", [1, 512]) for i in range(2)]
    ssq = [ph.ps(f"ssq{i}", [1, 512]) for i in range(2)]

    c_ld = ph.sem("c_ld")
    px_ld = [ph.sem("px_ld0"), ph.sem("px_ld1")]
    pr_cv = ph.sem("pr_cv")
    pr_pe = ph.sem("pr_pe")
    cch = ph.sem("cch")
    w_ld = [ph.sem("w_ld0"), ph.sem("w_ld1")]
    w_cv = ph.sem("w_cv")
    x_ld = [ph.sem("x_ld0"), ph.sem("x_ld1")]
    p_mm = ph.sem("p_mm")
    p_sd = ph.sem("p_sd")
    p_bc = ph.sem("p_bc")
    a_ev = ph.sem("a_ev")
    v_t = ph.sem("v_t")
    v_st = ph.sem("v_st")
    a_sq = ph.sem("a_sq")
    p_pre = [ph.sem("p_pre0"), ph.sem("p_pre1")]
    v_row = ph.sem("v_row")
    a_row = ph.sem("a_row")
    v_bc = ph.sem("v_bc")
    x2_ld = [ph.sem("x2_ld0"), ph.sem("x2_ld1")]
    v_p2 = ph.sem("v_p2")
    v_p3 = ph.sem("v_p3")
    p_out = [ph.sem("p_out0"), ph.sem("p_out1")]
    ini = ph.sem("ini")

    NOB = 32

    with nc.Block(name) as blk:

        @blk.sync
        def _(sy):
            sy.dma_start(out=sonesc[:, :], in_=ones_col[:, :]).then_inc(c_ld, 16); ph.a(c_ld, 16)
            sy.dma_start(out=sonesr[:, :], in_=ones_row[:, :]).then_inc(c_ld, 16); ph.a(c_ld, 16)
            sy.dma_start(out=s_ws[:, :], in_=WS[:, :]).then_inc(c_ld, 16); ph.a(c_ld, 16)
            sy.dma_start(out=s_wzp[:, :], in_=WZP[:, :]).then_inc(c_ld, 16); ph.a(c_ld, 16)
            sy.dma_start(out=s_bs[:, :], in_=BS[:, :]).then_inc(c_ld, 16); ph.a(c_ld, 16)
            sy.dma_start(out=s_bzp[:, :], in_=BZP[:, :]).then_inc(c_ld, 16); ph.a(c_ld, 16)
            sy.dma_start(out=sqb[:, :], in_=QB_r[:, :]).then_inc(c_ld, 16); ph.a(c_ld, 16)
            for ch in range(32):
                if ch >= 2:
                    ph.w(sy, pr_cv, ch - 1)
                sy.dma_start(out=land[ch % 2][:, :],
                             in_=X1[ch * 128:(ch + 1) * 128, :]
                             ).then_inc(px_ld[ch % 2], 16); ph.a(px_ld[ch % 2], 16)
            for ob in range(NOB):
                if ob >= 2:
                    ph.w(sy, w_cv, ob - 1)
                for hh in range(2):
                    sy.dma_start(
                        out=landw[ob % 2][:, hh * 16:(hh + 1) * 16, :],
                        in_=QW_r[ob, hh * 16:(hh + 1) * 16, :, :].rearrange(
                            "i p c -> p i c")).then_inc(w_ld[ob % 2], 16); ph.a(w_ld[ob % 2], 16)
                if ob >= 2:
                    ph.w(sy, v_t, 2 * (ob - 1))
                sy.dma_start(out=sx[ob % 2][:, :],
                             in_=X1[ob * 128:(ob + 1) * 128, :]
                             ).then_inc(x_ld[ob % 2], 16); ph.a(x_ld[ob % 2], 16)
            for ch in range(NOB):
                ph.w(sy, p_pre[ch % 2], 16 * (ch // 2 + 1))
                if ch >= 2:
                    ph.w(sy, v_p2, ch - 1)
                sy.dma_start(out=land2[ch % 2][:, :],
                             in_=XPRE[ch * 128:(ch + 1) * 128, :]
                             ).then_inc(x2_ld[ch % 2], 16); ph.a(x2_ld[ch % 2], 16)

        @blk.vector
        def _(ve):
            ve.memset(eps_t[:, :], EPS).then_inc(ini, 1); ph.a(ini, 1)
            for ch in range(32):
                ph.w(ve, px_ld[ch % 2], 16 * (ch // 2 + 1))
                ve.tensor_copy(out=sxb[:, ch, :],
                               in_=land[ch % 2][:, :]).then_inc(pr_cv, 1); ph.a(pr_cv, 1)
            ph.w(ve, c_ld, 112)
            ve.tensor_copy(out=s_wzpf[:, :], in_=s_wzp[:, :]).then_inc(cch, 1); ph.a(cch, 1)
            ve.tensor_copy(out=s_bzpf[:, :], in_=s_bzp[:, :]).then_inc(cch, 1); ph.a(cch, 1)
            ve.tensor_copy(out=sqbf[:, :], in_=sqb[:, :]).then_inc(cch, 1); ph.a(cch, 1)
            ph.w(ve, cch, 3)
            ve.tensor_scalar(out=wzrow[:, :], in0=sonesr[:, :],
                             scalar1=s_wzpf[0:1, 0:1], scalar2=-1.0,
                             op0=ALU.mult, op1=ALU.mult).then_inc(cch, 1); ph.a(cch, 1)
            ph.w(ve, cch, 4)
            ve.tensor_copy(out=wzrow_b[:, :], in_=wzrow[:, :]).then_inc(v_row, 1); ph.a(v_row, 1)
            ph.w(ve, pr_pe, 1)
            ve.tensor_copy(out=wsb[:, :], in_=yb[0][:, 0:1]).then_inc(cch, 1); ph.a(cch, 1)
            ve.tensor_copy(out=bzpb[:, :], in_=yb[1][:, 0:1]).then_inc(cch, 1); ph.a(cch, 1)
            ve.tensor_copy(out=bsb[:, :], in_=yb[2][:, 0:1]).then_inc(cch, 1); ph.a(cch, 1)
            ph.w(ve, cch, 7)
            ve.tensor_scalar(out=sbias[:, :], in0=sqbf[:, :],
                             scalar1=bzpb[:, 0:1], scalar2=bsb[:, 0:1],
                             op0=ALU.subtract, op1=ALU.mult).then_inc(v_row, 1); ph.a(v_row, 1)
            ph.w(ve, pr_pe, 2)
            ve.tensor_copy(out=rsx_b[:, 0:512], in_=ssum[0][:, :]).then_inc(cch, 1); ph.a(cch, 1)
            ve.tensor_copy(out=rsx_b[:, 512:1024],
                           in_=ssum[1][:, :]).then_inc(v_row, 1); ph.a(v_row, 1)
            for ob in range(NOB):
                ph.w(ve, w_ld[ob % 2], 32 * (ob // 2 + 1))
                if ob >= 2:
                    ph.w(ve, p_mm, 2 * (ob - 1))
                ve.tensor_copy(
                    out=swb[ob % 2][:, :].rearrange("p (i c) -> p i c", i=32),
                    in_=landw[ob % 2][:, :, :]).then_inc(w_cv, 1); ph.a(w_cv, 1)
                for half in range(2):
                    hs = slice(half * 512, (half + 1) * 512)
                    ph.w(ve, a_ev, 2 * ob + half + 1)
                    ph.w(ve, x_ld[ob % 2], 16 * (ob // 2 + 1))
                    if ob >= 2:
                        ph.w(ve, p_pre[ob % 2], 16 * (ob // 2))
                        ph.w(ve, a_sq, ob - 1)
                        ph.w(ve, p_sd, ob - 1)
                    ve.tensor_tensor(out=st_t[ob % 2][:, hs],
                                     in0=sy1[ob % 2][:, hs],
                                     in1=sx[ob % 2][:, hs],
                                     op=ALU.add).then_inc(v_t, 1); ph.a(v_t, 1)
                    ph.w(ve, v_t, 2 * ob + half + 1)
                    ins = ve.tensor_copy(out=stb[ob % 2][:, hs],
                                         in_=st_t[ob % 2][:, hs])
                    if half == 1:
                        ins.then_inc(v_st, 1); ph.a(v_st, 1)
            ph.w(ve, p_sd, NOB)
            ve.tensor_copy(out=srow[:, 0:512], in_=ssum[0][:, :]).then_inc(cch, 1); ph.a(cch, 1)
            ve.tensor_copy(out=srow[:, 512:1024], in_=ssum[1][:, :]).then_inc(cch, 1); ph.a(cch, 1)
            ve.tensor_copy(out=srow[:, 1024:1536], in_=ssq[0][:, :]).then_inc(cch, 1); ph.a(cch, 1)
            ve.tensor_copy(out=srow[:, 1536:2048], in_=ssq[1][:, :]).then_inc(cch, 1); ph.a(cch, 1)
            ph.w(ve, cch, 12)
            ve.tensor_scalar(out=srow[:, 2048:3072], in0=srow[:, 0:1024],
                             scalar1=1.0 / C, scalar2=None,
                             op0=ALU.mult).then_inc(cch, 1); ph.a(cch, 1)
            ve.tensor_scalar(out=srow[:, 3072:4096], in0=srow[:, 1024:2048],
                             scalar1=1.0 / C, scalar2=None,
                             op0=ALU.mult).then_inc(cch, 1); ph.a(cch, 1)
            ph.w(ve, cch, 14)
            ve.tensor_tensor(out=srow[:, 0:1024], in0=srow[:, 2048:3072],
                             in1=srow[:, 2048:3072],
                             op=ALU.mult).then_inc(cch, 1); ph.a(cch, 1)
            ph.w(ve, cch, 15)
            ve.tensor_tensor(out=srow[:, 1024:2048], in0=srow[:, 3072:4096],
                             in1=srow[:, 0:1024],
                             op=ALU.subtract).then_inc(v_row, 1); ph.a(v_row, 1)
            ph.w(ve, a_row, 1)
            ve.reciprocal(out=srow[:, 3072:4096],
                          in_=srow[:, 0:1024]).then_inc(cch, 1); ph.a(cch, 1)
            ph.w(ve, cch, 16)
            ve.tensor_tensor(out=srow[:, 1024:2048], in0=srow[:, 2048:3072],
                             in1=srow[:, 3072:4096],
                             op=ALU.mult).then_inc(cch, 1); ph.a(cch, 1)
            ph.w(ve, cch, 17)
            ve.tensor_scalar(out=srow[:, 1024:2048], in0=srow[:, 1024:2048],
                             scalar1=-1.0, scalar2=None,
                             op0=ALU.mult).then_inc(v_row, 1); ph.a(v_row, 1)
            ph.w(ve, p_bc, 1)
            ve.tensor_copy(out=sra[:, 0:512], in_=yb[0][:, :]).then_inc(cch, 1); ph.a(cch, 1)
            ve.tensor_copy(out=sra[:, 512:1024], in_=yb[1][:, :]).then_inc(cch, 1); ph.a(cch, 1)
            ve.tensor_copy(out=srb2[:, 0:512], in_=yb[2][:, :]).then_inc(cch, 1); ph.a(cch, 1)
            ve.tensor_copy(out=srb2[:, 512:1024],
                           in_=yb[3][:, :]).then_inc(v_bc, 1); ph.a(v_bc, 1)
            for ch in range(NOB):
                ph.w(ve, x2_ld[ch % 2], 16 * (ch // 2 + 1))
                ph.w(ve, v_bc, 1)
                ph.w(ve, cch, 20)
                if ch >= 2:
                    ph.w(ve, p_out[ch % 2], 16 * (ch // 2))
                if ch >= 1:
                    ph.w(ve, v_p3, ch)
                ve.tensor_tensor(out=tmp2[:, :], in0=land2[ch % 2][:, :],
                                 in1=sra[:, :], op=ALU.mult).then_inc(v_p2, 1); ph.a(v_p2, 1)
                ph.w(ve, v_p2, ch + 1)
                ve.tensor_tensor(out=stage2[ch % 2][:, :], in0=tmp2[:, :],
                                 in1=srb2[:, :], op=ALU.add).then_inc(v_p3, 1); ph.a(v_p3, 1)

        @blk.scalar
        def _(sc):
            for ob in range(NOB):
                for half in range(2):
                    hs = slice(half * 512, (half + 1) * 512)
                    ph.w(sc, p_mm, 2 * ob + half + 1)
                    if ob >= 2:
                        ph.w(sc, v_t, 2 * (ob - 1))
                    ph.w(sc, v_row, 2)
                    sc.activation(out=sy1[ob % 2][:, hs],
                                  in_=yb[(ob % 2) * 2 + half][:, :],
                                  func=AF.Identity, scale=wsb[:, 0:1],
                                  bias=sbias[:, ob:ob + 1]).then_inc(a_ev, 1); ph.a(a_ev, 1)
                    ph.w(sc, v_t, 2 * ob + half + 1)
                    if ob >= 2:
                        ph.w(sc, p_sd, ob - 1)
                    ins = sc.activation(out=stsq[ob % 2][:, hs],
                                        in_=st_t[ob % 2][:, hs], func=AF.Square)
                    if half == 1:
                        ins.then_inc(a_sq, 1); ph.a(a_sq, 1)
            ph.w(sc, v_row, 4)
            ph.w(sc, ini, 1)
            sc.activation(out=srow[:, 0:1024], in_=srow[:, 1024:2048],
                          func=AF.Sqrt, bias=eps_t[:, 0:1]).then_inc(a_row, 1); ph.a(a_row, 1)

        def _stats3(te, o):
            ph.w(te, v_st, o + 1)
            ph.w(te, a_sq, o + 1)
            for half in range(2):
                hs = slice(half * 512, (half + 1) * 512)
                te.matmul(out=ssum[half][:, :], lhsT=sonesc[:, :],
                          rhs=stb[o % 2][:, hs], start=(o == 0), stop=(o == NOB - 1),
                          skip_group_check=True)
                ins = te.matmul(out=ssq[half][:, :], lhsT=sonesc[:, :],
                                rhs=stsq[o % 2][:, hs], start=(o == 0),
                                stop=(o == NOB - 1), skip_group_check=True)
            ins.then_inc(p_sd, 1); ph.a(p_sd, 1)

        @blk.tensor
        def _(te):
            ph.w(te, c_ld, 112)
            ph.w(te, cch, 3)
            te.matmul(out=yb[0][:, 0:1], lhsT=sonesr[:, :], rhs=s_ws[:, :],
                      start=True, stop=True)
            te.matmul(out=yb[1][:, 0:1], lhsT=sonesr[:, :], rhs=s_bzpf[:, :],
                      start=True, stop=True)
            te.matmul(out=yb[2][:, 0:1], lhsT=sonesr[:, :], rhs=s_bs[:, :],
                      start=True, stop=True).then_inc(pr_pe, 1); ph.a(pr_pe, 1)
            ph.w(te, pr_cv, 32)
            for cc in range(32):
                for half in range(2):
                    ins = te.matmul(out=ssum[half][:, :], lhsT=sonesc[:, :],
                                    rhs=sxb[:, cc, half * 512:(half + 1) * 512],
                                    start=(cc == 0), stop=(cc == 31),
                                    skip_group_check=True)
            ins.then_inc(pr_pe, 1); ph.a(pr_pe, 1)
            for ob in range(NOB):
                ph.w(te, w_cv, ob + 1)
                ph.w(te, v_row, 3)
                if ob >= 2:
                    ph.w(te, a_ev, 2 * (ob - 1))
                for half in range(2):
                    hs = slice(half * 512, (half + 1) * 512)
                    for cc in range(32):
                        te.matmul(
                            out=yb[(ob % 2) * 2 + half][:, :],
                            lhsT=swb[ob % 2][:, cc * 128:(cc + 1) * 128],
                            rhs=sxb[:, cc, hs], start=(cc == 0), stop=False)
                    te.matmul(out=yb[(ob % 2) * 2 + half][:, :],
                              lhsT=wzrow_b[:, :], rhs=rsx_b[0:1, hs],
                              start=False, stop=True).then_inc(p_mm, 1); ph.a(p_mm, 1)
                if ob >= 1:
                    _stats3(te, ob - 1)
            _stats3(te, NOB - 1)
            ph.w(te, v_row, 5)
            ph.w(te, a_ev, 2 * NOB)
            te.matmul(out=yb[0][:, :], lhsT=sonesr[:, :],
                      rhs=srow[:, 3072:3584], start=True, stop=True)
            te.matmul(out=yb[1][:, :], lhsT=sonesr[:, :],
                      rhs=srow[:, 3584:4096], start=True, stop=True)
            te.matmul(out=yb[2][:, :], lhsT=sonesr[:, :],
                      rhs=srow[:, 1024:1536], start=True, stop=True)
            te.matmul(out=yb[3][:, :], lhsT=sonesr[:, :],
                      rhs=srow[:, 1536:2048], start=True, stop=True).then_inc(p_bc, 1); ph.a(p_bc, 1)

        @blk.gpsimd
        def _(gp):
            for ob in range(NOB):
                ph.w(gp, v_t, 2 * ob + 2)
                ph.w(gp, a_sq, ob + 1)
                gp.dma_start(out=XPRE[ob * 128:(ob + 1) * 128, :],
                             in_=st_t[ob % 2][:, :]).then_inc(p_pre[ob % 2], 16); ph.a(p_pre[ob % 2], 16)
            for ch in range(NOB):
                ph.w(gp, v_p3, ch + 1)
                gp.dma_start(out=XOUT[ch * 128:(ch + 1) * 128, :],
                             in_=stage2[ch % 2][:, :]).then_inc(p_out[ch % 2], 16); ph.a(p_out[ch % 2], 16)
            ph.w(gp, p_out[0], 16 * 16)
            ph.w(gp, p_out[1], 16 * 16)

    ph.close()


def emit_p4(nc, io, name, X2, WQ1_r, WK1, WV1, QT1, KT1c, V1c):
    ph = P(nc, name, io.get("pool"))

    sxb = ph.sb("sxb", [128, 32, 1024], BF16)
    land = [ph.sb(f"land{i}", [128, 1024], F32) for i in range(2)]
    landw = [ph.sb(f"landw{i}", [128, 32, 128], F32) for i in range(2)]
    kland = ph.sb("kland", [128, 16, 128], F32)
    swk = ph.sb("swk", [128, 4096], BF16)
    swv = ph.sb("swv", [128, 4096], BF16)
    swq = [ph.sb(f"swq{i}", [128, 4096], BF16) for i in range(2)]
    kstg = [ph.sb(f"kstg{i}", [128, 512], BF16) for i in range(2)]
    vstg = [ph.sb(f"vstg{i}", [128, 128], BF16) for i in range(2)]
    qstg = [ph.sb(f"qstg{i}", [128, 1024], BF16) for i in range(2)]

    kb = [ph.ps(f"kb{i}", [128, 512]) for i in range(2)]
    vb = [ph.ps(f"vb{i}", [128, 128]) for i in range(2)]
    qb = [ph.ps(f"qb{i}", [128, 512]) for i in range(4)]

    px_ld = [ph.sem("px_ld0"), ph.sem("px_ld1")]
    pr_cv = ph.sem("pr_cv")
    kv_ld = ph.sem("kv_ld")
    kv_cv = ph.sem("kv_cv")
    w_ld = [ph.sem("w_ld0"), ph.sem("w_ld1")]
    w_cv = ph.sem("w_cv")
    p_k = ph.sem("p_k")
    p_v = ph.sem("p_v")
    p_q = ph.sem("p_q")
    v_k = ph.sem("v_k")
    v_v = ph.sem("v_v")
    v_q = ph.sem("v_q")
    s_k = [ph.sem("s_k0"), ph.sem("s_k1")]
    s_v = [ph.sem("s_v0"), ph.sem("s_v1")]
    s_q = [ph.sem("s_q0"), ph.sem("s_q1")]

    with nc.Block(name) as blk:

        @blk.sync
        def _(sy):
            for w in range(4):
                src = WK1 if w < 2 else WV1
                hh = w % 2
                if w > 0:
                    ph.w(sy, kv_cv, w)
                sy.dma_start(out=kland[:, :, :],
                             in_=src[hh * 2048:(hh + 1) * 2048, :]
                             .rearrange("(i p) d -> p i d", p=128)).then_inc(kv_ld, 16); ph.a(kv_ld, 16)
            for ch in range(32):
                if ch >= 2:
                    ph.w(sy, pr_cv, ch - 1)
                sy.dma_start(out=land[ch % 2][:, :],
                             in_=X2[ch * 128:(ch + 1) * 128, :]
                             ).then_inc(px_ld[ch % 2], 16); ph.a(px_ld[ch % 2], 16)
            for hd in range(32):
                if hd >= 2:
                    ph.w(sy, w_cv, hd - 1)
                for hh in range(2):
                    sy.dma_start(
                        out=landw[hd % 2][:, hh * 16:(hh + 1) * 16, :],
                        in_=WQ1_r[hd, hh * 16:(hh + 1) * 16, :, :].rearrange(
                            "i p c -> p i c")).then_inc(w_ld[hd % 2], 16); ph.a(w_ld[hd % 2], 16)

        @blk.vector
        def _(ve):
            for w in range(4):
                ph.w(ve, kv_ld, 16 * (w + 1))
                dst = swk if w < 2 else swv
                off = (w % 2) * 2048
                ve.tensor_copy(
                    out=dst[:, off:off + 2048].rearrange("p (i c) -> p i c", i=16),
                    in_=kland[:, :, :]).then_inc(kv_cv, 1); ph.a(kv_cv, 1)
            for ch in range(32):
                ph.w(ve, px_ld[ch % 2], 16 * (ch // 2 + 1))
                ve.tensor_copy(out=sxb[:, ch, :],
                               in_=land[ch % 2][:, :]).then_inc(pr_cv, 1); ph.a(pr_cv, 1)
            for half in range(2):
                ph.w(ve, p_k, half + 1)
                if half >= 2:
                    ph.w(ve, s_k[half % 2], 16 * (half // 2))
                ve.tensor_copy(out=kstg[half % 2][:, :],
                               in_=kb[half][:, :]).then_inc(v_k, 1); ph.a(v_k, 1)
            for tt in range(8):
                ph.w(ve, p_v, tt + 1)
                if tt >= 2:
                    ph.w(ve, s_v[tt % 2], 16 * (tt // 2))
                ve.tensor_copy(out=vstg[tt % 2][:, :],
                               in_=vb[tt % 2][:, :]).then_inc(v_v, 1); ph.a(v_v, 1)
            for hd in range(32):
                for half in range(2):
                    g = hd * 2 + half
                    ph.w(ve, p_q, g + 1)
                    if hd >= 2:
                        ph.w(ve, s_q[hd % 2], 16 * (hd // 2))
                    ve.tensor_copy(out=qstg[hd % 2][:, half * 512:(half + 1) * 512],
                                   in_=qb[(hd % 2) * 2 + half][:, :]).then_inc(v_q, 1); ph.a(v_q, 1)

        @blk.tensor
        def _(te):
            ph.w(te, kv_cv, 4)
            ph.w(te, pr_cv, 32)
            for half in range(2):
                for i in range(32):
                    ins = te.matmul(out=kb[half][:, :],
                                    lhsT=swk[:, i * 128:(i + 1) * 128],
                                    rhs=sxb[:, i, half * 512:(half + 1) * 512],
                                    start=(i == 0), stop=(i == 31))
                ins.then_inc(p_k, 1); ph.a(p_k, 1)
            for tt in range(8):
                if tt >= 2:
                    ph.w(te, v_v, tt - 1)
                for i in range(32):
                    ins = te.matmul(out=vb[tt % 2][:, :],
                                    lhsT=sxb[:, i, tt * 128:(tt + 1) * 128],
                                    rhs=swv[:, i * 128:(i + 1) * 128],
                                    start=(i == 0), stop=(i == 31))
                ins.then_inc(p_v, 1); ph.a(p_v, 1)
            for hd in range(32):
                ph.w(te, w_cv, hd + 1)
                if hd >= 2:
                    ph.w(te, v_q, 2 * (hd - 1))
                for half in range(2):
                    for i in range(32):
                        ins = te.matmul(
                            out=qb[(hd % 2) * 2 + half][:, :],
                            lhsT=swq[hd % 2][:, i * 128:(i + 1) * 128],
                            rhs=sxb[:, i, half * 512:(half + 1) * 512],
                            start=(i == 0), stop=(i == 31))
                    ins.then_inc(p_q, 1); ph.a(p_q, 1)

        @blk.scalar
        def _(sc):
            for hd in range(32):
                ph.w(sc, w_ld[hd % 2], 32 * (hd // 2 + 1))
                if hd >= 2:
                    ph.w(sc, p_q, 2 * (hd - 1))
                sc.activation(out=swq[hd % 2][:, :].rearrange(
                                  "p (i c) -> p i c", i=32),
                              in_=landw[hd % 2][:, :, :],
                              func=AF.Copy).then_inc(w_cv, 1); ph.a(w_cv, 1)

        @blk.gpsimd
        def _(gp):
            for half in range(2):
                ph.w(gp, v_k, half + 1)
                gp.dma_start(out=KT1c[:, half * 512:(half + 1) * 512],
                             in_=kstg[half % 2][:, :]).then_inc(s_k[half % 2], 16); ph.a(s_k[half % 2], 16)
            for tt in range(8):
                ph.w(gp, v_v, tt + 1)
                gp.dma_start(out=V1c[tt * 128:(tt + 1) * 128, :],
                             in_=vstg[tt % 2][:, :]).then_inc(s_v[tt % 2], 16); ph.a(s_v[tt % 2], 16)
            for hd in range(32):
                ph.w(gp, v_q, 2 * hd + 2)
                gp.dma_start(out=QT1[hd * 128:(hd + 1) * 128, :],
                             in_=qstg[hd % 2][:, :]).then_inc(s_q[hd % 2], 16); ph.a(s_q[hd % 2], 16)
            ph.w(gp, s_k[0], 16)
            ph.w(gp, s_k[1], 16)
            ph.w(gp, s_v[0], 16 * 4)
            ph.w(gp, s_v[1], 16 * 4)
            ph.w(gp, s_q[0], 16 * 16)
            ph.w(gp, s_q[1], 16 * 16)

    ph.close()


def emit_p5(nc, io, name, QT1, KG1, VG1, TTHR, OT1):
    ph = P(nc, name, io.get("pool"))
    iota_d, ones_col, ones_row = io["iota"], io["ones_col"], io["ones_row"]

    sk = ph.sb("sk", [128, T], BF16)
    sv = ph.sb("sv", [128, 32, 128], BF16)
    siota = ph.sb("siota", [128, 512], F32)
    sthr = ph.sb("sthr", [128, 48], F32)
    sq = [ph.sb(f"sq{i}", [128, 512], BF16) for i in range(2)]
    sp = [ph.sb(f"sp{i}", [128, 512], BF16) for i in range(2)]
    spm = [ph.sb(f"spm{i}", [128, 512], BF16) for i in range(2)]
    sm01 = ph.sb("sm01", [128, 32, 512], BF16)
    sones = ph.sb("sones", [128, 1], BF16)
    sonesr = ph.sb("sonesr", [1, 128], F32)
    srcp = ph.sb("srcp", [1, 512], F32)
    srb = ph.sb("srb", [128, 512], F32)
    ostg = [ph.sb(f"ostg{i}", [128, 512], BF16) for i in range(2)]
    sbias = ph.sb("sbias", [128, 1], F32)

    bO = ph.ps("bO", [128, 512])
    bS = [ph.ps(f"bS{i}", [128, 512]) for i in range(2)]
    bD = ph.ps("bD", [1, 512])
    bRB = ph.ps("bRB", [128, 512])

    c_ld = ph.sem("c_ld")
    q_ld = [ph.sem("q_ld0"), ph.sem("q_ld1")]
    s_mm = ph.sem("s_mm")
    s_pv = ph.sem("s_pv")
    s_dn = ph.sem("s_dn")
    s_bc = ph.sem("s_bc")
    a_ex = ph.sem("a_ex")
    v_mk = ph.sem("v_mk")
    v_m01 = ph.sem("v_m01")
    v_rcp = ph.sem("v_rcp")
    v_ev = ph.sem("v_ev")
    p_st = [ph.sem("p_st0"), ph.sem("p_st1")]
    v_ini = ph.sem("v_ini")
    ch_ve = ph.sem("ch_ve")

    PA = [16, 32]
    NHD = 32

    def g_of(a, h, b):
        return (a * NHD * PA[0] if a else 0) + h * PA[a] + b

    G_TOT = NHD * (PA[0] + PA[1])

    with nc.Block(name) as blk:

        fused = io.get("fused", False)
        NCLD = 20 if fused else 9

        @blk.sync
        def _(sy):
            if fused:
                for g in range(8):
                    j = g if g < 4 else 7 - g
                    half = 0 if g < 4 else 1
                    sy.dma_start(out=sk[:, g * 512:(g + 1) * 512],
                                 in_=KG1[j, :, half * 512:(half + 1) * 512]
                                 ).then_inc(c_ld, 16); ph.a(c_ld, 16)
                for g in range(8):
                    j = g if g < 4 else 7 - g
                    half = 0 if g < 4 else 1
                    sy.dma_start(
                        out=sv[:, g * 4:(g + 1) * 4, :],
                        in_=VG1[j, half * 512:(half + 1) * 512, :].rearrange(
                            "(kc p) d -> p kc d", p=128)).then_inc(c_ld, 16); ph.a(c_ld, 16)
            else:
                sy.dma_start(out=sk[:, :], in_=KG1[:, :]).then_inc(c_ld, 16); ph.a(c_ld, 16)
                for part in range(4):
                    sy.dma_start(
                        out=sv[:, part * 8:(part + 1) * 8, :],
                        in_=VG1[part * 1024:(part + 1) * 1024, :].rearrange(
                            "(kc p) d -> p kc d", p=128)).then_inc(c_ld, 16); ph.a(c_ld, 16)
            sy.dma_start(out=siota[:, :], in_=iota_d[:, :]).then_inc(c_ld, 16); ph.a(c_ld, 16)
            sy.dma_start(out=sthr[:, :], in_=TTHR[:, :]).then_inc(c_ld, 16); ph.a(c_ld, 16)
            sy.dma_start(out=sones[:, :], in_=ones_col[:, :]).then_inc(c_ld, 16); ph.a(c_ld, 16)
            sy.dma_start(out=sonesr[:, :], in_=ones_row[:, :]).then_inc(c_ld, 16); ph.a(c_ld, 16)
            for a in range(2):
                for h in range(NHD):
                    ha = a * NHD + h
                    if ha >= 2:
                        ph.w(sy, s_dn, ha - 1)
                    sy.dma_start(out=sq[ha % 2][:, :],
                                 in_=QT1[h * 128:(h + 1) * 128,
                                         a * 512:(a + 1) * 512]
                                 ).then_inc(q_ld[ha % 2], 16); ph.a(q_ld[ha % 2], 16)

        @blk.vector
        def _(ve):
            ve.memset(sbias[:, :], -SOFT_B).then_inc(v_ini, 1); ph.a(v_ini, 1)
            ph.w(ve, c_ld, 16 * NCLD)
            for a in range(2):
                if a == 1:
                    ph.w(ve, v_mk, NHD * PA[0])
                for b in range(PA[a]):
                    ve.tensor_scalar(out=sm01[:, b, :], in0=siota[:, :],
                                     scalar1=sthr[:, 16 * a + b:16 * a + b + 1],
                                     scalar2=None,
                                     op0=ALU.is_ge).then_inc(v_m01, 1); ph.a(v_m01, 1)
                for h in range(NHD):
                    ha = a * NHD + h
                    km = PA[a]
                    for b in range(km):
                        g = g_of(a, h, b)
                        ph.w(ve, a_ex, g + 1)
                        if g >= 2:
                            ph.w(ve, s_pv, g - 1)
                        ve.tensor_tensor(out=spm[g % 2][:, :], in0=sp[g % 2][:, :],
                                         in1=sm01[:, b, :],
                                         op=ALU.mult).then_inc(v_mk, 1); ph.a(v_mk, 1)
                    ph.w(ve, s_dn, ha + 1)
                    ve.reciprocal(out=srcp[:, :], in_=bD[0:1, :]).then_inc(v_rcp, 1); ph.a(v_rcp, 1)
                    ph.w(ve, s_bc, ha + 1)
                    ve.tensor_copy(out=srb[:, :], in_=bRB[:, :]).then_inc(ch_ve, 1); ph.a(ch_ve, 1)
                    if ha >= 2:
                        ph.w(ve, p_st[ha % 2], 16 * (ha // 2))
                    ph.w(ve, ch_ve, ha + 1)
                    ve.tensor_tensor(out=ostg[ha % 2][:, :], in0=bO[:, :],
                                     in1=srb[:, :], op=ALU.mult).then_inc(v_ev, 1); ph.a(v_ev, 1)

        @blk.tensor
        def _(te):
            ph.w(te, c_ld, 16 * NCLD)
            for a in range(2):
                for h in range(NHD):
                    ha = a * NHD + h
                    km = PA[a]
                    for b in range(km):
                        g = g_of(a, h, b)
                        if b == 0:
                            ph.w(te, q_ld[ha % 2], 16 * (ha // 2 + 1))
                            ph.w(te, v_ev, ha)
                            ph.w(te, v_rcp, ha)
                            ph.w(te, v_m01,
                                 PA[0] if a == 0 else PA[0] + PA[1])
                        if g >= 2:
                            ph.w(te, a_ex, g - 1)
                        te.matmul(out=bS[g % 2][:, :],
                                  lhsT=sk[:, b * 128:(b + 1) * 128],
                                  rhs=sq[ha % 2][:, :], start=True, stop=True
                                  ).then_inc(s_mm, 1); ph.a(s_mm, 1)
                        ph.w(te, v_mk, g + 1)
                        te.matmul(out=bO[:, :],
                                  lhsT=sv[:, b, :], rhs=spm[g % 2][:, :],
                                  start=(b == 0), stop=(b == km - 1)
                                  ).then_inc(s_pv, 1); ph.a(s_pv, 1)
                        ins_d = te.matmul(out=bD[:, :], lhsT=sones[:, :],
                                          rhs=spm[g % 2][:, :],
                                          start=(b == 0), stop=(b == km - 1),
                                          skip_group_check=True)
                        if b == km - 1:
                            ins_d.then_inc(s_dn, 1); ph.a(s_dn, 1)
                    ph.w(te, v_rcp, ha + 1)
                    ph.w(te, v_ev, ha)
                    te.matmul(out=bRB[:, :], lhsT=sonesr[:, :], rhs=srcp[:, :],
                              start=True, stop=True).then_inc(s_bc, 1); ph.a(s_bc, 1)

        @blk.scalar
        def _(sc):
            ph.w(sc, v_ini, 1)
            for a in range(2):
                for h in range(NHD):
                    km = PA[a]
                    for b in range(km):
                        g = g_of(a, h, b)
                        ph.w(sc, s_mm, g + 1)
                        if g >= 2:
                            ph.w(sc, v_mk, g - 1)
                        sc.activation(out=sp[g % 2][:, :], in_=bS[g % 2][:, :],
                                      func=AF.Exp, scale=SCALE,
                                      bias=sbias[:, 0:1]).then_inc(a_ex, 1); ph.a(a_ex, 1)

        @blk.gpsimd
        def _(gp):
            for a in range(2):
                for h in range(NHD):
                    ha = a * NHD + h
                    ph.w(gp, v_ev, ha + 1)
                    gp.dma_start(out=OT1[h * 128:(h + 1) * 128,
                                         a * 512:(a + 1) * 512],
                                 in_=ostg[ha % 2][:, :]).then_inc(p_st[ha % 2], 16); ph.a(p_st[ha % 2], 16)
            ph.w(gp, p_st[0], 16 * 32)
            ph.w(gp, p_st[1], 16 * 32)

    ph.close()


def emit_ffn1(nc, io, name, X3, BW_r, GWT, ATALL, BTALL, BB_r, OUT, OUTPRE):
    ph = P(nc, name, io.get("pool"))
    ones_col, ones_row = io["ones_col"], io["ones_row"]
    ident_d, exp8_d = io["ident"], io["exp8"]

    sxb = ph.sb("sxb", [128, 32, 1024], BF16)
    land = [ph.sb(f"land{i}", [128, 1024], F32) for i in range(2)]
    landw = [ph.sb(f"landw{i}", [128, 16, 128], F32) for i in range(2)]
    swb = [ph.sb(f"swb{i}", [128, 4096], BF16) for i in range(2)]
    sx = [ph.sb(f"sx{i}", [128, 1024], F32) for i in range(2)]
    sy1 = [ph.sb(f"sy1{i}", [128, 1024], F32) for i in range(2)]
    st_t = [ph.sb(f"stt{i}", [128, 1024], F32) for i in range(2)]
    stb = [ph.sb(f"stb{i}", [128, 1024], BF16) for i in range(2)]
    stsq = [ph.sb(f"stsq{i}", [128, 1024], BF16) for i in range(2)]
    sonesc = ph.sb("sonesc", [128, 1], BF16)
    sonesr = ph.sb("sonesr", [1, 128], F32)
    sid = ph.sb("sid", [128, 128], F32)
    sexp8 = ph.sb("sexp8", [8, 128], BF16)
    sgw = ph.sb("sgw", [128, 32, 8], BF16)
    sat = ph.sb("sat", [128, 32, 128], BF16)
    sbt = ph.sb("sbt", [128, 4096], BF16)
    sbb = ph.sb("sbb", [128, 32], F32)
    sgrow = ph.sb("sgrow", [8, 1024], F32)
    sgt = ph.sb("sgt", [128, 64], F32)
    swork = ph.sb("swork", [128, 32], F32)
    svec = ph.sb("svec", [128, 8], F32)
    swf = ph.sb("swf", [128, 64], F32)
    swfT = ph.sb("swfT", [8, 1024], BF16)
    swfe = ph.sb("swfe", [128, 1024], BF16)
    slw = ph.sb("slw", [128, 1024], BF16)
    srow = ph.sb("srow", [1, 4096], F32)
    sra = ph.sb("sra", [128, 1024], F32)
    srb2 = ph.sb("srb2", [128, 1024], F32)
    land2 = [ph.sb(f"land2{i}", [128, 1024], F32) for i in range(2)]
    stage2 = [ph.sb(f"stage2{i}", [128, 1024], F32) for i in range(2)]
    eps_t = ph.sb("eps_t", [1, 1], F32)

    yb = [ph.ps(f"yb{i}", [128, 512]) for i in range(4)]
    ssum = [ph.ps(f"ssum{i}", [8, 512]) for i in range(2)]
    ssq = [ph.ps(f"ssq{i}", [1, 512]) for i in range(2)]

    c_ld = ph.sem("c_ld")
    px_ld = [ph.sem("px_ld0"), ph.sem("px_ld1")]
    pr_cv = ph.sem("pr_cv")
    g_pe = ph.sem("g_pe")
    g_ve = ph.sem("g_ve")
    g_ac = ph.sem("g_ac")
    w_ld = [ph.sem("w_ld0"), ph.sem("w_ld1")]
    w_cv = ph.sem("w_cv")
    x_ld = [ph.sem("x_ld0"), ph.sem("x_ld1")]
    p_mm = ph.sem("p_mm")
    p_sd = ph.sem("p_sd")
    p_bc = ph.sem("p_bc")
    a_ev = ph.sem("a_ev")
    v_t = ph.sem("v_t")
    v_st = ph.sem("v_st")
    a_sq = ph.sem("a_sq")
    p_pre = [ph.sem("p_pre0"), ph.sem("p_pre1")]
    v_row = ph.sem("v_row")
    a_row = ph.sem("a_row")
    v_bc = ph.sem("v_bc")
    x2_ld = [ph.sem("x2_ld0"), ph.sem("x2_ld1")]
    v_p2 = ph.sem("v_p2")
    v_p3 = ph.sem("v_p3")
    p_out = [ph.sem("p_out0"), ph.sem("p_out1")]
    ini = ph.sem("ini")
    cch = ph.sem("cch")

    NOB = 32

    with nc.Block(name) as blk:

        @blk.sync
        def _(sy):
            sy.dma_start(out=sonesc[:, :], in_=ones_col[:, :]).then_inc(c_ld, 16); ph.a(c_ld, 16)
            sy.dma_start(out=sonesr[:, :], in_=ones_row[:, :]).then_inc(c_ld, 16); ph.a(c_ld, 16)
            sy.dma_start(out=sid[:, :], in_=ident_d[:, :]).then_inc(c_ld, 16); ph.a(c_ld, 16)
            sy.dma_start(out=sexp8[:, :], in_=exp8_d[:, :]).then_inc(c_ld, 16); ph.a(c_ld, 16)
            sy.dma_start(out=sbb[:, :], in_=BB_r[:, :]).then_inc(c_ld, 16); ph.a(c_ld, 16)
            for ch in range(32):
                if ch >= 2:
                    ph.w(sy, pr_cv, ch - 1)
                sy.dma_start(out=land[ch % 2][:, :],
                             in_=X3[ch * 128:(ch + 1) * 128, :]
                             ).then_inc(px_ld[ch % 2], 16); ph.a(px_ld[ch % 2], 16)
            ph.w(sy, pr_cv, 31)
            sy.dma_start(out=land[0][:, 0:256].rearrange("p (i e) -> p i e", i=32),
                         in_=GWT[:, :].rearrange("(i p) e -> p i e", p=128)
                         ).then_inc(c_ld, 16); ph.a(c_ld, 16)
            for j in range(4):
                ph.w(sy, pr_cv, 33 + j - 1 if j >= 2 else (33 if j == 0 else 32))
                sy.dma_start(
                    out=land[j % 2][:, :].rearrange("p (i e) -> p i e", i=8),
                    in_=ATALL[j * 1024:(j + 1) * 1024, :].rearrange(
                        "(i p) e -> p i e", p=128)).then_inc(c_ld, 16); ph.a(c_ld, 16)
            for k in range(4):
                ph.w(sy, pr_cv, 36 + k)
                sy.dma_start(out=land[k % 2][:, :],
                             in_=BTALL[:, k * 1024:(k + 1) * 1024]
                             ).then_inc(c_ld, 16); ph.a(c_ld, 16)
            for ob in range(NOB):
                for hh in range(2):
                    pcn = 2 * ob + hh
                    if hh == 1:
                        ph.w(sy, w_cv, pcn)
                    elif ob >= 2:
                        ph.w(sy, w_cv, pcn - 2)
                    sy.dma_start(
                        out=landw[ob % 2][:, :, :],
                        in_=BW_r[ob, hh * 16:(hh + 1) * 16, :, :].rearrange(
                            "i p c -> p i c")).then_inc(w_ld[ob % 2], 16); ph.a(w_ld[ob % 2], 16)
                if ob >= 2:
                    ph.w(sy, v_t, 2 * (ob - 1))
                sy.dma_start(out=sx[ob % 2][:, :],
                             in_=X3[ob * 128:(ob + 1) * 128, :]
                             ).then_inc(x_ld[ob % 2], 16); ph.a(x_ld[ob % 2], 16)
            for ch in range(NOB):
                ph.w(sy, p_pre[ch % 2], 16 * (ch // 2 + 1))
                if ch >= 2:
                    ph.w(sy, v_p2, ch - 1)
                sy.dma_start(out=land2[ch % 2][:, :],
                             in_=OUTPRE[ch * 128:(ch + 1) * 128, :]
                             ).then_inc(x2_ld[ch % 2], 16); ph.a(x2_ld[ch % 2], 16)

        @blk.vector
        def _(ve):
            ve.memset(eps_t[:, :], EPS).then_inc(ini, 1); ph.a(ini, 1)
            for ch in range(32):
                ph.w(ve, px_ld[ch % 2], 16 * (ch // 2 + 1))
                ve.tensor_copy(out=sxb[:, ch, :],
                               in_=land[ch % 2][:, :]).then_inc(pr_cv, 1); ph.a(pr_cv, 1)
            ph.w(ve, c_ld, 96)
            ve.tensor_copy(out=sgw[:, :, :],
                           in_=land[0][:, 0:256].rearrange(
                               "p (i e) -> p i e", i=32)).then_inc(pr_cv, 1); ph.a(pr_cv, 1)
            for j in range(4):
                ph.w(ve, c_ld, 128 if j < 2 else 160)
                ve.tensor_copy(out=sat[:, j * 8:(j + 1) * 8, :],
                               in_=land[j % 2][:, :].rearrange(
                                   "p (i e) -> p i e", i=8)).then_inc(pr_cv, 1); ph.a(pr_cv, 1)
            for k in range(4):
                ph.w(ve, c_ld, 192 if k < 2 else 224)
                ve.tensor_copy(out=sbt[:, k * 1024:(k + 1) * 1024],
                               in_=land[k % 2][:, :]).then_inc(pr_cv, 1); ph.a(pr_cv, 1)
            ph.w(ve, g_pe, 2)
            ve.tensor_copy(out=sgrow[:, 0:512], in_=ssum[0][0:8, :])
            ve.tensor_copy(out=sgrow[:, 512:1024],
                           in_=ssum[1][0:8, :]).then_inc(g_ve, 1); ph.a(g_ve, 1)
            gv = 1
            io["_p7_a"] = a_marks = []
            io["_p7_b"] = b_marks = []
            io["_p7_h"] = h_marks = []
            io["_p7_l"] = l_marks = []
            for tt in range(8):
                ph.w(ve, g_pe, 2 + tt + 1)
                ve.tensor_copy(out=sgt[:, tt * 8:(tt + 1) * 8],
                               in_=yb[0][:, 0:8]).then_inc(g_ve, 1); ph.a(g_ve, 1)
                gv += 1; a_marks.append(gv)
                gts = sgt[:, tt * 8:(tt + 1) * 8]
                ph.w(ve, g_ve, gv)
                if tt >= 1:
                    ph.w(ve, g_ac, tt)
                ve.reduce_max(out=svec[:, 0:1], in_=gts, axis=AX,
                              negate=True).then_inc(g_ve, 1); ph.a(g_ve, 1)
                gv += 1; b_marks.append(gv)
                ph.w(ve, g_ve, gv)
                ve.tensor_scalar(out=svec[:, 1:2], in0=svec[:, 0:1],
                                 scalar1=-1.0, scalar2=None,
                                 op0=ALU.mult).then_inc(g_ve, 1); ph.a(g_ve, 1)
                gv += 1
                ph.w(ve, g_ve, gv)
                ve.tensor_scalar(out=swork[:, 0:8], in0=gts,
                                 scalar1=svec[:, 1:2], scalar2=None,
                                 op0=ALU.is_ge).then_inc(g_ve, 1); ph.a(g_ve, 1)
                gv += 1
                ph.w(ve, g_ve, gv)
                ve.scalar_tensor_tensor(out=swork[:, 8:16], in0=swork[:, 0:8],
                                        scalar=1e30, in1=gts,
                                        op0=ALU.mult,
                                        op1=ALU.subtract).then_inc(g_ve, 1); ph.a(g_ve, 1)
                gv += 1
                ph.w(ve, g_ve, gv)
                ve.tensor_reduce(out=svec[:, 2:3], in_=swork[:, 8:16], axis=AX,
                                 op=ALU.min, negate=True).then_inc(g_ve, 1); ph.a(g_ve, 1)
                gv += 1
                ph.w(ve, g_ve, gv)
                ve.tensor_scalar(out=swork[:, 24:32], in0=gts,
                                 scalar1=svec[:, 2:3], scalar2=None,
                                 op0=ALU.is_ge).then_inc(g_ve, 1); ph.a(g_ve, 1)
                gv += 1
                ph.w(ve, g_ac, tt + 1)
                ph.w(ve, g_ve, gv)
                ve.tensor_tensor(out=swork[:, 16:24], in0=swork[:, 16:24],
                                 in1=swork[:, 24:32],
                                 op=ALU.mult).then_inc(g_ve, 1); ph.a(g_ve, 1)
                gv += 1; h_marks.append(gv)
                ph.w(ve, g_ve, gv)
                ve.reduce_sum(out=svec[:, 3:4], in_=swork[:, 16:24],
                              axis=AX).then_inc(g_ve, 1); ph.a(g_ve, 1)
                gv += 1
                ph.w(ve, g_ve, gv)
                ve.reciprocal(out=svec[:, 4:5], in_=svec[:, 3:4]).then_inc(g_ve, 1); ph.a(g_ve, 1)
                gv += 1
                ph.w(ve, g_ve, gv)
                ve.tensor_scalar(out=svec[:, 5:6], in0=svec[:, 4:5],
                                 scalar1=2.0, scalar2=None,
                                 op0=ALU.mult).then_inc(g_ve, 1); ph.a(g_ve, 1)
                gv += 1
                ph.w(ve, g_ve, gv)
                ve.tensor_scalar(out=swf[:, tt * 8:(tt + 1) * 8],
                                 in0=swork[:, 16:24],
                                 scalar1=svec[:, 5:6], scalar2=None,
                                 op0=ALU.mult).then_inc(g_ve, 1); ph.a(g_ve, 1)
                gv += 1; l_marks.append(gv)
            t_marks = []
            for tt in range(8):
                ph.w(ve, g_pe, 10 + 2 * (tt + 1))
                ve.tensor_copy(out=swfT[:, tt * 128:(tt + 1) * 128],
                               in_=yb[1][0:8, 0:128]).then_inc(g_ve, 1); ph.a(g_ve, 1)
                gv += 1; t_marks.append(gv)
            io["_p7_t"] = t_marks
            ph.w(ve, g_pe, 28)
            ve.tensor_copy(out=swfe[:, 0:512], in_=yb[2][:, :]).then_inc(g_ve, 1); ph.a(g_ve, 1)
            gv += 1
            ve.tensor_copy(out=swfe[:, 512:1024],
                           in_=yb[3][:, :]).then_inc(g_ve, 1); ph.a(g_ve, 1)
            gv += 1
            swfe_mark = gv
            ph.w(ve, g_pe, 30)
            ph.w(ve, g_ve, swfe_mark)
            ve.tensor_tensor(out=slw[:, 0:512], in0=yb[0][:, :],
                             in1=swfe[:, 0:512], op=ALU.mult).then_inc(g_ve, 1); ph.a(g_ve, 1)
            gv += 1
            ve.tensor_tensor(out=slw[:, 512:1024], in0=yb[1][:, :],
                             in1=swfe[:, 512:1024], op=ALU.mult).then_inc(g_ve, 1); ph.a(g_ve, 1)
            gv += 1
            io["_p7_done"] = gv
            for ob in range(NOB):
                for hh in range(2):
                    ph.w(ve, w_ld[ob % 2], 32 * (ob // 2) + 16 * (hh + 1))
                    if ob >= 2 and hh == 0:
                        ph.w(ve, p_mm, 2 * (ob - 1))
                    ve.tensor_copy(
                        out=swb[ob % 2][:, hh * 2048:(hh + 1) * 2048].rearrange(
                            "p (i c) -> p i c", i=16),
                        in_=landw[ob % 2][:, :, :]).then_inc(w_cv, 1); ph.a(w_cv, 1)
                for half in range(2):
                    hs = slice(half * 512, (half + 1) * 512)
                    ph.w(ve, a_ev, 2 * ob + half + 1)
                    ph.w(ve, x_ld[ob % 2], 16 * (ob // 2 + 1))
                    if ob >= 2:
                        ph.w(ve, p_pre[ob % 2], 16 * (ob // 2))
                        ph.w(ve, a_sq, ob - 1)
                        ph.w(ve, p_sd, ob - 1)
                    ve.tensor_tensor(out=st_t[ob % 2][:, hs],
                                     in0=sy1[ob % 2][:, hs],
                                     in1=sx[ob % 2][:, hs],
                                     op=ALU.add).then_inc(v_t, 1); ph.a(v_t, 1)
                    ph.w(ve, v_t, 2 * ob + half + 1)
                    ins = ve.tensor_copy(out=stb[ob % 2][:, hs],
                                         in_=st_t[ob % 2][:, hs])
                    if half == 1:
                        ins.then_inc(v_st, 1); ph.a(v_st, 1)
            ph.w(ve, p_sd, NOB)
            ve.tensor_copy(out=srow[:, 0:512], in_=ssum[0][0:1, :]).then_inc(cch, 1); ph.a(cch, 1)
            ve.tensor_copy(out=srow[:, 512:1024],
                           in_=ssum[1][0:1, :]).then_inc(cch, 1); ph.a(cch, 1)
            ve.tensor_copy(out=srow[:, 1024:1536], in_=ssq[0][:, :]).then_inc(cch, 1); ph.a(cch, 1)
            ve.tensor_copy(out=srow[:, 1536:2048], in_=ssq[1][:, :]).then_inc(cch, 1); ph.a(cch, 1)
            ph.w(ve, cch, 4)
            ve.tensor_scalar(out=srow[:, 2048:3072], in0=srow[:, 0:1024],
                             scalar1=1.0 / C, scalar2=None,
                             op0=ALU.mult).then_inc(cch, 1); ph.a(cch, 1)
            ve.tensor_scalar(out=srow[:, 3072:4096], in0=srow[:, 1024:2048],
                             scalar1=1.0 / C, scalar2=None,
                             op0=ALU.mult).then_inc(cch, 1); ph.a(cch, 1)
            ph.w(ve, cch, 6)
            ve.tensor_tensor(out=srow[:, 0:1024], in0=srow[:, 2048:3072],
                             in1=srow[:, 2048:3072], op=ALU.mult).then_inc(cch, 1); ph.a(cch, 1)
            ph.w(ve, cch, 7)
            ve.tensor_tensor(out=srow[:, 1024:2048], in0=srow[:, 3072:4096],
                             in1=srow[:, 0:1024],
                             op=ALU.subtract).then_inc(v_row, 1); ph.a(v_row, 1)
            ph.w(ve, a_row, 1)
            ve.reciprocal(out=srow[:, 3072:4096],
                          in_=srow[:, 0:1024]).then_inc(cch, 1); ph.a(cch, 1)
            ph.w(ve, cch, 8)
            ve.tensor_tensor(out=srow[:, 1024:2048], in0=srow[:, 2048:3072],
                             in1=srow[:, 3072:4096], op=ALU.mult).then_inc(cch, 1); ph.a(cch, 1)
            ph.w(ve, cch, 9)
            ve.tensor_scalar(out=srow[:, 1024:2048], in0=srow[:, 1024:2048],
                             scalar1=-1.0, scalar2=None,
                             op0=ALU.mult).then_inc(v_row, 1); ph.a(v_row, 1)
            ph.w(ve, p_bc, 1)
            ve.tensor_copy(out=sra[:, 0:512], in_=yb[0][:, :]).then_inc(cch, 1); ph.a(cch, 1)
            ve.tensor_copy(out=sra[:, 512:1024], in_=yb[1][:, :]).then_inc(cch, 1); ph.a(cch, 1)
            ve.tensor_copy(out=srb2[:, 0:512], in_=yb[2][:, :]).then_inc(cch, 1); ph.a(cch, 1)
            ve.tensor_copy(out=srb2[:, 512:1024],
                           in_=yb[3][:, :]).then_inc(v_bc, 1); ph.a(v_bc, 1)
            for ch in range(NOB):
                ph.w(ve, x2_ld[ch % 2], 16 * (ch // 2 + 1))
                ph.w(ve, v_bc, 1)
                ph.w(ve, cch, 12)
                if ch >= 2:
                    ph.w(ve, p_out[ch % 2], 16 * (ch // 2))
                ve.tensor_tensor(out=stage2[ch % 2][:, :],
                                 in0=land2[ch % 2][:, :],
                                 in1=sra[:, :], op=ALU.mult).then_inc(v_p2, 1); ph.a(v_p2, 1)
                ph.w(ve, v_p2, ch + 1)
                ve.tensor_tensor(out=stage2[ch % 2][:, :],
                                 in0=stage2[ch % 2][:, :],
                                 in1=srb2[:, :], op=ALU.add).then_inc(v_p3, 1); ph.a(v_p3, 1)

        @blk.scalar
        def _(sc):
            for tt in range(8):
                ph.w(sc, g_ve, io["_p7_b"][tt])
                if tt >= 1:
                    ph.w(sc, g_ve, io["_p7_h"][tt - 1])
                sc.activation(out=swork[:, 16:24],
                              in_=sgt[:, tt * 8:(tt + 1) * 8], func=AF.Exp,
                              bias=svec[:, 0:1]).then_inc(g_ac, 1); ph.a(g_ac, 1)
            for ob in range(NOB):
                for half in range(2):
                    hs = slice(half * 512, (half + 1) * 512)
                    ph.w(sc, p_mm, 2 * ob + half + 1)
                    if ob >= 2:
                        ph.w(sc, v_t, 2 * (ob - 1))
                    sc.activation(out=sy1[ob % 2][:, hs],
                                  in_=yb[(ob % 2) * 2 + half][:, :],
                                  func=AF.Identity, scale=1.0,
                                  bias=sbb[:, ob:ob + 1]).then_inc(a_ev, 1); ph.a(a_ev, 1)
                    ph.w(sc, v_t, 2 * ob + half + 1)
                    if ob >= 2:
                        ph.w(sc, p_sd, ob - 1)
                    ins = sc.activation(out=stsq[ob % 2][:, hs],
                                        in_=st_t[ob % 2][:, hs], func=AF.Square)
                    if half == 1:
                        ins.then_inc(a_sq, 1); ph.a(a_sq, 1)
            ph.w(sc, v_row, 1)
            ph.w(sc, ini, 1)
            sc.activation(out=srow[:, 0:1024], in_=srow[:, 1024:2048],
                          func=AF.Sqrt, bias=eps_t[:, 0:1]).then_inc(a_row, 1); ph.a(a_row, 1)

        def _stats7(te, o):
            ph.w(te, v_st, o + 1)
            ph.w(te, a_sq, o + 1)
            for half in range(2):
                hs = slice(half * 512, (half + 1) * 512)
                te.matmul(out=ssum[half][0:1, :], lhsT=sonesc[:, :],
                          rhs=stb[o % 2][:, hs], start=(o == 0), stop=(o == NOB - 1),
                          skip_group_check=True)
                ins = te.matmul(out=ssq[half][:, :], lhsT=sonesc[:, :],
                                rhs=stsq[o % 2][:, hs], start=(o == 0),
                                stop=(o == NOB - 1), skip_group_check=True)
            ins.then_inc(p_sd, 1); ph.a(p_sd, 1)

        @blk.tensor
        def _(te):
            ph.w(te, c_ld, 96)
            ph.w(te, pr_cv, 37)
            for half in range(2):
                for i in range(32):
                    ins = te.matmul(out=ssum[half][0:8, :],
                                    lhsT=sgw[:, i, :],
                                    rhs=sxb[:, i, half * 512:(half + 1) * 512],
                                    start=(i == 0), stop=(i == 31),
                                    skip_group_check=True)
            ins.then_inc(g_pe, 2); ph.a(g_pe, 2)
            for tt in range(8):
                ph.w(te, g_ve, 1)
                if tt >= 1:
                    ph.w(te, g_ve, io["_p7_a"][tt - 1])
                te.transpose(out=yb[0][:, 0:8],
                             in_=sgrow[0:8, tt * 128:(tt + 1) * 128],
                             identity=sid[0:8, 0:8]).then_inc(g_pe, 1); ph.a(g_pe, 1)
            for tt in range(8):
                ph.w(te, g_ve, io["_p7_l"][tt])
                if tt >= 1:
                    ph.w(te, g_ve, io["_p7_t"][tt - 1])
                te.transpose(out=yb[1][0:8, 0:128],
                             in_=swf[:, tt * 8:(tt + 1) * 8],
                             identity=sid[:, :]).then_inc(g_pe, 2); ph.a(g_pe, 2)
            ph.w(te, g_ve, io["_p7_t"][7])
            te.matmul(out=yb[2][:, :], lhsT=sexp8[:, :], rhs=swfT[0:8, 0:512],
                      start=True, stop=True)
            te.matmul(out=yb[3][:, :], lhsT=sexp8[:, :], rhs=swfT[0:8, 512:1024],
                      start=True, stop=True).then_inc(g_pe, 2); ph.a(g_pe, 2)
            for half in range(2):
                for i in range(32):
                    ins = te.matmul(out=yb[half][:, :], lhsT=sat[:, i, :],
                                    rhs=sxb[:, i, half * 512:(half + 1) * 512],
                                    start=(i == 0), stop=(i == 31))
            ins.then_inc(g_pe, 2); ph.a(g_pe, 2)
            ph.w(te, pr_cv, 41)
            ph.w(te, g_ve, io["_p7_done"])
            for ob in range(NOB):
                ph.w(te, w_cv, 2 * ob + 2)
                if ob >= 2:
                    ph.w(te, a_ev, 2 * (ob - 1))
                for half in range(2):
                    hs = slice(half * 512, (half + 1) * 512)
                    for cc in range(32):
                        te.matmul(
                            out=yb[(ob % 2) * 2 + half][:, :],
                            lhsT=swb[ob % 2][:, cc * 128:(cc + 1) * 128],
                            rhs=sxb[:, cc, hs], start=(cc == 0), stop=False)
                    te.matmul(out=yb[(ob % 2) * 2 + half][:, :],
                              lhsT=sbt[:, ob * 128:(ob + 1) * 128],
                              rhs=slw[:, hs],
                              start=False, stop=True).then_inc(p_mm, 1); ph.a(p_mm, 1)
                if ob >= 1:
                    _stats7(te, ob - 1)
            _stats7(te, NOB - 1)
            ph.w(te, v_row, 2)
            ph.w(te, a_ev, 2 * NOB)
            te.matmul(out=yb[0][:, :], lhsT=sonesr[:, :],
                      rhs=srow[:, 3072:3584], start=True, stop=True)
            te.matmul(out=yb[1][:, :], lhsT=sonesr[:, :],
                      rhs=srow[:, 3584:4096], start=True, stop=True)
            te.matmul(out=yb[2][:, :], lhsT=sonesr[:, :],
                      rhs=srow[:, 1024:1536], start=True, stop=True)
            te.matmul(out=yb[3][:, :], lhsT=sonesr[:, :],
                      rhs=srow[:, 1536:2048], start=True, stop=True).then_inc(p_bc, 1); ph.a(p_bc, 1)

        @blk.gpsimd
        def _(gp):
            for ob in range(NOB):
                ph.w(gp, v_t, 2 * ob + 2)
                ph.w(gp, a_sq, ob + 1)
                gp.dma_start(out=OUTPRE[ob * 128:(ob + 1) * 128, :],
                             in_=st_t[ob % 2][:, :]).then_inc(p_pre[ob % 2], 16); ph.a(p_pre[ob % 2], 16)
            for ch in range(NOB):
                ph.w(gp, v_p3, ch + 1)
                gp.dma_start(out=OUT[ch * 128:(ch + 1) * 128, :],
                             in_=stage2[ch % 2][:, :]).then_inc(p_out[ch % 2], 16); ph.a(p_out[ch % 2], 16)
            ph.w(gp, p_out[0], 16 * 16)
            ph.w(gp, p_out[1], 16 * 16)

    ph.close()


def build_launch2():
    nc = bass.Bass("TRN2", target_bir_lowering=False)
    io = {}
    io["ones_col"] = _dram(nc, "ones_col", [128, 1], BF16, "ExternalInput")
    io["ones_row"] = _dram(nc, "ones_row", [1, 128], F32, "ExternalInput")
    OT = _dram(nc, "OT", [C, TPC], BF16, "ExternalInput")
    wo_r = _dram(nc, "wo_r", [32, 32, 128, 128], F32, "ExternalInput")
    xres = _dram(nc, "xres", [C, TPC], F32, "ExternalInput")
    qw_r = _dram(nc, "qw_r", [32, 32, 128, 128], I32, "ExternalInput")
    qb_r = _dram(nc, "qb_r", [128, 32], I32, "ExternalInput")
    ws = _dram(nc, "ws", [1, 1], F32, "ExternalInput")
    wzp = _dram(nc, "wzp", [1, 1], I32, "ExternalInput")
    bs = _dram(nc, "bs", [1, 1], F32, "ExternalInput")
    bzp = _dram(nc, "bzp", [1, 1], I32, "ExternalInput")
    wq1_r = _dram(nc, "wq1_r", [32, 32, 128, 128], F32, "ExternalInput")
    wk1 = _dram(nc, "wk1", [C, 128], F32, "ExternalInput")
    wv1 = _dram(nc, "wv1", [C, 128], F32, "ExternalInput")
    X1 = _dram(nc, "X1", [C, TPC], F32, "Internal")
    X1p = _dram(nc, "X1p", [C, TPC], F32, "Internal")
    X2p = _dram(nc, "X2p", [C, TPC], F32, "Internal")
    X2 = _dram(nc, "X2", [C, TPC], F32, "ExternalOutput")
    QT1 = _dram(nc, "QT1", [C, TPC], BF16, "ExternalOutput")
    KT1c = _dram(nc, "KT1c", [128, TPC], BF16, "ExternalOutput")
    V1c = _dram(nc, "V1c", [TPC, 128], BF16, "ExternalOutput")
    emit_woln(nc, io, "p2", OT, wo_r, xres, X1, X1p)
    nc.all_engine_barrier(sem_only=True)
    emit_ffn0(nc, io, "p3", X1, qw_r, qb_r, ws, wzp, bs, bzp, X2, X2p)
    nc.all_engine_barrier(sem_only=True)
    emit_p4(nc, io, "p4", X2, wq1_r, wk1, wv1, QT1, KT1c, V1c)
    return nc


def build_launch3():
    nc = bass.Bass("TRN2", target_bir_lowering=False)
    io = {}
    io["ones_col"] = _dram(nc, "ones_col", [128, 1], BF16, "ExternalInput")
    io["ones_row"] = _dram(nc, "ones_row", [1, 128], F32, "ExternalInput")
    io["iota"] = _dram(nc, "iota", [128, 512], F32, "ExternalInput")
    io["ident"] = _dram(nc, "ident", [128, 128], F32, "ExternalInput")
    io["exp8"] = _dram(nc, "exp8", [8, 128], BF16, "ExternalInput")
    X2 = _dram(nc, "X2", [C, TPC], F32, "ExternalInput")
    QT1 = _dram(nc, "QT1", [C, TPC], BF16, "ExternalInput")
    KG1 = _dram(nc, "KG1", [128, T], BF16, "ExternalInput")
    VG1 = _dram(nc, "VG1", [T, 128], BF16, "ExternalInput")
    tthr = _dram(nc, "tthr", [128, 48], F32, "ExternalInput")
    wo1_r = _dram(nc, "wo1_r", [32, 32, 128, 128], F32, "ExternalInput")
    bw_r = _dram(nc, "bw_r", [32, 32, 128, 128], F32, "ExternalInput")
    gwT = _dram(nc, "gwT", [C, 8], F32, "ExternalInput")
    atall = _dram(nc, "atall", [C, 128], F32, "ExternalInput")
    btall = _dram(nc, "btall", [128, C], F32, "ExternalInput")
    bb_r = _dram(nc, "bb_r", [128, 32], F32, "ExternalInput")
    OT1 = _dram(nc, "OT1", [C, TPC], BF16, "Internal")
    X3 = _dram(nc, "X3", [C, TPC], F32, "Internal")
    X3p = _dram(nc, "X3p", [C, TPC], F32, "Internal")
    OUTp = _dram(nc, "OUTp", [C, TPC], F32, "Internal")
    OUT = _dram(nc, "OUT", [C, TPC], F32, "ExternalOutput")
    emit_p5(nc, io, "p5", QT1, KG1, VG1, tthr, OT1)
    nc.all_engine_barrier(sem_only=True)
    emit_woln(nc, io, "p6", OT1, wo1_r, X2, X3, X3p)
    nc.all_engine_barrier(sem_only=True)
    emit_ffn1(nc, io, "p7", X3, bw_r, gwT, atall, btall, bb_r, OUT, OUTp)
    return nc


_NC_CACHE = {}


def _get_nc(which):
    if which not in _NC_CACHE:
        _NC_CACHE[which] = {1: build_launch1, 2: build_launch2,
                            3: build_launch3, 9: build_fused}[which]()
    return _NC_CACHE[which]


def _prep_all(inputs):
    cst = _consts()
    cosT, sin2 = _rope_tables()
    x = np.asarray(inputs["L0_attn_x"])
    xT = [np.ascontiguousarray(x[b].T) for b in range(B)]
    wq_all = _wr_blocks(np.ascontiguousarray(np.asarray(inputs["L0_attn_wq"]).T))
    wk_all = _wr_blocks(np.ascontiguousarray(np.asarray(inputs["L0_attn_wk"]).T))
    wvT = np.ascontiguousarray(np.asarray(inputs["L0_attn_wv"]).T)
    wo_r = _wr_blocks(np.ascontiguousarray(np.asarray(inputs["L0_attn_wo"]).T))
    qw_r = np.ascontiguousarray(np.asarray(
        inputs["L0_ffn_quantized_weight"]).T.reshape(
            32, 128, 32, 128).transpose(2, 0, 1, 3))
    qb_r = np.ascontiguousarray(np.asarray(
        inputs["L0_ffn_quantized_bias"]).reshape(32, 128).T)
    wq1_r = np.ascontiguousarray(np.asarray(
        inputs["L1_attn_wq"]).reshape(32, 128, 32, 128).transpose(2, 0, 1, 3))
    wo1_r = _wr_blocks(np.ascontiguousarray(np.asarray(inputs["L1_attn_wo"])))
    bw_r = _wr_blocks(np.ascontiguousarray(
        np.asarray(inputs["L1_ffn_base_w"]).T))
    in_maps = []
    for c in range(NCORE):
        b, j = c // 4, c % 4
        tok = _tok_idx(j)
        in_maps.append({
            "xTb": xT[b],
            "wq_r": np.ascontiguousarray(wq_all[j * 8:(j + 1) * 8]),
            "wk_r": np.ascontiguousarray(wk_all[j * 8:(j + 1) * 8]),
            "wv_r": np.ascontiguousarray(wvT[:, j * 1024:(j + 1) * 1024]),
            "ropec": cosT, "ropes2": sin2,
            "tri": cst["tri"], "ones_col": cst["ones_col"],
            "ones_row": cst["ones_row_f32"], "iota": cst["iota"],
            "ident": cst["ident"], "exp8": cst["exp8"],
            "wo_r": wo_r,
            "xres": np.ascontiguousarray(x[b][tok, :].T),
            "qw_r": qw_r, "qb_r": qb_r,
            "ws": np.asarray(inputs["L0_ffn_weight_scale"],
                             dtype=np.float32).reshape(1, 1),
            "wzp": np.asarray(inputs["L0_ffn_weight_zero_point"],
                              dtype=np.int32).reshape(1, 1),
            "bs": np.asarray(inputs["L0_ffn_bias_scale"],
                             dtype=np.float32).reshape(1, 1),
            "bzp": np.asarray(inputs["L0_ffn_bias_zero_point"],
                              dtype=np.int32).reshape(1, 1),
            "wq1_r": wq1_r,
            "wk1": np.ascontiguousarray(np.asarray(inputs["L1_attn_wk"])),
            "wv1": np.ascontiguousarray(np.asarray(inputs["L1_attn_wv"])),
            "tthr": _tthr(j),
            "wo1_r": wo1_r, "bw_r": bw_r,
            "gwT": np.ascontiguousarray(np.asarray(inputs["L1_ffn_gate_w"]).T),
            "atall": np.ascontiguousarray(np.asarray(
                inputs["L1_ffn_lora_A"]).transpose(2, 0, 1).reshape(C, 128)),
            "btall": np.ascontiguousarray(np.asarray(
                inputs["L1_ffn_lora_B"]).transpose(0, 2, 1).reshape(128, C)),
            "bb_r": np.ascontiguousarray(np.asarray(
                inputs["L1_ffn_base_b"]).reshape(32, 128).T),
            "myj": np.array([[j]], dtype=np.uint32),
        })
    return in_maps


def kernel(**inputs):
    in_maps = _prep_all(inputs)
    res = run_bass_kernel_spmd(_get_nc(9), in_maps,
                               core_ids=list(range(NCORE))).results
    out = np.empty((B, T, C), dtype=np.float32)
    for c in range(NCORE):
        b, j = c // 4, c % 4
        out[b, _tok_idx(j), :] = np.asarray(res[c]["OUT"]).T
    return out


def kernel_unfused(**inputs):
    cst = _consts()
    cosT, sin2 = _rope_tables()
    x = np.asarray(inputs["L0_attn_x"])

    in_maps1 = prep_launch1(inputs)
    res1 = run_bass_kernel_spmd(_get_nc(1), in_maps1,
                                core_ids=list(range(NCORE))).results

    wo_r = _wr_blocks(np.ascontiguousarray(
        np.asarray(inputs["L0_attn_wo"]).T))
    qw_r = np.ascontiguousarray(np.asarray(
        inputs["L0_ffn_quantized_weight"]).T.reshape(
            32, 128, 32, 128).transpose(2, 0, 1, 3))
    qb_r = np.ascontiguousarray(np.asarray(
        inputs["L0_ffn_quantized_bias"]).reshape(32, 128).T)
    wq1_r = np.ascontiguousarray(np.asarray(
        inputs["L1_attn_wq"]).reshape(32, 128, 32, 128).transpose(2, 0, 1, 3))
    wk1 = np.ascontiguousarray(np.asarray(inputs["L1_attn_wk"]))
    wv1 = np.ascontiguousarray(np.asarray(inputs["L1_attn_wv"]))
    in_maps2 = []
    for c in range(NCORE):
        b, j = c // 4, c % 4
        tok = _tok_idx(j)
        OT = np.concatenate(
            [np.asarray(res1[b * 4 + g]["OTh"])[:, tok] for g in range(4)],
            axis=0)
        in_maps2.append({
            "ones_col": cst["ones_col"], "ones_row": cst["ones_row_f32"],
            "OT": np.ascontiguousarray(OT),
            "wo_r": wo_r,
            "xres": np.ascontiguousarray(x[b][tok, :].T),
            "qw_r": qw_r, "qb_r": qb_r,
            "ws": np.asarray(inputs["L0_ffn_weight_scale"],
                             dtype=np.float32).reshape(1, 1),
            "wzp": np.asarray(inputs["L0_ffn_weight_zero_point"],
                              dtype=np.int32).reshape(1, 1),
            "bs": np.asarray(inputs["L0_ffn_bias_scale"],
                             dtype=np.float32).reshape(1, 1),
            "bzp": np.asarray(inputs["L0_ffn_bias_zero_point"],
                              dtype=np.int32).reshape(1, 1),
            "wq1_r": wq1_r, "wk1": wk1, "wv1": wv1,
        })
    res2 = run_bass_kernel_spmd(_get_nc(2), in_maps2,
                                core_ids=list(range(NCORE))).results

    wo1_r = _wr_blocks(np.ascontiguousarray(np.asarray(inputs["L1_attn_wo"])))
    bw_r = _wr_blocks(np.ascontiguousarray(
        np.asarray(inputs["L1_ffn_base_w"]).T))
    gwT = np.ascontiguousarray(np.asarray(inputs["L1_ffn_gate_w"]).T)
    atall = np.ascontiguousarray(np.asarray(
        inputs["L1_ffn_lora_A"]).transpose(2, 0, 1).reshape(C, 128))
    btall = np.ascontiguousarray(np.asarray(
        inputs["L1_ffn_lora_B"]).transpose(0, 2, 1).reshape(128, C))
    bb_r = np.ascontiguousarray(np.asarray(
        inputs["L1_ffn_base_b"]).reshape(32, 128).T)
    KG = []
    VG = []
    for b in range(B):
        kg = np.empty((128, T), dtype=BF)
        vg = np.empty((T, 128), dtype=BF)
        for g in range(8):
            j = g if g < 4 else 7 - g
            half = 0 if g < 4 else 1
            kt = np.asarray(res2[b * 4 + j]["KT1c"])
            vt = np.asarray(res2[b * 4 + j]["V1c"])
            kg[:, g * 512:(g + 1) * 512] = kt[:, half * 512:(half + 1) * 512]
            vg[g * 512:(g + 1) * 512, :] = vt[half * 512:(half + 1) * 512, :]
        KG.append(kg)
        VG.append(vg)
    in_maps3 = []
    for c in range(NCORE):
        b, j = c // 4, c % 4
        in_maps3.append({
            "ones_col": cst["ones_col"], "ones_row": cst["ones_row_f32"],
            "iota": cst["iota"], "ident": cst["ident"], "exp8": cst["exp8"],
            "X2": np.asarray(res2[c]["X2"]),
            "QT1": np.asarray(res2[c]["QT1"]),
            "KG1": KG[b], "VG1": VG[b],
            "tthr": _tthr(j),
            "wo1_r": wo1_r, "bw_r": bw_r, "gwT": gwT,
            "atall": atall, "btall": btall, "bb_r": bb_r,
        })
    res3 = run_bass_kernel_spmd(_get_nc(3), in_maps3,
                                core_ids=list(range(NCORE))).results

    out = np.empty((B, T, C), dtype=np.float32)
    for c in range(NCORE):
        b, j = c // 4, c % 4
        out[b, _tok_idx(j), :] = np.asarray(res3[c]["OUT"]).T
    return out


RG4 = [[0, 1, 2, 3], [4, 5, 6, 7]]


def _phase_break(nc, pool):
    nc.all_engine_barrier(sem_only=True)
    pool.reset()


USE_POOL = [True]


def build_fused(upto=11):
    nc = bass.Bass("TRN2", target_bir_lowering=False, num_devices=8)
    pool = SemPool(nc, 60)
    io = {"pool": pool if USE_POOL[0] else None, "fused": True}
    io["xTb"] = _dram(nc, "xTb", [C, T], F32, "ExternalInput")
    io["wq_r"] = _dram(nc, "wq_r", [8, 32, 128, 128], F32, "ExternalInput")
    io["wk_r"] = _dram(nc, "wk_r", [8, 32, 128, 128], F32, "ExternalInput")
    io["wv_r"] = _dram(nc, "wv_r", [C, 1024], F32, "ExternalInput")
    io["ropec"] = _dram(nc, "ropec", [128, T], F32, "ExternalInput")
    io["ropes2"] = _dram(nc, "ropes2", [128, T], F32, "ExternalInput")
    io["tri"] = _dram(nc, "tri", [4, 128, 512], BF16, "ExternalInput")
    io["ones_col"] = _dram(nc, "ones_col", [128, 1], BF16, "ExternalInput")
    io["ones_row"] = _dram(nc, "ones_row", [1, 128], F32, "ExternalInput")
    io["iota"] = _dram(nc, "iota", [128, 512], F32, "ExternalInput")
    io["ident"] = _dram(nc, "ident", [128, 128], F32, "ExternalInput")
    io["exp8"] = _dram(nc, "exp8", [8, 128], BF16, "ExternalInput")
    wo_r = _dram(nc, "wo_r", [32, 32, 128, 128], F32, "ExternalInput")
    xres = _dram(nc, "xres", [C, TPC], F32, "ExternalInput")
    qw_r = _dram(nc, "qw_r", [32, 32, 128, 128], I32, "ExternalInput")
    qb_r = _dram(nc, "qb_r", [128, 32], I32, "ExternalInput")
    ws = _dram(nc, "ws", [1, 1], F32, "ExternalInput")
    wzp = _dram(nc, "wzp", [1, 1], I32, "ExternalInput")
    bs = _dram(nc, "bs", [1, 1], F32, "ExternalInput")
    bzp = _dram(nc, "bzp", [1, 1], I32, "ExternalInput")
    wq1_r = _dram(nc, "wq1_r", [32, 32, 128, 128], F32, "ExternalInput")
    wk1 = _dram(nc, "wk1", [C, 128], F32, "ExternalInput")
    wv1 = _dram(nc, "wv1", [C, 128], F32, "ExternalInput")
    tthr = _dram(nc, "tthr", [128, 48], F32, "ExternalInput")
    wo1_r = _dram(nc, "wo1_r", [32, 32, 128, 128], F32, "ExternalInput")
    bw_r = _dram(nc, "bw_r", [32, 32, 128, 128], F32, "ExternalInput")
    gwT = _dram(nc, "gwT", [C, 8], F32, "ExternalInput")
    atall = _dram(nc, "atall", [C, 128], F32, "ExternalInput")
    btall = _dram(nc, "btall", [128, C], F32, "ExternalInput")
    bb_r = _dram(nc, "bb_r", [128, 32], F32, "ExternalInput")
    myj = _dram(nc, "myj", [1, 1], mybir.dt.uint32, "ExternalInput")
    io["xbd"] = _dram(nc, "xbd", [C, T], BF16, "Internal")
    io["QT0"] = _dram(nc, "QT0", [8, 128, T], BF16, "Internal")
    io["KT0"] = _dram(nc, "KT0", [8, 128, T], BF16, "Internal")
    io["V0"] = _dram(nc, "V0", [8, T, 128], BF16, "Internal")
    OTC = _dram(nc, "OTC", [4, 1024, TPC], BF16, "Internal")
    io["OTh"] = OTC
    OTG = _dram(nc, "OTG", [4, 4, 1024, TPC], BF16, "Internal")
    X1 = _dram(nc, "X1", [C, TPC], F32, "Internal")
    X1p = _dram(nc, "X1p", [C, TPC], F32, "Internal")
    X2 = _dram(nc, "X2", [C, TPC], F32, "Internal")
    X2p = _dram(nc, "X2p", [C, TPC], F32, "Internal")
    QT1 = _dram(nc, "QT1", [C, TPC], BF16, "Internal")
    KT1c = _dram(nc, "KT1c", [128, TPC], BF16, "Internal")
    V1c = _dram(nc, "V1c", [TPC, 128], BF16, "Internal")
    KGg = _dram(nc, "KGg", [4, 128, TPC], BF16, "Internal")
    VGg = _dram(nc, "VGg", [4, TPC, 128], BF16, "Internal")
    OT1 = _dram(nc, "OT1", [C, TPC], BF16, "Internal")
    X3 = _dram(nc, "X3", [C, TPC], F32, "Internal")
    X3p = _dram(nc, "X3p", [C, TPC], F32, "Internal")
    OUTp = _dram(nc, "OUTp", [C, TPC], F32, "Internal")
    OUT = _dram(nc, "OUT", [C, TPC], F32, "ExternalOutput")
    OTme = _dram(nc, "OTme", [C, TPC], BF16, "Internal")

    emit_p0(nc, io)
    if upto <= 1:
        return nc
    _phase_break(nc, pool)
    emit_p0v(nc, io)
    if upto <= 2:
        return nc
    _phase_break(nc, pool)
    if upto == 31:
        io["fused"] = False
        io["OTh"] = _dram(nc, "OThf", [1024, T], BF16, "Internal")
        emit_p1(nc, io)
        io["fused"] = True
        io["OTh"] = OTC
        return nc
    emit_p1(nc, io)
    if upto <= 3:
        return nc
    _phase_break(nc, pool)

    from concourse.bass import ts as _ts
    ccs = pool.take()
    cb = pool.acc.get(ccs.num, 0)
    pool.acc[ccs.num] = cb + 65
    with nc.Block() as blk:
        @blk.gpsimd
        def _(gp):
            gp.collective_compute(
                "AllGather", ALU.bypass, replica_groups=RG4,
                ins=[OTC.rearrange("a b c -> (a b) c")],
                outs=[OTG.rearrange("r a b c -> (r a b) c")]).then_inc(ccs, 1)
            gp.wait_ge(ccs, cb + 1)
            pid = nc.partition_id(engines=[mybir.EngineType.Pool])
            jv = pid % 4
            for pc in range(4):
                gp.dma_start(out=OTme[pc * 1024:(pc + 1) * 1024, :],
                             in_=OTG[pc, _ts(jv, 1), :, :]).then_inc(ccs, 16)
            gp.wait_ge(ccs, cb + 1 + 64)
    if upto <= 4:
        return nc
    _phase_break(nc, pool)

    emit_woln(nc, io, "p2", OTme, wo_r, xres, X1, X1p)
    if upto <= 5:
        return nc
    _phase_break(nc, pool)
    emit_ffn0(nc, io, "p3", X1, qw_r, qb_r, ws, wzp, bs, bzp, X2, X2p)
    if upto <= 6:
        return nc
    _phase_break(nc, pool)
    emit_p4(nc, io, "p4", X2, wq1_r, wk1, wv1, QT1, KT1c, V1c)
    if upto <= 7:
        return nc
    _phase_break(nc, pool)

    ccs2 = pool.take()
    cb2 = pool.acc.get(ccs2.num, 0)
    pool.acc[ccs2.num] = cb2 + 2
    with nc.Block() as blk:
        @blk.gpsimd
        def _(gp):
            gp.collective_compute(
                "AllGather", ALU.bypass, replica_groups=RG4,
                ins=[KT1c[:, :]],
                outs=[KGg.rearrange("a b c -> (a b) c")]).then_inc(ccs2, 1)
            gp.collective_compute(
                "AllGather", ALU.bypass, replica_groups=RG4,
                ins=[V1c[:, :]],
                outs=[VGg.rearrange("a b c -> (a b) c")]).then_inc(ccs2, 1)
            gp.wait_ge(ccs2, cb2 + 2)
    if upto <= 8:
        return nc
    _phase_break(nc, pool)

    emit_p5(nc, io, "p5", QT1, KGg, VGg, tthr, OT1)
    if upto <= 9:
        return nc
    _phase_break(nc, pool)
    emit_woln(nc, io, "p6", OT1, wo1_r, X2, X3, X3p)
    if upto <= 10:
        return nc
    _phase_break(nc, pool)
    emit_ffn1(nc, io, "p7", X3, bw_r, gwT, atall, btall, bb_r, OUT, OUTp)
    return nc


# version 06
# speedup vs baseline: 1.6662x; 1.5178x over previous
import math
from contextlib import ExitStack

import numpy as np
import ml_dtypes

import concourse.bass as bass
import concourse.mybir as mybir
from concourse.bass_utils import run_bass_kernel_spmd

F32 = mybir.dt.float32
BF16 = mybir.dt.bfloat16
I32 = mybir.dt.int32
AF = mybir.ActivationFunctionType
ALU = mybir.AluOpType
AX = mybir.AxisListType.X
BF = ml_dtypes.bfloat16

B, T, C, NH, HD = 2, 4096, 4096, 32, 128
NCORE = 8
TPC = 1024
NCH = C // 128
SOFT_B = 20.0
SCALE = 1.0 / math.sqrt(HD)
EPS = 1e-5


def _tok_idx(j):
    a0 = np.arange(j * 512, (j + 1) * 512)
    a1 = np.arange((7 - j) * 512, (8 - j) * 512)
    return np.concatenate([a0, a1])


def _wr_blocks(wT):
    return np.ascontiguousarray(
        wT.reshape(32, 128, 32, 128).transpose(2, 0, 1, 3))


def _rope_tables():
    pos = np.arange(T, dtype=np.float32)
    thetas = (1.0 / 10000.0 ** (np.arange(0, HD, 2, dtype=np.float32) / HD))
    pt = np.outer(pos, thetas)
    ptc = np.concatenate([pt, pt], axis=1)
    cosT = np.ascontiguousarray(np.cos(ptc).T)
    sinT = np.sin(ptc).T
    sign = np.where(np.arange(HD) < HD // 2, -1.0, 1.0).astype(np.float32)
    sin2 = np.ascontiguousarray(sinT * sign[:, None])
    return cosT.astype(np.float32), sin2.astype(np.float32)


def _consts():
    iota = np.broadcast_to(np.arange(512, dtype=np.float32), (128, 512)).copy()
    ident = np.eye(128, dtype=np.float32)
    ones_col = np.ones((128, 1), dtype=BF)
    ones_row_f32 = np.ones((1, 128), dtype=np.float32)
    f = np.arange(512)[None, :]
    p = np.arange(128)[:, None]
    tri = np.stack([(f >= 128 * i + p) for i in range(4)]).astype(BF)
    exp8 = np.zeros((8, 128), dtype=BF)
    for e in range(8):
        exp8[e, e * 16:(e + 1) * 16] = 1.0
    return dict(iota=iota, ident=ident, ones_col=ones_col,
                ones_row_f32=ones_row_f32, tri=tri, exp8=exp8)


def _tthr(j):
    out = np.zeros((128, 48), dtype=np.float32)
    p = np.arange(128, dtype=np.float32)
    col = 0
    for a, pa in ((0, 16), (1, 32)):
        gq = j if a == 0 else 7 - j
        for b in range(pa):
            out[:, col] = b * 128 + p - gq * 512
            col += 1
    return out



class SemPool:

    def __init__(self, nc, n=60, base=192):
        self.sems = [nc.alloc_semaphore(f"pool{i}", num=base + i)
                     for i in range(n)]
        self.acc = {}
        self.idx = 0

    def take(self):
        s = self.sems[self.idx]
        self.idx += 1
        return s

    def reset(self):
        self.idx = 0


class P:

    def __init__(self, nc, name, pool=None):
        self.nc = nc
        self.name = name
        self.pool = pool
        self.es = ExitStack()
        self._sems = {}
        self._base = {}

    def sem(self, n):
        if n not in self._sems:
            if self.pool is not None:
                h = self.pool.take()
                self._base[h.num] = self.pool.acc.get(h.num, 0)
            else:
                h = self.nc.alloc_semaphore(f"{self.name}_{n}")
                self._base[h.num] = 0
            self._sems[n] = h
        return self._sems[n]

    def w(self, eng, sem, v):
        eng.wait_ge(sem, self._base[sem.num] + v)

    def a(self, sem, k):
        if self.pool is not None:
            self.pool.acc[sem.num] = self.pool.acc.get(sem.num, 0) + k

    def sb(self, n, shape, dtype):
        return self.es.enter_context(
            self.nc.sbuf_tensor(f"{self.name}_{n}", shape, dtype))

    def ps(self, n, shape, dtype=F32):
        return self.es.enter_context(
            self.nc.psum_tensor(f"{self.name}_{n}", shape, dtype))

    def close(self):
        self.es.close()



def emit_p0(nc, io):
    ph = P(nc, "p0", io.get("pool"))
    xTb, wq_r, wk_r = io["xTb"], io["wq_r"], io["wk_r"]
    ropec, ropes2 = io["ropec"], io["ropes2"]
    QT0, KT0, xbd = io["QT0"], io["KT0"], io["xbd"]

    land = [ph.sb(f"land{i}", [128, 8, 512], F32) for i in range(2)]
    xa = [ph.sb(f"xa{i}", [128, 8, 512], BF16) for i in range(2)]
    landw = ph.sb("landw", [128, 16, 128], F32)
    wqb = [ph.sb(f"wqb{i}", [128, 8192], BF16) for i in range(2)]
    wkb = [ph.sb(f"wkb{i}", [128, 8192], BF16) for i in range(2)]
    xb = [ph.sb(f"xb{i}", [128, 32, 512], BF16) for i in range(2)]
    scc = [ph.sb(f"scc{i}", [128, 512], F32) for i in range(2)]
    ssn = [ph.sb(f"ssn{i}", [128, 512], F32) for i in range(2)]
    qs = [ph.sb(f"qs{i}", [128, 512], F32) for i in range(2)]
    t1 = ph.sb("t1", [128, 512], F32)
    t2 = ph.sb("t2", [128, 512], F32)
    stg = [ph.sb(f"stg{i}", [128, 512], BF16) for i in range(4)]

    bqk = [ph.ps(f"bqk{i}", [128, 512]) for i in range(4)]

    a_ld = [ph.sem("a_ld0"), ph.sem("a_ld1")]
    a_st = [ph.sem("a_st0"), ph.sem("a_st1")]
    w_ld = ph.sem("w_ld")
    w_cv = ph.sem("w_cv")
    r_ld = [ph.sem("r_ld0"), ph.sem("r_ld1")]
    x_ld = [ph.sem("x_ld0"), ph.sem("x_ld1")]
    mm = ph.sem("mm")
    rp = ph.sem("rp")
    rpd = ph.sem("rpd")
    ev = ph.sem("ev")
    outq = [ph.sem("outq0"), ph.sem("outq1")]
    acv = ph.sem("acv")
    ch_ve = ph.sem("ch_ve")

    NSUB = 32
    NTRG = io.get("p0_ntrg", 32)

    with nc.Block("p0") as blk:

        SKIP_A = io.get("p0_skip_a", False)
        NO_ROPE = io.get("p0_no_rope", False)
        NO_STORE = io.get("p0_no_store", False)
        NO_ACT = io.get("p0_no_act", False)

        @blk.sync
        def _(sy):
            if SKIP_A:
                return
            for sub in range(NSUB):
                tr, q = sub // 4, sub % 4
                if sub >= 2:
                    ph.w(sy, acv, sub - 1)
                sy.dma_start(
                    out=land[sub % 2][:, :, :],
                    in_=xTb[q * 1024:(q + 1) * 1024,
                            tr * 512:(tr + 1) * 512].rearrange(
                        "(i p) c -> p i c", p=128)).then_inc(a_ld[sub % 2], 16); ph.a(a_ld[sub % 2], 16)

        @blk.scalar
        def _(sc):
            def load_w(hp):
                for w, src in enumerate((wq_r, wk_r)):
                    for pc in range(4):
                        idx = hp * 8 + w * 4 + pc
                        if idx > 0:
                            ph.w(sc, w_cv, idx)
                        h, ir = pc // 2, (pc % 2) * 16
                        sc.dma_start(
                            out=landw[:, :, :],
                            in_=src[2 * hp + h, ir:ir + 16, :, :].rearrange(
                                "i p c -> p i c")).then_inc(w_ld, 16); ph.a(w_ld, 16)

            def load_xb(trg):
                hp, tr = trg // 8, trg % 8
                if not SKIP_A:
                    ph.w(sc, a_st[0], 16 * (2 * tr + 2))
                    ph.w(sc, a_st[1], 16 * (2 * tr + 2))
                if trg >= 2:
                    ph.w(sc, mm, trg - 1)
                sc.dma_start(
                    out=xb[trg % 2][:, :, :],
                    in_=xbd[:, tr * 512:(tr + 1) * 512].rearrange(
                        "(i p) c -> p i c", p=128)).then_inc(x_ld[trg % 2], 16); ph.a(x_ld[trg % 2], 16)

            load_w(0)
            load_xb(0)
            load_xb(1)
            if not NO_ROPE:
                sc.dma_start(out=scc[0][:, :],
                             in_=ropec[:, 0:512]).then_inc(r_ld[0], 16); ph.a(r_ld[0], 16)
                sc.dma_start(out=ssn[0][:, :],
                             in_=ropes2[:, 0:512]).then_inc(r_ld[0], 16); ph.a(r_ld[0], 16)
                sc.dma_start(out=scc[1][:, :],
                             in_=ropec[:, 512:1024]).then_inc(r_ld[1], 16); ph.a(r_ld[1], 16)
                sc.dma_start(out=ssn[1][:, :],
                             in_=ropes2[:, 512:1024]).then_inc(r_ld[1], 16); ph.a(r_ld[1], 16)
            for trg in range(NTRG):
                hp, tr = trg // 8, trg % 8
                if trg < NTRG - 2:
                    load_xb(trg + 2)
                if tr == 4 and hp < 3 and (hp + 1) * 8 < NTRG:
                    load_w(hp + 1)
                if not NO_ROPE and trg + 2 < NTRG:
                    tr2 = (trg + 2) % 8
                    ph.w(sc, ev, 4 * (trg + 1))
                    sc.dma_start(out=scc[trg % 2][:, :],
                                 in_=ropec[:, tr2 * 512:(tr2 + 1) * 512]
                                 ).then_inc(r_ld[trg % 2], 16); ph.a(r_ld[trg % 2], 16)
                    sc.dma_start(out=ssn[trg % 2][:, :],
                                 in_=ropes2[:, tr2 * 512:(tr2 + 1) * 512]
                                 ).then_inc(r_ld[trg % 2], 16); ph.a(r_ld[trg % 2], 16)

        @blk.gpsimd
        def _(gp):
            def p0a(sub):
                ph.w(gp, a_ld[sub % 2], 16 * (sub // 2 + 1))
                if sub >= 2:
                    ph.w(gp, a_st[sub % 2], 16 * (sub // 2))
                tr, q = sub // 4, sub % 4
                gp.tensor_copy(out=xa[sub % 2][:, :, :],
                               in_=land[sub % 2][:, :, :]).then_inc(acv, 1); ph.a(acv, 1)
                ph.w(gp, acv, sub + 1)
                gp.dma_start(
                    out=xbd[q * 1024:(q + 1) * 1024,
                            tr * 512:(tr + 1) * 512].rearrange(
                        "(i p) c -> p i c", p=128),
                    in_=xa[sub % 2][:, :, :]).then_inc(a_st[sub % 2], 16); ph.a(a_st[sub % 2], 16)

            if not SKIP_A:
                for sub in range(12):
                    p0a(sub)
            for trg in range(NTRG):
                if not SKIP_A and trg < 5:
                    for sub in range(12 + 4 * trg, 16 + 4 * trg):
                        p0a(sub)
                hp, tr = trg // 8, trg % 8
                if NO_STORE:
                    continue
                for bk in range(4):
                    head = 2 * hp + (bk % 2)
                    dst = QT0 if bk < 2 else KT0
                    ph.w(gp, ev, trg * 4 + bk + 1)
                    gp.dma_start(out=dst[head, :, tr * 512:(tr + 1) * 512],
                                 in_=stg[bk][:, :]).then_inc(outq[trg % 2], 16); ph.a(outq[trg % 2], 16)
            if not NO_STORE:
                ph.w(gp, outq[0], 64 * ((NTRG + 1) // 2))
                ph.w(gp, outq[1], 64 * (NTRG // 2))
            ph.w(gp, ev, 4 * NTRG)
            if not SKIP_A:
                ph.w(gp, a_st[0], 16 * 16)
                ph.w(gp, a_st[1], 16 * 16)

        @blk.vector
        def _(ve):
            def conv_w(hp):
                for pc in range(8):
                    idx = hp * 8 + pc
                    ph.w(ve, w_ld, 16 * (idx + 1))
                    if hp >= 2:
                        ph.w(ve, mm, 8 * (hp - 1))
                    dst = wqb[hp % 2] if pc < 4 else wkb[hp % 2]
                    off = (pc % 4) * 2048
                    ve.tensor_copy(
                        out=dst[:, off:off + 2048].rearrange(
                            "p (i c) -> p i c", i=16),
                        in_=landw[:, :, :]).then_inc(w_cv, 1); ph.a(w_cv, 1)

            conv_w(0)
            for trg in range(NTRG):
                hp, tr = trg // 8, trg % 8
                if tr == 4 and hp < 3 and (hp + 1) * 8 < NTRG:
                    conv_w(hp + 1)
                ph.w(ve, mm, trg + 1)
                if not NO_ROPE:
                    ph.w(ve, r_ld[trg % 2], 32 * (trg // 2 + 1))
                if not NO_STORE and trg >= 1:
                    ph.w(ve, outq[(trg - 1) % 2], 64 * ((trg - 1) // 2 + 1))
                for bk in range(4):
                    g = trg * 4 + bk
                    if NO_ROPE:
                        ve.tensor_copy(out=stg[bk][:, :],
                                       in_=bqk[bk][:, :]).then_inc(ev, 1); ph.a(ev, 1)
                        continue
                    if g >= 1:
                        ph.w(ve, ev, g)
                    if g >= 2:
                        ph.w(ve, rpd, g - 1)
                    ve.tensor_copy(out=qs[bk % 2][0:64, :],
                                   in_=bqk[bk][64:128, :])
                    ve.tensor_copy(out=qs[bk % 2][64:128, :],
                                   in_=bqk[bk][0:64, :]).then_inc(rp, 1); ph.a(rp, 1)
                    ve.tensor_tensor(out=t1[:, :], in0=bqk[bk][:, :],
                                     in1=scc[trg % 2][:, :],
                                     op=ALU.mult).then_inc(ch_ve, 1); ph.a(ch_ve, 1)
                    ph.w(ve, rp, g + 1)
                    ve.tensor_tensor(out=t2[:, :], in0=qs[bk % 2][:, :],
                                     in1=ssn[trg % 2][:, :],
                                     op=ALU.mult).then_inc(rpd, 1); ph.a(rpd, 1)
                    ph.w(ve, ch_ve, g + 1)
                    ph.w(ve, rpd, g + 1)
                    ve.tensor_tensor(out=stg[bk][:, :], in0=t1[:, :], in1=t2[:, :],
                                     op=ALU.add).then_inc(ev, 1); ph.a(ev, 1)

        @blk.tensor
        def _(te):
            for trg in range(NTRG):
                hp, tr = trg // 8, trg % 8
                ph.w(te, w_cv, 8 * hp + 8)
                ph.w(te, x_ld[trg % 2], 16 * (trg // 2 + 1))
                if trg >= 1:
                    ph.w(te, ev, 4 * trg)
                for i in range(32):
                    st = (i == 0)
                    sp = (i == 31)
                    for hh in range(2):
                        te.matmul(out=bqk[hh][:, :],
                                  lhsT=wqb[hp % 2][:, (hh * 32 + i) * 128:
                                                   (hh * 32 + i) * 128 + 128],
                                  rhs=xb[trg % 2][:, i, :], start=st, stop=sp)
                        ins = te.matmul(out=bqk[2 + hh][:, :],
                                        lhsT=wkb[hp % 2][:, (hh * 32 + i) * 128:
                                                         (hh * 32 + i) * 128 + 128],
                                        rhs=xb[trg % 2][:, i, :], start=st, stop=sp)
                    if i == 31:
                        ins.then_inc(mm, 1); ph.a(mm, 1)

    ph.close()



def emit_p0v(nc, io):
    ph = P(nc, "p0v", io.get("pool"))
    wv_r, V0, xbd = io["wv_r"], io["V0"], io["xbd"]

    landw = ph.sb("landw", [128, 4, 512], F32)
    swv = ph.sb("swv", [128, 16384], BF16)
    xb = [ph.sb(f"xb{i}", [128, 32, 512], BF16) for i in range(2)]
    vst = [ph.sb(f"vst{i}", [128, 512], BF16) for i in range(4)]

    bv = [ph.ps(f"bv{i}", [128, 512]) for i in range(4)]

    w_ld = ph.sem("w_ld")
    w_cv = ph.sem("w_cv")
    x_ld = [ph.sem("x_ld0"), ph.sem("x_ld1")]
    mm = ph.sem("mm")
    ev = ph.sem("ev")
    outq = [ph.sem("outq0"), ph.sem("outq1")]

    with nc.Block("p0v") as blk:

        @blk.scalar
        def _(sc):
            for op in range(2):
                for pc in range(8):
                    idx = op * 8 + pc
                    if idx > 0:
                        ph.w(sc, w_cv, idx)
                    if op == 1 and pc == 0:
                        ph.w(sc, mm, 8)
                    sc.dma_start(
                        out=landw[:, :, :],
                        in_=wv_r[pc * 512:(pc + 1) * 512,
                                 op * 512:(op + 1) * 512].rearrange(
                            "(i p) c -> p i c", p=128)).then_inc(w_ld, 16); ph.a(w_ld, 16)

        @blk.vector
        def _(ve):
            def conv_v(op):
                for pc in range(8):
                    idx = op * 8 + pc
                    ph.w(ve, w_ld, 16 * (idx + 1))
                    if op == 1 and pc == 0:
                        ph.w(ve, mm, 8)
                    ve.tensor_copy(
                        out=swv[:, pc * 2048:(pc + 1) * 2048].rearrange(
                            "p (i c) -> p i c", i=4),
                        in_=landw[:, :, :]).then_inc(w_cv, 1); ph.a(w_cv, 1)

            conv_v(0)
            for trg in range(16):
                ph.w(ve, mm, trg + 1)
                if trg >= 1:
                    ph.w(ve, outq[(trg - 1) % 2], 256 * ((trg - 1) // 2 + 1))
                for tt in range(4):
                    ve.tensor_copy(out=vst[tt][:, :],
                                   in_=bv[tt][:, :]).then_inc(ev, 1); ph.a(ev, 1)
                if trg == 7:
                    conv_v(1)

        @blk.sync
        def _(sy):
            for trg in range(16):
                tr = trg % 8
                if trg >= 2:
                    ph.w(sy, mm, trg - 1)
                sy.dma_start(
                    out=xb[trg % 2][:, :, :],
                    in_=xbd[:, tr * 512:(tr + 1) * 512].rearrange(
                        "(i p) c -> p i c", p=128)).then_inc(x_ld[trg % 2], 16); ph.a(x_ld[trg % 2], 16)

        @blk.tensor
        def _(te):
            for trg in range(16):
                op, tr = trg // 8, trg % 8
                ph.w(te, w_cv, 8 * (op + 1))
                ph.w(te, x_ld[trg % 2], 16 * (trg // 2 + 1))
                if trg >= 1:
                    ph.w(te, ev, 4 * trg)
                for i in range(32):
                    st = (i == 0)
                    sp = (i == 31)
                    for tt in range(4):
                        ins = te.matmul(out=bv[tt][:, :],
                                        lhsT=xb[trg % 2][:, i, tt * 128:(tt + 1) * 128],
                                        rhs=swv[:, i * 512:(i + 1) * 512],
                                        start=st, stop=sp)
                    if i == 31:
                        ins.then_inc(mm, 1); ph.a(mm, 1)

        @blk.gpsimd
        def _(gp):
            for trg in range(16):
                op, tr = trg // 8, trg % 8
                for tt in range(4):
                    ph.w(gp, ev, trg * 4 + tt + 1)
                    for hh in range(4):
                        gp.dma_start(
                            out=V0[op * 4 + hh,
                                   tr * 512 + tt * 128:tr * 512 + (tt + 1) * 128, :],
                            in_=vst[tt][:, hh * 128:(hh + 1) * 128]
                            ).then_inc(outq[trg % 2], 16); ph.a(outq[trg % 2], 16)
            ph.w(gp, outq[0], 256 * 8)
            ph.w(gp, outq[1], 256 * 8)

    ph.close()



def emit_p1(nc, io):
    ph = P(nc, "p1", io.get("pool"))
    QT0, KT0, V0, OTh = io["QT0"], io["KT0"], io["V0"], io["OTh"]
    tri, ones_col, ones_row = io["tri"], io["ones_col"], io["ones_row"]
    fused = io.get("fused", False)

    sk = [ph.sb(f"sk{i}", [128, T], BF16) for i in range(2)]
    sv = [ph.sb(f"sv{i}", [128, 32, 128], BF16) for i in range(2)]
    sq = [ph.sb(f"sq{i}", [128, 512], BF16) for i in range(2)]
    sp = [ph.sb(f"sp{i}", [128, 512], BF16) for i in range(3)]
    spm = [ph.sb(f"spm{i}", [128, 512], BF16) for i in range(3)]
    stri = ph.sb("stri", [128, 4, 512], BF16)
    sones = ph.sb("sones", [128, 1], BF16)
    sonesr = ph.sb("sonesr", [1, 128], F32)
    srcp = [ph.sb(f"srcp{i}", [1, 512], F32) for i in range(2)]
    srb = ph.sb("srb", [128, 512], F32)
    ostg = [ph.sb(f"ostg{i}", [128, 512], BF16) for i in range(2)]
    sbias = ph.sb("sbias", [128, 1], F32)

    bS = [ph.ps(f"bS{i}", [128, 512]) for i in range(3)]
    bO = [ph.ps(f"bO{i}", [128, 512]) for i in range(2)]
    bD = [ph.ps(f"bD{i}", [1, 512]) for i in range(2)]
    bRB = ph.ps("bRB", [128, 512])

    kv_ld = [ph.sem("kv_ld0"), ph.sem("kv_ld1")]
    c_ld = ph.sem("c_ld")
    q_ld = [ph.sem("q_ld0"), ph.sem("q_ld1")]
    s_mm = ph.sem("s_mm")
    s_pv = ph.sem("s_pv")
    s_dn = ph.sem("s_dn")
    s_bc = ph.sem("s_bc")
    a_ex = ph.sem("a_ex")
    v_mk = ph.sem("v_mk")
    v_rcp = ph.sem("v_rcp")
    v_ev = ph.sem("v_ev")
    p_st = [ph.sem("p_st0"), ph.sem("p_st1")]
    v_ini = ph.sem("v_ini")
    ch_ve = ph.sem("ch_ve")

    NHD = 8
    NQ = 8
    NG = NHD * NQ
    KM = [4 * (ha % NQ + 1) for ha in range(NG)]
    CUM = [0]
    for km in KM:
        CUM.append(CUM[-1] + km)
    G_TOT = CUM[-1]
    IT = [(ha, b) for ha in range(NG) for b in range(KM[ha])]
    DIAG = {}
    mi = 0
    for ha in range(NG):
        for b in range(KM[ha] - 4, KM[ha]):
            DIAG[CUM[ha] + b] = (mi, b - (KM[ha] - 4))
            mi += 1

    with nc.Block("p1") as blk:

        @blk.sync
        def _(sy):
            sy.dma_start(out=stri[:, :, :],
                         in_=tri[:, :, :].rearrange("t p c -> p t c")
                         ).then_inc(c_ld, 16); ph.a(c_ld, 16)
            sy.dma_start(out=sones[:, :],
                         in_=ones_col[:, :]).then_inc(c_ld, 16); ph.a(c_ld, 16)
            sy.dma_start(out=sonesr[:, :],
                         in_=ones_row[:, :]).then_inc(c_ld, 16); ph.a(c_ld, 16)
            for h in range(NHD):
                if h >= 2:
                    ph.w(sy, s_pv, CUM[NQ * (h - 1)])
                sy.dma_start(out=sk[h % 2][:, :],
                             in_=KT0[h, :, :]).then_inc(kv_ld[h % 2], 16)
                ph.a(kv_ld[h % 2], 16)
                for part in range(4):
                    sy.dma_start(
                        out=sv[h % 2][:, part * 8:(part + 1) * 8, :],
                        in_=V0[h, part * 1024:(part + 1) * 1024, :].rearrange(
                            "(kc p) d -> p kc d", p=128)
                        ).then_inc(kv_ld[h % 2], 16)
                    ph.a(kv_ld[h % 2], 16)
                for a in range(NQ):
                    ha = h * NQ + a
                    if ha >= 2:
                        ph.w(sy, s_mm, CUM[ha - 1])
                    sy.dma_start(out=sq[ha % 2][:, :],
                                 in_=QT0[h, :, a * 512:(a + 1) * 512]
                                 ).then_inc(q_ld[ha % 2], 16)
                    ph.a(q_ld[ha % 2], 16)

        @blk.tensor
        def _(te):
            def s_issue(k):
                ha, b = IT[k]
                h = ha // NQ
                if b == 0:
                    if ha % NQ == 0:
                        ph.w(te, c_ld, 48)
                        ph.w(te, kv_ld[h % 2], 80 * (h // 2 + 1))
                    ph.w(te, q_ld[ha % 2], 16 * (ha // 2 + 1))
                if k >= 3:
                    ph.w(te, a_ex, k - 2)
                te.matmul(out=bS[k % 3][:, :],
                          lhsT=sk[h % 2][:, b * 128:(b + 1) * 128],
                          rhs=sq[ha % 2][:, :], start=True, stop=True
                          ).then_inc(s_mm, 1); ph.a(s_mm, 1)

            def bcast(j):
                ph.w(te, v_rcp, j + 1)
                if j >= 1:
                    ph.w(te, ch_ve, j)
                te.matmul(out=bRB[:, :], lhsT=sonesr[:, :],
                          rhs=srcp[j % 2][:, :], start=True, stop=True
                          ).then_inc(s_bc, 1); ph.a(s_bc, 1)

            for k in range(min(3, G_TOT)):
                s_issue(k)
            for g in range(G_TOT):
                ha, b = IT[g]
                h = ha // NQ
                if g + 3 < G_TOT:
                    s_issue(g + 3)
                if b == 1 and ha >= 1:
                    bcast(ha - 1)
                if b == 0 and ha >= 2:
                    ph.w(te, v_ev, ha - 1)
                    ph.w(te, v_rcp, ha - 1)
                if g in DIAG:
                    ph.w(te, v_mk, DIAG[g][0] + 1)
                    rhs_p = spm[g % 3][:, :]
                else:
                    ph.w(te, a_ex, g + 1)
                    rhs_p = sp[g % 3][:, :]
                te.matmul(out=bO[ha % 2][:, :],
                          lhsT=sv[h % 2][:, b, :], rhs=rhs_p,
                          start=(b == 0), stop=(b == KM[ha] - 1))
                te.matmul(out=bD[ha % 2][:, :], lhsT=sones[:, :],
                          rhs=rhs_p,
                          start=(b == 0), stop=(b == KM[ha] - 1),
                          skip_group_check=True
                          ).then_inc(s_pv, 1); ph.a(s_pv, 1)
            bcast(NG - 1)

        @blk.scalar
        def _(sc):
            ph.w(sc, v_ini, 1)
            for k in range(G_TOT):
                ph.w(sc, s_mm, k + 1)
                if k >= 3:
                    ph.w(sc, s_pv, k - 2)
                sc.activation(out=sp[k % 3][:, :], in_=bS[k % 3][:, :],
                              func=AF.Exp, scale=SCALE,
                              bias=sbias[:, 0:1]).then_inc(a_ex, 1); ph.a(a_ex, 1)

        @blk.vector
        def _(ve):
            ve.memset(sbias[:, :], -SOFT_B).then_inc(v_ini, 1); ph.a(v_ini, 1)
            ph.w(ve, c_ld, 48)

            def emit_ev(j):
                ph.w(ve, s_bc, j + 1)
                if j >= 1:
                    ph.w(ve, v_ev, j)
                ve.tensor_copy(out=srb[:, :],
                               in_=bRB[:, :]).then_inc(ch_ve, 1); ph.a(ch_ve, 1)
                if j >= 2:
                    ph.w(ve, p_st[j % 2], 16 * (j // 2))
                ph.w(ve, ch_ve, j + 1)
                ve.tensor_tensor(out=ostg[j % 2][:, :], in0=bO[j % 2][:, :],
                                 in1=srb[:, :],
                                 op=ALU.mult).then_inc(v_ev, 1); ph.a(v_ev, 1)

            for ha in range(NG):
                km = KM[ha]
                for b in range(km - 4, km):
                    g = CUM[ha] + b
                    i = b - (km - 4)
                    ph.w(ve, a_ex, g + 1)
                    if g >= 3:
                        ph.w(ve, s_pv, g - 2)
                    ve.tensor_tensor(out=spm[g % 3][:, :], in0=sp[g % 3][:, :],
                                     in1=stri[:, i, :],
                                     op=ALU.mult).then_inc(v_mk, 1)
                    ph.a(v_mk, 1)
                if ha >= 1:
                    emit_ev(ha - 1)
                ph.w(ve, s_pv, CUM[ha + 1])
                ve.reciprocal(out=srcp[ha % 2][:, :],
                              in_=bD[ha % 2][0:1, :]).then_inc(v_rcp, 1)
                ph.a(v_rcp, 1)
            emit_ev(NG - 1)

        @blk.gpsimd
        def _(gp):
            for ha in range(NG):
                h, a = ha // NQ, ha % NQ
                ph.w(gp, v_ev, ha + 1)
                if fused:
                    peer = a if a < 4 else 7 - a
                    half = 0 if a < 4 else 1
                    dst = OTh[peer, h * 128:(h + 1) * 128,
                              half * 512:(half + 1) * 512]
                else:
                    dst = OTh[h * 128:(h + 1) * 128, a * 512:(a + 1) * 512]
                gp.dma_start(out=dst, in_=ostg[ha % 2][:, :]
                             ).then_inc(p_st[ha % 2], 16); ph.a(p_st[ha % 2], 16)
            ph.w(gp, p_st[0], 16 * 32)
            ph.w(gp, p_st[1], 16 * 32)

    ph.close()



def _dram(nc, name, shape, dtype, kind):
    return nc.dram_tensor(name, shape, dtype, kind=kind).ap()


def build_launch1(nphases=3):
    nc = bass.Bass("TRN2", target_bir_lowering=False)
    io = {}
    io["xTb"] = _dram(nc, "xTb", [C, T], F32, "ExternalInput")
    io["wq_r"] = _dram(nc, "wq_r", [8, 32, 128, 128], F32, "ExternalInput")
    io["wk_r"] = _dram(nc, "wk_r", [8, 32, 128, 128], F32, "ExternalInput")
    io["wv_r"] = _dram(nc, "wv_r", [C, 1024], F32, "ExternalInput")
    io["ropec"] = _dram(nc, "ropec", [128, T], F32, "ExternalInput")
    io["ropes2"] = _dram(nc, "ropes2", [128, T], F32, "ExternalInput")
    io["tri"] = _dram(nc, "tri", [4, 128, 512], BF16, "ExternalInput")
    io["ones_col"] = _dram(nc, "ones_col", [128, 1], BF16, "ExternalInput")
    io["ones_row"] = _dram(nc, "ones_row", [1, 128], F32, "ExternalInput")
    io["xbd"] = _dram(nc, "xbd", [C, T], BF16, "Internal")
    io["QT0"] = _dram(nc, "QT0", [8, 128, T], BF16, "Internal")
    io["KT0"] = _dram(nc, "KT0", [8, 128, T], BF16, "Internal")
    io["V0"] = _dram(nc, "V0", [8, T, 128], BF16, "Internal")
    io["OTh"] = _dram(nc, "OTh", [1024, T], BF16, "ExternalOutput")
    emit_p0(nc, io)
    if nphases >= 2:
        nc.all_engine_barrier(sem_only=True)
        emit_p0v(nc, io)
    if nphases >= 3:
        nc.all_engine_barrier(sem_only=True)
        emit_p1(nc, io)
    return nc


def prep_launch1(inputs):
    x = np.asarray(inputs["L0_attn_x"])
    wq = np.asarray(inputs["L0_attn_wq"])
    wk = np.asarray(inputs["L0_attn_wk"])
    wv = np.asarray(inputs["L0_attn_wv"])
    cosT, sin2 = _rope_tables()
    cst = _consts()
    xT = [np.ascontiguousarray(x[b].T) for b in range(B)]
    wq_all = _wr_blocks(np.ascontiguousarray(wq.T))
    wk_all = _wr_blocks(np.ascontiguousarray(wk.T))
    wvT = np.ascontiguousarray(wv.T)
    in_maps = []
    for c in range(NCORE):
        b, g = c // 4, c % 4
        in_maps.append({
            "xTb": xT[b],
            "wq_r": np.ascontiguousarray(wq_all[g * 8:(g + 1) * 8]),
            "wk_r": np.ascontiguousarray(wk_all[g * 8:(g + 1) * 8]),
            "wv_r": np.ascontiguousarray(wvT[:, g * 1024:(g + 1) * 1024]),
            "ropec": cosT, "ropes2": sin2,
            "tri": cst["tri"], "ones_col": cst["ones_col"],
            "ones_row": cst["ones_row_f32"],
        })
    return in_maps


def emit_woln(nc, io, name, OT, W_r, XRES, XOUT, XPRE):
    ph = P(nc, name, io.get("pool"))
    ones_col, ones_row = io["ones_col"], io["ones_row"]

    so = ph.sb("so", [128, 32, 1024], BF16)
    landw = [ph.sb(f"landw{i}", [128, 32, 128], F32) for i in range(2)]
    swb = [ph.sb(f"swb{i}", [128, 4096], BF16) for i in range(2)]
    sx = [ph.sb(f"sx{i}", [128, 1024], F32) for i in range(2)]
    st_t = [ph.sb(f"stt{i}", [128, 1024], F32) for i in range(2)]
    stb = [ph.sb(f"stb{i}", [128, 1024], BF16) for i in range(2)]
    stsq = [ph.sb(f"stsq{i}", [128, 1024], BF16) for i in range(2)]
    sonesc = ph.sb("sonesc", [128, 1], BF16)
    sonesr = ph.sb("sonesr", [1, 128], F32)
    srow = ph.sb("srow", [1, 4096], F32)
    srow2 = ph.sb("srow2", [1, 4096], F32)
    sra = ph.sb("sra", [128, 1024], F32)
    srb2 = ph.sb("srb2", [128, 1024], F32)
    land2 = [ph.sb(f"land2{i}", [128, 1024], F32) for i in range(2)]
    tmp2 = ph.sb("tmp2", [128, 1024], F32)
    stage2 = [ph.sb(f"stage2{i}", [128, 1024], F32) for i in range(2)]
    eps_t = ph.sb("eps_t", [1, 1], F32)

    yb = [ph.ps(f"yb{i}", [128, 512]) for i in range(4)]
    ssum = [ph.ps(f"ssum{i}", [1, 512]) for i in range(2)]
    ssq = [ph.ps(f"ssq{i}", [1, 512]) for i in range(2)]

    c_ld = ph.sem("c_ld")
    w_ld = [ph.sem("w_ld0"), ph.sem("w_ld1")]
    w_cv = ph.sem("w_cv")
    x_ld = [ph.sem("x_ld0"), ph.sem("x_ld1")]
    p_mm = ph.sem("p_mm")
    p_sd = ph.sem("p_sd")
    p_bc = ph.sem("p_bc")
    v_t = ph.sem("v_t")
    v_st = ph.sem("v_st")
    a_sq = ph.sem("a_sq")
    p_pre = [ph.sem("p_pre0"), ph.sem("p_pre1")]
    v_row = ph.sem("v_row")
    a_row = ph.sem("a_row")
    v_bc = ph.sem("v_bc")
    x2_ld = [ph.sem("x2_ld0"), ph.sem("x2_ld1")]
    v_p2 = ph.sem("v_p2")
    v_p3 = ph.sem("v_p3")
    p_out = [ph.sem("p_out0"), ph.sem("p_out1")]
    ini = ph.sem("ini")
    cch = ph.sem("cch")

    NOB = 32

    with nc.Block(name) as blk:

        @blk.sync
        def _(sy):
            for pc in range(4):
                sy.dma_start(
                    out=so[:, pc * 8:(pc + 1) * 8, :],
                    in_=OT[pc * 1024:(pc + 1) * 1024, :].rearrange(
                        "(cc p) t -> p cc t", p=128)).then_inc(c_ld, 16); ph.a(c_ld, 16)
            sy.dma_start(out=sonesc[:, :], in_=ones_col[:, :]).then_inc(c_ld, 16); ph.a(c_ld, 16)
            sy.dma_start(out=sonesr[:, :], in_=ones_row[:, :]).then_inc(c_ld, 16); ph.a(c_ld, 16)
            for ob in range(NOB):
                if ob >= 2:
                    ph.w(sy, w_cv, ob - 1)
                for hh in range(2):
                    sy.dma_start(
                        out=landw[ob % 2][:, hh * 16:(hh + 1) * 16, :],
                        in_=W_r[ob, hh * 16:(hh + 1) * 16, :, :].rearrange(
                            "i p c -> p i c")).then_inc(w_ld[ob % 2], 16); ph.a(w_ld[ob % 2], 16)
                if ob >= 2:
                    ph.w(sy, v_t, 2 * (ob - 1))
                sy.dma_start(out=sx[ob % 2][:, :],
                             in_=XRES[ob * 128:(ob + 1) * 128, :]
                             ).then_inc(x_ld[ob % 2], 16); ph.a(x_ld[ob % 2], 16)
            for ch in range(NOB):
                ph.w(sy, p_pre[ch % 2], 16 * (ch // 2 + 1))
                if ch >= 2:
                    ph.w(sy, v_p2, ch - 1)
                sy.dma_start(out=land2[ch % 2][:, :],
                             in_=XPRE[ch * 128:(ch + 1) * 128, :]
                             ).then_inc(x2_ld[ch % 2], 16); ph.a(x2_ld[ch % 2], 16)

        @blk.vector
        def _(ve):
            ve.memset(eps_t[:, :], EPS).then_inc(ini, 1); ph.a(ini, 1)
            for ob in range(NOB):
                ph.w(ve, w_ld[ob % 2], 32 * (ob // 2 + 1))
                if ob >= 2:
                    ph.w(ve, p_mm, 2 * (ob - 1))
                ve.tensor_copy(
                    out=swb[ob % 2][:, :].rearrange("p (i c) -> p i c", i=32),
                    in_=landw[ob % 2][:, :, :]).then_inc(w_cv, 1); ph.a(w_cv, 1)
                for half in range(2):
                    hs = slice(half * 512, (half + 1) * 512)
                    ph.w(ve, p_mm, 2 * ob + half + 1)
                    ph.w(ve, x_ld[ob % 2], 16 * (ob // 2 + 1))
                    if ob >= 2:
                        ph.w(ve, p_pre[ob % 2], 16 * (ob // 2))
                        ph.w(ve, a_sq, ob - 1)
                        ph.w(ve, p_sd, ob - 1)
                    ve.tensor_tensor(out=st_t[ob % 2][:, hs],
                                     in0=yb[(ob % 2) * 2 + half][:, :],
                                     in1=sx[ob % 2][:, hs],
                                     op=ALU.add).then_inc(v_t, 1); ph.a(v_t, 1)
                    ph.w(ve, v_t, 2 * ob + half + 1)
                    ins = ve.tensor_copy(out=stb[ob % 2][:, hs],
                                         in_=st_t[ob % 2][:, hs])
                    if half == 1:
                        ins.then_inc(v_st, 1); ph.a(v_st, 1)
            ph.w(ve, p_sd, NOB)
            ve.tensor_copy(out=srow[:, 0:512], in_=ssum[0][:, :]).then_inc(cch, 1); ph.a(cch, 1)
            ve.tensor_copy(out=srow[:, 512:1024], in_=ssum[1][:, :]).then_inc(cch, 1); ph.a(cch, 1)
            ve.tensor_copy(out=srow[:, 1024:1536], in_=ssq[0][:, :]).then_inc(cch, 1); ph.a(cch, 1)
            ve.tensor_copy(out=srow[:, 1536:2048], in_=ssq[1][:, :]).then_inc(cch, 1); ph.a(cch, 1)
            ph.w(ve, cch, 4)
            ve.tensor_scalar(out=srow2[:, 0:1024], in0=srow[:, 0:1024],
                             scalar1=1.0 / C, scalar2=None,
                             op0=ALU.mult).then_inc(cch, 1); ph.a(cch, 1)
            ve.tensor_scalar(out=srow2[:, 1024:2048], in0=srow[:, 1024:2048],
                             scalar1=1.0 / C, scalar2=None,
                             op0=ALU.mult).then_inc(cch, 1); ph.a(cch, 1)
            ph.w(ve, cch, 6)
            ve.tensor_tensor(out=srow2[:, 2048:3072], in0=srow2[:, 0:1024],
                             in1=srow2[:, 0:1024], op=ALU.mult).then_inc(cch, 1); ph.a(cch, 1)
            ph.w(ve, cch, 7)
            ve.tensor_tensor(out=srow2[:, 2048:3072], in0=srow2[:, 1024:2048],
                             in1=srow2[:, 2048:3072],
                             op=ALU.subtract).then_inc(v_row, 1); ph.a(v_row, 1)
            ph.w(ve, a_row, 1)
            ve.reciprocal(out=srow[:, 2048:3072],
                          in_=srow2[:, 3072:4096]).then_inc(cch, 1); ph.a(cch, 1)
            ph.w(ve, cch, 8)
            ve.tensor_tensor(out=srow[:, 3072:4096], in0=srow2[:, 0:1024],
                             in1=srow[:, 2048:3072], op=ALU.mult).then_inc(cch, 1); ph.a(cch, 1)
            ph.w(ve, cch, 9)
            ve.tensor_scalar(out=srow[:, 3072:4096], in0=srow[:, 3072:4096],
                             scalar1=-1.0, scalar2=None,
                             op0=ALU.mult).then_inc(v_row, 1); ph.a(v_row, 1)
            ph.w(ve, p_bc, 1)
            ve.tensor_copy(out=sra[:, 0:512], in_=yb[0][:, :]).then_inc(cch, 1); ph.a(cch, 1)
            ve.tensor_copy(out=sra[:, 512:1024], in_=yb[1][:, :]).then_inc(cch, 1); ph.a(cch, 1)
            ve.tensor_copy(out=srb2[:, 0:512], in_=yb[2][:, :]).then_inc(cch, 1); ph.a(cch, 1)
            ve.tensor_copy(out=srb2[:, 512:1024],
                           in_=yb[3][:, :]).then_inc(v_bc, 1); ph.a(v_bc, 1)
            for ch in range(NOB):
                ph.w(ve, x2_ld[ch % 2], 16 * (ch // 2 + 1))
                ph.w(ve, v_bc, 1)
                ph.w(ve, cch, 12)
                if ch >= 2:
                    ph.w(ve, p_out[ch % 2], 16 * (ch // 2))
                if ch >= 1:
                    ph.w(ve, v_p3, ch)
                ve.tensor_tensor(out=tmp2[:, :], in0=land2[ch % 2][:, :],
                                 in1=sra[:, :], op=ALU.mult).then_inc(v_p2, 1); ph.a(v_p2, 1)
                ph.w(ve, v_p2, ch + 1)
                ve.tensor_tensor(out=stage2[ch % 2][:, :], in0=tmp2[:, :],
                                 in1=srb2[:, :], op=ALU.add).then_inc(v_p3, 1); ph.a(v_p3, 1)

        @blk.scalar
        def _(sc):
            for ob in range(NOB):
                for half in range(2):
                    hs = slice(half * 512, (half + 1) * 512)
                    ph.w(sc, v_t, 2 * ob + half + 1)
                    if ob >= 2:
                        ph.w(sc, p_sd, ob - 1)
                    ins = sc.activation(out=stsq[ob % 2][:, hs],
                                        in_=st_t[ob % 2][:, hs], func=AF.Square)
                    if half == 1:
                        ins.then_inc(a_sq, 1); ph.a(a_sq, 1)
            ph.w(sc, v_row, 1)
            ph.w(sc, ini, 1)
            sc.activation(out=srow2[:, 3072:4096], in_=srow2[:, 2048:3072],
                          func=AF.Sqrt, bias=eps_t[:, 0:1]).then_inc(a_row, 1); ph.a(a_row, 1)

        def _stats(te, o):
            ph.w(te, v_st, o + 1)
            ph.w(te, a_sq, o + 1)
            for half in range(2):
                hs = slice(half * 512, (half + 1) * 512)
                te.matmul(out=ssum[half][:, :], lhsT=sonesc[:, :],
                          rhs=stb[o % 2][:, hs], start=(o == 0), stop=(o == NOB - 1),
                          skip_group_check=True)
                ins = te.matmul(out=ssq[half][:, :], lhsT=sonesc[:, :],
                                rhs=stsq[o % 2][:, hs], start=(o == 0),
                                stop=(o == NOB - 1), skip_group_check=True)
            ins.then_inc(p_sd, 1); ph.a(p_sd, 1)

        @blk.tensor
        def _(te):
            ph.w(te, c_ld, 96)
            for ob in range(NOB):
                ph.w(te, w_cv, ob + 1)
                if ob >= 2:
                    ph.w(te, v_t, 2 * (ob - 1))
                for cc in range(32):
                    for half in range(2):
                        ins = te.matmul(
                            out=yb[(ob % 2) * 2 + half][:, :],
                            lhsT=swb[ob % 2][:, cc * 128:(cc + 1) * 128],
                            rhs=so[:, cc, half * 512:(half + 1) * 512],
                            start=(cc == 0), stop=(cc == 31))
                        if cc == 31:
                            ins.then_inc(p_mm, 1); ph.a(p_mm, 1)
                if ob >= 1:
                    _stats(te, ob - 1)
            _stats(te, NOB - 1)
            ph.w(te, v_row, 2)
            ph.w(te, v_t, 2 * NOB)
            te.matmul(out=yb[0][:, :], lhsT=sonesr[:, :],
                      rhs=srow[:, 2048:2560], start=True, stop=True)
            te.matmul(out=yb[1][:, :], lhsT=sonesr[:, :],
                      rhs=srow[:, 2560:3072], start=True, stop=True)
            te.matmul(out=yb[2][:, :], lhsT=sonesr[:, :],
                      rhs=srow[:, 3072:3584], start=True, stop=True)
            te.matmul(out=yb[3][:, :], lhsT=sonesr[:, :],
                      rhs=srow[:, 3584:4096], start=True, stop=True).then_inc(p_bc, 1); ph.a(p_bc, 1)

        @blk.gpsimd
        def _(gp):
            for ob in range(NOB):
                ph.w(gp, v_t, 2 * ob + 2)
                gp.dma_start(out=XPRE[ob * 128:(ob + 1) * 128, :],
                             in_=st_t[ob % 2][:, :]).then_inc(p_pre[ob % 2], 16); ph.a(p_pre[ob % 2], 16)
            for ch in range(NOB):
                ph.w(gp, v_p3, ch + 1)
                gp.dma_start(out=XOUT[ch * 128:(ch + 1) * 128, :],
                             in_=stage2[ch % 2][:, :]).then_inc(p_out[ch % 2], 16); ph.a(p_out[ch % 2], 16)
            ph.w(gp, p_out[0], 16 * 16)
            ph.w(gp, p_out[1], 16 * 16)

    ph.close()


def emit_ffn0(nc, io, name, X1, QW_r, QB_r, WS, WZP, BS, BZP, XOUT, XPRE):
    ph = P(nc, name, io.get("pool"))
    ones_col, ones_row = io["ones_col"], io["ones_row"]

    sxb = ph.sb("sxb", [128, 32, 1024], BF16)
    land = [ph.sb(f"land{i}", [128, 1024], F32) for i in range(2)]
    landw = [ph.sb(f"landw{i}", [128, 32, 128], I32) for i in range(2)]
    swb = [ph.sb(f"swb{i}", [128, 4096], BF16) for i in range(2)]
    sx = [ph.sb(f"sx{i}", [128, 1024], F32) for i in range(2)]
    sy1 = [ph.sb(f"sy1{i}", [128, 1024], F32) for i in range(2)]
    st_t = [ph.sb(f"stt{i}", [128, 1024], F32) for i in range(2)]
    stb = [ph.sb(f"stb{i}", [128, 1024], BF16) for i in range(2)]
    stsq = [ph.sb(f"stsq{i}", [128, 1024], BF16) for i in range(2)]
    sonesc = ph.sb("sonesc", [128, 1], BF16)
    sonesr = ph.sb("sonesr", [1, 128], F32)
    s_ws = ph.sb("s_ws", [1, 1], F32)
    s_wzp = ph.sb("s_wzp", [1, 1], I32)
    s_bs = ph.sb("s_bs", [1, 1], F32)
    s_bzp = ph.sb("s_bzp", [1, 1], I32)
    s_wzpf = ph.sb("s_wzpf", [1, 1], F32)
    s_bzpf = ph.sb("s_bzpf", [1, 1], F32)
    sqb = ph.sb("sqb", [128, 32], I32)
    sqbf = ph.sb("sqbf", [128, 32], F32)
    sbias = ph.sb("sbias", [128, 32], F32)
    wsb = ph.sb("wsb", [128, 1], F32)
    bzpb = ph.sb("bzpb", [128, 1], F32)
    bsb = ph.sb("bsb", [128, 1], F32)
    wzrow = ph.sb("wzrow", [1, 128], F32)
    wzrow_b = ph.sb("wzrow_b", [1, 128], BF16)
    rsx_b = ph.sb("rsx_b", [1, 1024], BF16)
    srow = ph.sb("srow", [1, 4096], F32)
    sra = ph.sb("sra", [128, 1024], F32)
    srb2 = ph.sb("srb2", [128, 1024], F32)
    land2 = [ph.sb(f"land2{i}", [128, 1024], F32) for i in range(2)]
    tmp2 = ph.sb("tmp2", [128, 1024], F32)
    stage2 = [ph.sb(f"stage2{i}", [128, 1024], F32) for i in range(2)]
    eps_t = ph.sb("eps_t", [1, 1], F32)

    yb = [ph.ps(f"yb{i}", [128, 512]) for i in range(4)]
    ssum = [ph.ps(f"ssum{i}", [1, 512]) for i in range(2)]
    ssq = [ph.ps(f"ssq{i}", [1, 512]) for i in range(2)]

    c_ld = ph.sem("c_ld")
    px_ld = [ph.sem("px_ld0"), ph.sem("px_ld1")]
    pr_cv = ph.sem("pr_cv")
    pr_pe = ph.sem("pr_pe")
    cch = ph.sem("cch")
    w_ld = [ph.sem("w_ld0"), ph.sem("w_ld1")]
    w_cv = ph.sem("w_cv")
    x_ld = [ph.sem("x_ld0"), ph.sem("x_ld1")]
    p_mm = ph.sem("p_mm")
    p_sd = ph.sem("p_sd")
    p_bc = ph.sem("p_bc")
    a_ev = ph.sem("a_ev")
    v_t = ph.sem("v_t")
    v_st = ph.sem("v_st")
    a_sq = ph.sem("a_sq")
    p_pre = [ph.sem("p_pre0"), ph.sem("p_pre1")]
    v_row = ph.sem("v_row")
    a_row = ph.sem("a_row")
    v_bc = ph.sem("v_bc")
    x2_ld = [ph.sem("x2_ld0"), ph.sem("x2_ld1")]
    v_p2 = ph.sem("v_p2")
    v_p3 = ph.sem("v_p3")
    p_out = [ph.sem("p_out0"), ph.sem("p_out1")]
    ini = ph.sem("ini")

    NOB = 32

    with nc.Block(name) as blk:

        @blk.sync
        def _(sy):
            sy.dma_start(out=sonesc[:, :], in_=ones_col[:, :]).then_inc(c_ld, 16); ph.a(c_ld, 16)
            sy.dma_start(out=sonesr[:, :], in_=ones_row[:, :]).then_inc(c_ld, 16); ph.a(c_ld, 16)
            sy.dma_start(out=s_ws[:, :], in_=WS[:, :]).then_inc(c_ld, 16); ph.a(c_ld, 16)
            sy.dma_start(out=s_wzp[:, :], in_=WZP[:, :]).then_inc(c_ld, 16); ph.a(c_ld, 16)
            sy.dma_start(out=s_bs[:, :], in_=BS[:, :]).then_inc(c_ld, 16); ph.a(c_ld, 16)
            sy.dma_start(out=s_bzp[:, :], in_=BZP[:, :]).then_inc(c_ld, 16); ph.a(c_ld, 16)
            sy.dma_start(out=sqb[:, :], in_=QB_r[:, :]).then_inc(c_ld, 16); ph.a(c_ld, 16)
            for ch in range(32):
                if ch >= 2:
                    ph.w(sy, pr_cv, ch - 1)
                sy.dma_start(out=land[ch % 2][:, :],
                             in_=X1[ch * 128:(ch + 1) * 128, :]
                             ).then_inc(px_ld[ch % 2], 16); ph.a(px_ld[ch % 2], 16)
            for ob in range(NOB):
                if ob >= 2:
                    ph.w(sy, w_cv, ob - 1)
                for hh in range(2):
                    sy.dma_start(
                        out=landw[ob % 2][:, hh * 16:(hh + 1) * 16, :],
                        in_=QW_r[ob, hh * 16:(hh + 1) * 16, :, :].rearrange(
                            "i p c -> p i c")).then_inc(w_ld[ob % 2], 16); ph.a(w_ld[ob % 2], 16)
                if ob >= 2:
                    ph.w(sy, v_t, 2 * (ob - 1))
                sy.dma_start(out=sx[ob % 2][:, :],
                             in_=X1[ob * 128:(ob + 1) * 128, :]
                             ).then_inc(x_ld[ob % 2], 16); ph.a(x_ld[ob % 2], 16)
            for ch in range(NOB):
                ph.w(sy, p_pre[ch % 2], 16 * (ch // 2 + 1))
                if ch >= 2:
                    ph.w(sy, v_p2, ch - 1)
                sy.dma_start(out=land2[ch % 2][:, :],
                             in_=XPRE[ch * 128:(ch + 1) * 128, :]
                             ).then_inc(x2_ld[ch % 2], 16); ph.a(x2_ld[ch % 2], 16)

        @blk.vector
        def _(ve):
            ve.memset(eps_t[:, :], EPS).then_inc(ini, 1); ph.a(ini, 1)
            for ch in range(32):
                ph.w(ve, px_ld[ch % 2], 16 * (ch // 2 + 1))
                ve.tensor_copy(out=sxb[:, ch, :],
                               in_=land[ch % 2][:, :]).then_inc(pr_cv, 1); ph.a(pr_cv, 1)
            ph.w(ve, c_ld, 112)
            ve.tensor_copy(out=s_wzpf[:, :], in_=s_wzp[:, :]).then_inc(cch, 1); ph.a(cch, 1)
            ve.tensor_copy(out=s_bzpf[:, :], in_=s_bzp[:, :]).then_inc(cch, 1); ph.a(cch, 1)
            ve.tensor_copy(out=sqbf[:, :], in_=sqb[:, :]).then_inc(cch, 1); ph.a(cch, 1)
            ph.w(ve, cch, 3)
            ve.tensor_scalar(out=wzrow[:, :], in0=sonesr[:, :],
                             scalar1=s_wzpf[0:1, 0:1], scalar2=-1.0,
                             op0=ALU.mult, op1=ALU.mult).then_inc(cch, 1); ph.a(cch, 1)
            ph.w(ve, cch, 4)
            ve.tensor_copy(out=wzrow_b[:, :], in_=wzrow[:, :]).then_inc(v_row, 1); ph.a(v_row, 1)
            ph.w(ve, pr_pe, 1)
            ve.tensor_copy(out=wsb[:, :], in_=yb[0][:, 0:1]).then_inc(cch, 1); ph.a(cch, 1)
            ve.tensor_copy(out=bzpb[:, :], in_=yb[1][:, 0:1]).then_inc(cch, 1); ph.a(cch, 1)
            ve.tensor_copy(out=bsb[:, :], in_=yb[2][:, 0:1]).then_inc(cch, 1); ph.a(cch, 1)
            ph.w(ve, cch, 7)
            ve.tensor_scalar(out=sbias[:, :], in0=sqbf[:, :],
                             scalar1=bzpb[:, 0:1], scalar2=bsb[:, 0:1],
                             op0=ALU.subtract, op1=ALU.mult).then_inc(v_row, 1); ph.a(v_row, 1)
            ph.w(ve, pr_pe, 2)
            ve.tensor_copy(out=rsx_b[:, 0:512], in_=ssum[0][:, :]).then_inc(cch, 1); ph.a(cch, 1)
            ve.tensor_copy(out=rsx_b[:, 512:1024],
                           in_=ssum[1][:, :]).then_inc(v_row, 1); ph.a(v_row, 1)
            for ob in range(NOB):
                ph.w(ve, w_ld[ob % 2], 32 * (ob // 2 + 1))
                if ob >= 2:
                    ph.w(ve, p_mm, 2 * (ob - 1))
                ve.tensor_copy(
                    out=swb[ob % 2][:, :].rearrange("p (i c) -> p i c", i=32),
                    in_=landw[ob % 2][:, :, :]).then_inc(w_cv, 1); ph.a(w_cv, 1)
                for half in range(2):
                    hs = slice(half * 512, (half + 1) * 512)
                    ph.w(ve, a_ev, 2 * ob + half + 1)
                    ph.w(ve, x_ld[ob % 2], 16 * (ob // 2 + 1))
                    if ob >= 2:
                        ph.w(ve, p_pre[ob % 2], 16 * (ob // 2))
                        ph.w(ve, a_sq, ob - 1)
                        ph.w(ve, p_sd, ob - 1)
                    ve.tensor_tensor(out=st_t[ob % 2][:, hs],
                                     in0=sy1[ob % 2][:, hs],
                                     in1=sx[ob % 2][:, hs],
                                     op=ALU.add).then_inc(v_t, 1); ph.a(v_t, 1)
                    ph.w(ve, v_t, 2 * ob + half + 1)
                    ins = ve.tensor_copy(out=stb[ob % 2][:, hs],
                                         in_=st_t[ob % 2][:, hs])
                    if half == 1:
                        ins.then_inc(v_st, 1); ph.a(v_st, 1)
            ph.w(ve, p_sd, NOB)
            ve.tensor_copy(out=srow[:, 0:512], in_=ssum[0][:, :]).then_inc(cch, 1); ph.a(cch, 1)
            ve.tensor_copy(out=srow[:, 512:1024], in_=ssum[1][:, :]).then_inc(cch, 1); ph.a(cch, 1)
            ve.tensor_copy(out=srow[:, 1024:1536], in_=ssq[0][:, :]).then_inc(cch, 1); ph.a(cch, 1)
            ve.tensor_copy(out=srow[:, 1536:2048], in_=ssq[1][:, :]).then_inc(cch, 1); ph.a(cch, 1)
            ph.w(ve, cch, 12)
            ve.tensor_scalar(out=srow[:, 2048:3072], in0=srow[:, 0:1024],
                             scalar1=1.0 / C, scalar2=None,
                             op0=ALU.mult).then_inc(cch, 1); ph.a(cch, 1)
            ve.tensor_scalar(out=srow[:, 3072:4096], in0=srow[:, 1024:2048],
                             scalar1=1.0 / C, scalar2=None,
                             op0=ALU.mult).then_inc(cch, 1); ph.a(cch, 1)
            ph.w(ve, cch, 14)
            ve.tensor_tensor(out=srow[:, 0:1024], in0=srow[:, 2048:3072],
                             in1=srow[:, 2048:3072],
                             op=ALU.mult).then_inc(cch, 1); ph.a(cch, 1)
            ph.w(ve, cch, 15)
            ve.tensor_tensor(out=srow[:, 1024:2048], in0=srow[:, 3072:4096],
                             in1=srow[:, 0:1024],
                             op=ALU.subtract).then_inc(v_row, 1); ph.a(v_row, 1)
            ph.w(ve, a_row, 1)
            ve.reciprocal(out=srow[:, 3072:4096],
                          in_=srow[:, 0:1024]).then_inc(cch, 1); ph.a(cch, 1)
            ph.w(ve, cch, 16)
            ve.tensor_tensor(out=srow[:, 1024:2048], in0=srow[:, 2048:3072],
                             in1=srow[:, 3072:4096],
                             op=ALU.mult).then_inc(cch, 1); ph.a(cch, 1)
            ph.w(ve, cch, 17)
            ve.tensor_scalar(out=srow[:, 1024:2048], in0=srow[:, 1024:2048],
                             scalar1=-1.0, scalar2=None,
                             op0=ALU.mult).then_inc(v_row, 1); ph.a(v_row, 1)
            ph.w(ve, p_bc, 1)
            ve.tensor_copy(out=sra[:, 0:512], in_=yb[0][:, :]).then_inc(cch, 1); ph.a(cch, 1)
            ve.tensor_copy(out=sra[:, 512:1024], in_=yb[1][:, :]).then_inc(cch, 1); ph.a(cch, 1)
            ve.tensor_copy(out=srb2[:, 0:512], in_=yb[2][:, :]).then_inc(cch, 1); ph.a(cch, 1)
            ve.tensor_copy(out=srb2[:, 512:1024],
                           in_=yb[3][:, :]).then_inc(v_bc, 1); ph.a(v_bc, 1)
            for ch in range(NOB):
                ph.w(ve, x2_ld[ch % 2], 16 * (ch // 2 + 1))
                ph.w(ve, v_bc, 1)
                ph.w(ve, cch, 20)
                if ch >= 2:
                    ph.w(ve, p_out[ch % 2], 16 * (ch // 2))
                if ch >= 1:
                    ph.w(ve, v_p3, ch)
                ve.tensor_tensor(out=tmp2[:, :], in0=land2[ch % 2][:, :],
                                 in1=sra[:, :], op=ALU.mult).then_inc(v_p2, 1); ph.a(v_p2, 1)
                ph.w(ve, v_p2, ch + 1)
                ve.tensor_tensor(out=stage2[ch % 2][:, :], in0=tmp2[:, :],
                                 in1=srb2[:, :], op=ALU.add).then_inc(v_p3, 1); ph.a(v_p3, 1)

        @blk.scalar
        def _(sc):
            for ob in range(NOB):
                for half in range(2):
                    hs = slice(half * 512, (half + 1) * 512)
                    ph.w(sc, p_mm, 2 * ob + half + 1)
                    if ob >= 2:
                        ph.w(sc, v_t, 2 * (ob - 1))
                    ph.w(sc, v_row, 2)
                    sc.activation(out=sy1[ob % 2][:, hs],
                                  in_=yb[(ob % 2) * 2 + half][:, :],
                                  func=AF.Identity, scale=wsb[:, 0:1],
                                  bias=sbias[:, ob:ob + 1]).then_inc(a_ev, 1); ph.a(a_ev, 1)
                    ph.w(sc, v_t, 2 * ob + half + 1)
                    if ob >= 2:
                        ph.w(sc, p_sd, ob - 1)
                    ins = sc.activation(out=stsq[ob % 2][:, hs],
                                        in_=st_t[ob % 2][:, hs], func=AF.Square)
                    if half == 1:
                        ins.then_inc(a_sq, 1); ph.a(a_sq, 1)
            ph.w(sc, v_row, 4)
            ph.w(sc, ini, 1)
            sc.activation(out=srow[:, 0:1024], in_=srow[:, 1024:2048],
                          func=AF.Sqrt, bias=eps_t[:, 0:1]).then_inc(a_row, 1); ph.a(a_row, 1)

        def _stats3(te, o):
            ph.w(te, v_st, o + 1)
            ph.w(te, a_sq, o + 1)
            for half in range(2):
                hs = slice(half * 512, (half + 1) * 512)
                te.matmul(out=ssum[half][:, :], lhsT=sonesc[:, :],
                          rhs=stb[o % 2][:, hs], start=(o == 0), stop=(o == NOB - 1),
                          skip_group_check=True)
                ins = te.matmul(out=ssq[half][:, :], lhsT=sonesc[:, :],
                                rhs=stsq[o % 2][:, hs], start=(o == 0),
                                stop=(o == NOB - 1), skip_group_check=True)
            ins.then_inc(p_sd, 1); ph.a(p_sd, 1)

        @blk.tensor
        def _(te):
            ph.w(te, c_ld, 112)
            ph.w(te, cch, 3)
            te.matmul(out=yb[0][:, 0:1], lhsT=sonesr[:, :], rhs=s_ws[:, :],
                      start=True, stop=True)
            te.matmul(out=yb[1][:, 0:1], lhsT=sonesr[:, :], rhs=s_bzpf[:, :],
                      start=True, stop=True)
            te.matmul(out=yb[2][:, 0:1], lhsT=sonesr[:, :], rhs=s_bs[:, :],
                      start=True, stop=True).then_inc(pr_pe, 1); ph.a(pr_pe, 1)
            ph.w(te, pr_cv, 32)
            for cc in range(32):
                for half in range(2):
                    ins = te.matmul(out=ssum[half][:, :], lhsT=sonesc[:, :],
                                    rhs=sxb[:, cc, half * 512:(half + 1) * 512],
                                    start=(cc == 0), stop=(cc == 31),
                                    skip_group_check=True)
            ins.then_inc(pr_pe, 1); ph.a(pr_pe, 1)
            for ob in range(NOB):
                ph.w(te, w_cv, ob + 1)
                ph.w(te, v_row, 3)
                if ob >= 2:
                    ph.w(te, a_ev, 2 * (ob - 1))
                for half in range(2):
                    hs = slice(half * 512, (half + 1) * 512)
                    for cc in range(32):
                        te.matmul(
                            out=yb[(ob % 2) * 2 + half][:, :],
                            lhsT=swb[ob % 2][:, cc * 128:(cc + 1) * 128],
                            rhs=sxb[:, cc, hs], start=(cc == 0), stop=False)
                    te.matmul(out=yb[(ob % 2) * 2 + half][:, :],
                              lhsT=wzrow_b[:, :], rhs=rsx_b[0:1, hs],
                              start=False, stop=True).then_inc(p_mm, 1); ph.a(p_mm, 1)
                if ob >= 1:
                    _stats3(te, ob - 1)
            _stats3(te, NOB - 1)
            ph.w(te, v_row, 5)
            ph.w(te, a_ev, 2 * NOB)
            te.matmul(out=yb[0][:, :], lhsT=sonesr[:, :],
                      rhs=srow[:, 3072:3584], start=True, stop=True)
            te.matmul(out=yb[1][:, :], lhsT=sonesr[:, :],
                      rhs=srow[:, 3584:4096], start=True, stop=True)
            te.matmul(out=yb[2][:, :], lhsT=sonesr[:, :],
                      rhs=srow[:, 1024:1536], start=True, stop=True)
            te.matmul(out=yb[3][:, :], lhsT=sonesr[:, :],
                      rhs=srow[:, 1536:2048], start=True, stop=True).then_inc(p_bc, 1); ph.a(p_bc, 1)

        @blk.gpsimd
        def _(gp):
            for ob in range(NOB):
                ph.w(gp, v_t, 2 * ob + 2)
                ph.w(gp, a_sq, ob + 1)
                gp.dma_start(out=XPRE[ob * 128:(ob + 1) * 128, :],
                             in_=st_t[ob % 2][:, :]).then_inc(p_pre[ob % 2], 16); ph.a(p_pre[ob % 2], 16)
            for ch in range(NOB):
                ph.w(gp, v_p3, ch + 1)
                gp.dma_start(out=XOUT[ch * 128:(ch + 1) * 128, :],
                             in_=stage2[ch % 2][:, :]).then_inc(p_out[ch % 2], 16); ph.a(p_out[ch % 2], 16)
            ph.w(gp, p_out[0], 16 * 16)
            ph.w(gp, p_out[1], 16 * 16)

    ph.close()


def emit_p4(nc, io, name, X2, WQ1_r, WK1, WV1, QT1, KT1c, V1c):
    ph = P(nc, name, io.get("pool"))

    sxb = ph.sb("sxb", [128, 32, 1024], BF16)
    land = [ph.sb(f"land{i}", [128, 1024], F32) for i in range(2)]
    landw = [ph.sb(f"landw{i}", [128, 32, 128], F32) for i in range(2)]
    kland = ph.sb("kland", [128, 16, 128], F32)
    swk = ph.sb("swk", [128, 4096], BF16)
    swv = ph.sb("swv", [128, 4096], BF16)
    swq = [ph.sb(f"swq{i}", [128, 4096], BF16) for i in range(2)]
    kstg = [ph.sb(f"kstg{i}", [128, 512], BF16) for i in range(2)]
    vstg = [ph.sb(f"vstg{i}", [128, 128], BF16) for i in range(2)]
    qstg = [ph.sb(f"qstg{i}", [128, 1024], BF16) for i in range(2)]

    kb = [ph.ps(f"kb{i}", [128, 512]) for i in range(2)]
    vb = [ph.ps(f"vb{i}", [128, 128]) for i in range(2)]
    qb = [ph.ps(f"qb{i}", [128, 512]) for i in range(4)]

    px_ld = [ph.sem("px_ld0"), ph.sem("px_ld1")]
    pr_cv = ph.sem("pr_cv")
    kv_ld = ph.sem("kv_ld")
    kv_cv = ph.sem("kv_cv")
    w_ld = [ph.sem("w_ld0"), ph.sem("w_ld1")]
    w_cv = ph.sem("w_cv")
    p_k = ph.sem("p_k")
    p_v = ph.sem("p_v")
    p_q = ph.sem("p_q")
    v_k = ph.sem("v_k")
    v_v = ph.sem("v_v")
    v_q = ph.sem("v_q")
    s_k = [ph.sem("s_k0"), ph.sem("s_k1")]
    s_v = [ph.sem("s_v0"), ph.sem("s_v1")]
    s_q = [ph.sem("s_q0"), ph.sem("s_q1")]

    with nc.Block(name) as blk:

        @blk.sync
        def _(sy):
            for w in range(4):
                src = WK1 if w < 2 else WV1
                hh = w % 2
                if w > 0:
                    ph.w(sy, kv_cv, w)
                sy.dma_start(out=kland[:, :, :],
                             in_=src[hh * 2048:(hh + 1) * 2048, :]
                             .rearrange("(i p) d -> p i d", p=128)).then_inc(kv_ld, 16); ph.a(kv_ld, 16)
            for ch in range(32):
                if ch >= 2:
                    ph.w(sy, pr_cv, ch - 1)
                sy.dma_start(out=land[ch % 2][:, :],
                             in_=X2[ch * 128:(ch + 1) * 128, :]
                             ).then_inc(px_ld[ch % 2], 16); ph.a(px_ld[ch % 2], 16)
            for hd in range(32):
                if hd >= 2:
                    ph.w(sy, w_cv, hd - 1)
                for hh in range(2):
                    sy.dma_start(
                        out=landw[hd % 2][:, hh * 16:(hh + 1) * 16, :],
                        in_=WQ1_r[hd, hh * 16:(hh + 1) * 16, :, :].rearrange(
                            "i p c -> p i c")).then_inc(w_ld[hd % 2], 16); ph.a(w_ld[hd % 2], 16)

        @blk.vector
        def _(ve):
            for w in range(4):
                ph.w(ve, kv_ld, 16 * (w + 1))
                dst = swk if w < 2 else swv
                off = (w % 2) * 2048
                ve.tensor_copy(
                    out=dst[:, off:off + 2048].rearrange("p (i c) -> p i c", i=16),
                    in_=kland[:, :, :]).then_inc(kv_cv, 1); ph.a(kv_cv, 1)
            for ch in range(32):
                ph.w(ve, px_ld[ch % 2], 16 * (ch // 2 + 1))
                ve.tensor_copy(out=sxb[:, ch, :],
                               in_=land[ch % 2][:, :]).then_inc(pr_cv, 1); ph.a(pr_cv, 1)
            for half in range(2):
                ph.w(ve, p_k, half + 1)
                if half >= 2:
                    ph.w(ve, s_k[half % 2], 16 * (half // 2))
                ve.tensor_copy(out=kstg[half % 2][:, :],
                               in_=kb[half][:, :]).then_inc(v_k, 1); ph.a(v_k, 1)
            for tt in range(8):
                ph.w(ve, p_v, tt + 1)
                if tt >= 2:
                    ph.w(ve, s_v[tt % 2], 16 * (tt // 2))
                ve.tensor_copy(out=vstg[tt % 2][:, :],
                               in_=vb[tt % 2][:, :]).then_inc(v_v, 1); ph.a(v_v, 1)
            for hd in range(32):
                for half in range(2):
                    g = hd * 2 + half
                    ph.w(ve, p_q, g + 1)
                    if hd >= 2:
                        ph.w(ve, s_q[hd % 2], 16 * (hd // 2))
                    ve.tensor_copy(out=qstg[hd % 2][:, half * 512:(half + 1) * 512],
                                   in_=qb[(hd % 2) * 2 + half][:, :]).then_inc(v_q, 1); ph.a(v_q, 1)

        @blk.tensor
        def _(te):
            ph.w(te, kv_cv, 4)
            ph.w(te, pr_cv, 32)
            for half in range(2):
                for i in range(32):
                    ins = te.matmul(out=kb[half][:, :],
                                    lhsT=swk[:, i * 128:(i + 1) * 128],
                                    rhs=sxb[:, i, half * 512:(half + 1) * 512],
                                    start=(i == 0), stop=(i == 31))
                ins.then_inc(p_k, 1); ph.a(p_k, 1)
            for tt in range(8):
                if tt >= 2:
                    ph.w(te, v_v, tt - 1)
                for i in range(32):
                    ins = te.matmul(out=vb[tt % 2][:, :],
                                    lhsT=sxb[:, i, tt * 128:(tt + 1) * 128],
                                    rhs=swv[:, i * 128:(i + 1) * 128],
                                    start=(i == 0), stop=(i == 31))
                ins.then_inc(p_v, 1); ph.a(p_v, 1)
            for hd in range(32):
                ph.w(te, w_cv, hd + 1)
                if hd >= 2:
                    ph.w(te, v_q, 2 * (hd - 1))
                for half in range(2):
                    for i in range(32):
                        ins = te.matmul(
                            out=qb[(hd % 2) * 2 + half][:, :],
                            lhsT=swq[hd % 2][:, i * 128:(i + 1) * 128],
                            rhs=sxb[:, i, half * 512:(half + 1) * 512],
                            start=(i == 0), stop=(i == 31))
                    ins.then_inc(p_q, 1); ph.a(p_q, 1)

        @blk.scalar
        def _(sc):
            for hd in range(32):
                ph.w(sc, w_ld[hd % 2], 32 * (hd // 2 + 1))
                if hd >= 2:
                    ph.w(sc, p_q, 2 * (hd - 1))
                sc.activation(out=swq[hd % 2][:, :].rearrange(
                                  "p (i c) -> p i c", i=32),
                              in_=landw[hd % 2][:, :, :],
                              func=AF.Copy).then_inc(w_cv, 1); ph.a(w_cv, 1)

        @blk.gpsimd
        def _(gp):
            for half in range(2):
                ph.w(gp, v_k, half + 1)
                gp.dma_start(out=KT1c[:, half * 512:(half + 1) * 512],
                             in_=kstg[half % 2][:, :]).then_inc(s_k[half % 2], 16); ph.a(s_k[half % 2], 16)
            for tt in range(8):
                ph.w(gp, v_v, tt + 1)
                gp.dma_start(out=V1c[tt * 128:(tt + 1) * 128, :],
                             in_=vstg[tt % 2][:, :]).then_inc(s_v[tt % 2], 16); ph.a(s_v[tt % 2], 16)
            for hd in range(32):
                ph.w(gp, v_q, 2 * hd + 2)
                gp.dma_start(out=QT1[hd * 128:(hd + 1) * 128, :],
                             in_=qstg[hd % 2][:, :]).then_inc(s_q[hd % 2], 16); ph.a(s_q[hd % 2], 16)
            ph.w(gp, s_k[0], 16)
            ph.w(gp, s_k[1], 16)
            ph.w(gp, s_v[0], 16 * 4)
            ph.w(gp, s_v[1], 16 * 4)
            ph.w(gp, s_q[0], 16 * 16)
            ph.w(gp, s_q[1], 16 * 16)

    ph.close()


def emit_p5(nc, io, name, QT1, KG1, VG1, TTHR, OT1):
    ph = P(nc, name, io.get("pool"))
    iota_d, ones_col, ones_row = io["iota"], io["ones_col"], io["ones_row"]

    sk = ph.sb("sk", [128, T], BF16)
    sv = ph.sb("sv", [128, 32, 128], BF16)
    siota = ph.sb("siota", [128, 512], F32)
    sthr = ph.sb("sthr", [128, 48], F32)
    sq = [ph.sb(f"sq{i}", [128, 512], BF16) for i in range(2)]
    sp = [ph.sb(f"sp{i}", [128, 512], BF16) for i in range(3)]
    spm = [ph.sb(f"spm{i}", [128, 512], BF16) for i in range(3)]
    sm01 = ph.sb("sm01", [128, 32, 512], BF16)
    sones = ph.sb("sones", [128, 1], BF16)
    sonesr = ph.sb("sonesr", [1, 128], F32)
    srcp = [ph.sb(f"srcp{i}", [1, 512], F32) for i in range(2)]
    srb = ph.sb("srb", [128, 512], F32)
    ostg = [ph.sb(f"ostg{i}", [128, 512], BF16) for i in range(2)]
    sbias = ph.sb("sbias", [128, 1], F32)

    bS = [ph.ps(f"bS{i}", [128, 512]) for i in range(3)]
    bO = [ph.ps(f"bO{i}", [128, 512]) for i in range(2)]
    bD = [ph.ps(f"bD{i}", [1, 512]) for i in range(2)]
    bRB = ph.ps("bRB", [128, 512])

    c_ld = ph.sem("c_ld")
    q_ld = [ph.sem("q_ld0"), ph.sem("q_ld1")]
    s_mm = ph.sem("s_mm")
    s_pv = ph.sem("s_pv")
    s_dn = ph.sem("s_dn")
    s_bc = ph.sem("s_bc")
    a_ex = ph.sem("a_ex")
    v_mk = ph.sem("v_mk")
    v_m01 = ph.sem("v_m01")
    v_rcp = ph.sem("v_rcp")
    v_ev = ph.sem("v_ev")
    p_st = [ph.sem("p_st0"), ph.sem("p_st1")]
    v_ini = ph.sem("v_ini")
    ch_ve = ph.sem("ch_ve")

    PA = [16, 32]
    NHD = 32
    fused = io.get("fused", False)
    NCLD = 20 if fused else 9

    KM = [PA[0]] * NHD + [PA[1]] * NHD
    NG = 2 * NHD
    CUM = [0]
    for km in KM:
        CUM.append(CUM[-1] + km)
    G_TOT = CUM[-1]
    IT = [(ha, b) for ha in range(NG) for b in range(KM[ha])]

    def m01col(ha, b):
        a = ha // NHD
        return 16 * a + b

    with nc.Block(name) as blk:

        @blk.sync
        def _(sy):
            if fused:
                for g in range(8):
                    j = g if g < 4 else 7 - g
                    half = 0 if g < 4 else 1
                    sy.dma_start(out=sk[:, g * 512:(g + 1) * 512],
                                 in_=KG1[j, :, half * 512:(half + 1) * 512]
                                 ).then_inc(c_ld, 16); ph.a(c_ld, 16)
                for g in range(8):
                    j = g if g < 4 else 7 - g
                    half = 0 if g < 4 else 1
                    sy.dma_start(
                        out=sv[:, g * 4:(g + 1) * 4, :],
                        in_=VG1[j, half * 512:(half + 1) * 512, :].rearrange(
                            "(kc p) d -> p kc d", p=128)
                        ).then_inc(c_ld, 16); ph.a(c_ld, 16)
            else:
                sy.dma_start(out=sk[:, :],
                             in_=KG1[:, :]).then_inc(c_ld, 16); ph.a(c_ld, 16)
                for part in range(4):
                    sy.dma_start(
                        out=sv[:, part * 8:(part + 1) * 8, :],
                        in_=VG1[part * 1024:(part + 1) * 1024, :].rearrange(
                            "(kc p) d -> p kc d", p=128)
                        ).then_inc(c_ld, 16); ph.a(c_ld, 16)
            sy.dma_start(out=siota[:, :],
                         in_=iota_d[:, :]).then_inc(c_ld, 16); ph.a(c_ld, 16)
            sy.dma_start(out=sthr[:, :],
                         in_=TTHR[:, :]).then_inc(c_ld, 16); ph.a(c_ld, 16)
            sy.dma_start(out=sones[:, :],
                         in_=ones_col[:, :]).then_inc(c_ld, 16); ph.a(c_ld, 16)
            sy.dma_start(out=sonesr[:, :],
                         in_=ones_row[:, :]).then_inc(c_ld, 16); ph.a(c_ld, 16)
            for ha in range(NG):
                h, a = ha % NHD, ha // NHD
                if ha >= 2:
                    ph.w(sy, s_mm, CUM[ha - 1])
                sy.dma_start(out=sq[ha % 2][:, :],
                             in_=QT1[h * 128:(h + 1) * 128,
                                     a * 512:(a + 1) * 512]
                             ).then_inc(q_ld[ha % 2], 16); ph.a(q_ld[ha % 2], 16)

        @blk.tensor
        def _(te):
            def s_issue(k):
                ha, b = IT[k]
                if b == 0:
                    if ha == 0:
                        ph.w(te, c_ld, 16 * NCLD)
                    ph.w(te, q_ld[ha % 2], 16 * (ha // 2 + 1))
                if k >= 3:
                    ph.w(te, a_ex, k - 2)
                te.matmul(out=bS[k % 3][:, :],
                          lhsT=sk[:, b * 128:(b + 1) * 128],
                          rhs=sq[ha % 2][:, :], start=True, stop=True
                          ).then_inc(s_mm, 1); ph.a(s_mm, 1)

            def bcast(j):
                ph.w(te, v_rcp, j + 1)
                if j >= 1:
                    ph.w(te, ch_ve, j)
                te.matmul(out=bRB[:, :], lhsT=sonesr[:, :],
                          rhs=srcp[j % 2][:, :], start=True, stop=True
                          ).then_inc(s_bc, 1); ph.a(s_bc, 1)

            for k in range(min(3, G_TOT)):
                s_issue(k)
            for g in range(G_TOT):
                ha, b = IT[g]
                if g + 3 < G_TOT:
                    s_issue(g + 3)
                if b == 1 and ha >= 1:
                    bcast(ha - 1)
                if b == 0 and ha >= 2:
                    ph.w(te, v_ev, ha - 1)
                    ph.w(te, v_rcp, ha - 1)
                ph.w(te, v_mk, g + 1)
                te.matmul(out=bO[ha % 2][:, :],
                          lhsT=sv[:, b, :], rhs=spm[g % 3][:, :],
                          start=(b == 0), stop=(b == KM[ha] - 1))
                te.matmul(out=bD[ha % 2][:, :], lhsT=sones[:, :],
                          rhs=spm[g % 3][:, :],
                          start=(b == 0), stop=(b == KM[ha] - 1),
                          skip_group_check=True
                          ).then_inc(s_pv, 1); ph.a(s_pv, 1)
            bcast(NG - 1)

        @blk.scalar
        def _(sc):
            ph.w(sc, v_ini, 1)
            for k in range(G_TOT):
                ph.w(sc, s_mm, k + 1)
                if k >= 3:
                    ph.w(sc, s_pv, k - 2)
                sc.activation(out=sp[k % 3][:, :], in_=bS[k % 3][:, :],
                              func=AF.Exp, scale=SCALE,
                              bias=sbias[:, 0:1]).then_inc(a_ex, 1); ph.a(a_ex, 1)

        @blk.vector
        def _(ve):
            ve.memset(sbias[:, :], -SOFT_B).then_inc(v_ini, 1); ph.a(v_ini, 1)
            ph.w(ve, c_ld, 16 * NCLD)

            def group_epi(j):
                if j >= 1:
                    emit_ev(j - 1)
                ph.w(ve, s_pv, CUM[j + 1])
                ve.reciprocal(out=srcp[j % 2][:, :],
                              in_=bD[j % 2][0:1, :]).then_inc(v_rcp, 1)
                ph.a(v_rcp, 1)

            def emit_ev(j):
                ph.w(ve, s_bc, j + 1)
                if j >= 1:
                    ph.w(ve, v_ev, j)
                ve.tensor_copy(out=srb[:, :],
                               in_=bRB[:, :]).then_inc(ch_ve, 1); ph.a(ch_ve, 1)
                if j >= 2:
                    ph.w(ve, p_st[j % 2], 16 * (j // 2))
                ph.w(ve, ch_ve, j + 1)
                ve.tensor_tensor(out=ostg[j % 2][:, :], in0=bO[j % 2][:, :],
                                 in1=srb[:, :],
                                 op=ALU.mult).then_inc(v_ev, 1); ph.a(v_ev, 1)

            g = 0
            for ha in range(NG):
                if ha % NHD == 0:
                    a = ha // NHD
                    if a == 1:
                        ph.w(ve, v_mk, NHD * PA[0])
                    for b in range(PA[a]):
                        ve.tensor_scalar(out=sm01[:, b, :], in0=siota[:, :],
                                         scalar1=sthr[:, 16 * a + b:
                                                      16 * a + b + 1],
                                         scalar2=None, op0=ALU.is_ge
                                         ).then_inc(v_m01, 1); ph.a(v_m01, 1)
                for b in range(KM[ha]):
                    ph.w(ve, v_m01, (16 if ha >= NHD else 0) + b + 1)
                    ph.w(ve, a_ex, g + 1)
                    if g >= 3:
                        ph.w(ve, s_pv, g - 2)
                    ve.tensor_tensor(out=spm[g % 3][:, :], in0=sp[g % 3][:, :],
                                     in1=sm01[:, b, :],
                                     op=ALU.mult).then_inc(v_mk, 1)
                    ph.a(v_mk, 1)
                    g += 1
                group_epi(ha)
            emit_ev(NG - 1)

        @blk.gpsimd
        def _(gp):
            for ha in range(NG):
                h, a = ha % NHD, ha // NHD
                ph.w(gp, v_ev, ha + 1)
                gp.dma_start(out=OT1[h * 128:(h + 1) * 128,
                                     a * 512:(a + 1) * 512],
                             in_=ostg[ha % 2][:, :]
                             ).then_inc(p_st[ha % 2], 16); ph.a(p_st[ha % 2], 16)
            ph.w(gp, p_st[0], 16 * 32)
            ph.w(gp, p_st[1], 16 * 32)

    ph.close()



def emit_ffn1(nc, io, name, X3, BW_r, GWT, ATALL, BTALL, BB_r, OUT, OUTPRE):
    ph = P(nc, name, io.get("pool"))
    ones_col, ones_row = io["ones_col"], io["ones_row"]
    ident_d, exp8_d = io["ident"], io["exp8"]

    sxb = ph.sb("sxb", [128, 32, 1024], BF16)
    land = [ph.sb(f"land{i}", [128, 1024], F32) for i in range(2)]
    landw = [ph.sb(f"landw{i}", [128, 16, 128], F32) for i in range(2)]
    swb = [ph.sb(f"swb{i}", [128, 4096], BF16) for i in range(2)]
    sx = [ph.sb(f"sx{i}", [128, 1024], F32) for i in range(2)]
    sy1 = [ph.sb(f"sy1{i}", [128, 1024], F32) for i in range(2)]
    st_t = [ph.sb(f"stt{i}", [128, 1024], F32) for i in range(2)]
    stb = [ph.sb(f"stb{i}", [128, 1024], BF16) for i in range(2)]
    stsq = [ph.sb(f"stsq{i}", [128, 1024], BF16) for i in range(2)]
    sonesc = ph.sb("sonesc", [128, 1], BF16)
    sonesr = ph.sb("sonesr", [1, 128], F32)
    sid = ph.sb("sid", [128, 128], F32)
    sexp8 = ph.sb("sexp8", [8, 128], BF16)
    sgw = ph.sb("sgw", [128, 32, 8], BF16)
    sat = ph.sb("sat", [128, 32, 128], BF16)
    sbt = ph.sb("sbt", [128, 4096], BF16)
    sbb = ph.sb("sbb", [128, 32], F32)
    sgrow = ph.sb("sgrow", [8, 1024], F32)
    sgt = ph.sb("sgt", [128, 64], F32)
    swork = ph.sb("swork", [128, 32], F32)
    svec = ph.sb("svec", [128, 8], F32)
    swf = ph.sb("swf", [128, 64], F32)
    swfT = ph.sb("swfT", [8, 1024], BF16)
    swfe = ph.sb("swfe", [128, 1024], BF16)
    slw = ph.sb("slw", [128, 1024], BF16)
    srow = ph.sb("srow", [1, 4096], F32)
    sra = ph.sb("sra", [128, 1024], F32)
    srb2 = ph.sb("srb2", [128, 1024], F32)
    land2 = [ph.sb(f"land2{i}", [128, 1024], F32) for i in range(2)]
    stage2 = [ph.sb(f"stage2{i}", [128, 1024], F32) for i in range(2)]
    eps_t = ph.sb("eps_t", [1, 1], F32)

    yb = [ph.ps(f"yb{i}", [128, 512]) for i in range(4)]
    ssum = [ph.ps(f"ssum{i}", [8, 512]) for i in range(2)]
    ssq = [ph.ps(f"ssq{i}", [1, 512]) for i in range(2)]

    c_ld = ph.sem("c_ld")
    px_ld = [ph.sem("px_ld0"), ph.sem("px_ld1")]
    pr_cv = ph.sem("pr_cv")
    g_pe = ph.sem("g_pe")
    g_ve = ph.sem("g_ve")
    g_ac = ph.sem("g_ac")
    w_ld = [ph.sem("w_ld0"), ph.sem("w_ld1")]
    w_cv = ph.sem("w_cv")
    x_ld = [ph.sem("x_ld0"), ph.sem("x_ld1")]
    p_mm = ph.sem("p_mm")
    p_sd = ph.sem("p_sd")
    p_bc = ph.sem("p_bc")
    a_ev = ph.sem("a_ev")
    v_t = ph.sem("v_t")
    v_st = ph.sem("v_st")
    a_sq = ph.sem("a_sq")
    p_pre = [ph.sem("p_pre0"), ph.sem("p_pre1")]
    v_row = ph.sem("v_row")
    a_row = ph.sem("a_row")
    v_bc = ph.sem("v_bc")
    x2_ld = [ph.sem("x2_ld0"), ph.sem("x2_ld1")]
    v_p2 = ph.sem("v_p2")
    v_p3 = ph.sem("v_p3")
    p_out = [ph.sem("p_out0"), ph.sem("p_out1")]
    ini = ph.sem("ini")
    cch = ph.sem("cch")

    NOB = 32

    with nc.Block(name) as blk:

        @blk.sync
        def _(sy):
            sy.dma_start(out=sonesc[:, :], in_=ones_col[:, :]).then_inc(c_ld, 16); ph.a(c_ld, 16)
            sy.dma_start(out=sonesr[:, :], in_=ones_row[:, :]).then_inc(c_ld, 16); ph.a(c_ld, 16)
            sy.dma_start(out=sid[:, :], in_=ident_d[:, :]).then_inc(c_ld, 16); ph.a(c_ld, 16)
            sy.dma_start(out=sexp8[:, :], in_=exp8_d[:, :]).then_inc(c_ld, 16); ph.a(c_ld, 16)
            sy.dma_start(out=sbb[:, :], in_=BB_r[:, :]).then_inc(c_ld, 16); ph.a(c_ld, 16)
            for ch in range(32):
                if ch >= 2:
                    ph.w(sy, pr_cv, ch - 1)
                sy.dma_start(out=land[ch % 2][:, :],
                             in_=X3[ch * 128:(ch + 1) * 128, :]
                             ).then_inc(px_ld[ch % 2], 16); ph.a(px_ld[ch % 2], 16)
            ph.w(sy, pr_cv, 31)
            sy.dma_start(out=land[0][:, 0:256].rearrange("p (i e) -> p i e", i=32),
                         in_=GWT[:, :].rearrange("(i p) e -> p i e", p=128)
                         ).then_inc(c_ld, 16); ph.a(c_ld, 16)
            for j in range(4):
                ph.w(sy, pr_cv, 33 + j - 1 if j >= 2 else (33 if j == 0 else 32))
                sy.dma_start(
                    out=land[j % 2][:, :].rearrange("p (i e) -> p i e", i=8),
                    in_=ATALL[j * 1024:(j + 1) * 1024, :].rearrange(
                        "(i p) e -> p i e", p=128)).then_inc(c_ld, 16); ph.a(c_ld, 16)
            for k in range(4):
                ph.w(sy, pr_cv, 36 + k)
                sy.dma_start(out=land[k % 2][:, :],
                             in_=BTALL[:, k * 1024:(k + 1) * 1024]
                             ).then_inc(c_ld, 16); ph.a(c_ld, 16)
            for ob in range(NOB):
                for hh in range(2):
                    pcn = 2 * ob + hh
                    if hh == 1:
                        ph.w(sy, w_cv, pcn)
                    elif ob >= 2:
                        ph.w(sy, w_cv, pcn - 2)
                    sy.dma_start(
                        out=landw[ob % 2][:, :, :],
                        in_=BW_r[ob, hh * 16:(hh + 1) * 16, :, :].rearrange(
                            "i p c -> p i c")).then_inc(w_ld[ob % 2], 16); ph.a(w_ld[ob % 2], 16)
                if ob >= 2:
                    ph.w(sy, v_t, 2 * (ob - 1))
                sy.dma_start(out=sx[ob % 2][:, :],
                             in_=X3[ob * 128:(ob + 1) * 128, :]
                             ).then_inc(x_ld[ob % 2], 16); ph.a(x_ld[ob % 2], 16)
            for ch in range(NOB):
                ph.w(sy, p_pre[ch % 2], 16 * (ch // 2 + 1))
                if ch >= 2:
                    ph.w(sy, v_p2, ch - 1)
                sy.dma_start(out=land2[ch % 2][:, :],
                             in_=OUTPRE[ch * 128:(ch + 1) * 128, :]
                             ).then_inc(x2_ld[ch % 2], 16); ph.a(x2_ld[ch % 2], 16)

        @blk.vector
        def _(ve):
            ve.memset(eps_t[:, :], EPS).then_inc(ini, 1); ph.a(ini, 1)
            for ch in range(32):
                ph.w(ve, px_ld[ch % 2], 16 * (ch // 2 + 1))
                ve.tensor_copy(out=sxb[:, ch, :],
                               in_=land[ch % 2][:, :]).then_inc(pr_cv, 1); ph.a(pr_cv, 1)
            ph.w(ve, c_ld, 96)
            ve.tensor_copy(out=sgw[:, :, :],
                           in_=land[0][:, 0:256].rearrange(
                               "p (i e) -> p i e", i=32)).then_inc(pr_cv, 1); ph.a(pr_cv, 1)
            for j in range(4):
                ph.w(ve, c_ld, 128 if j < 2 else 160)
                ve.tensor_copy(out=sat[:, j * 8:(j + 1) * 8, :],
                               in_=land[j % 2][:, :].rearrange(
                                   "p (i e) -> p i e", i=8)).then_inc(pr_cv, 1); ph.a(pr_cv, 1)
            for k in range(4):
                ph.w(ve, c_ld, 192 if k < 2 else 224)
                ve.tensor_copy(out=sbt[:, k * 1024:(k + 1) * 1024],
                               in_=land[k % 2][:, :]).then_inc(pr_cv, 1); ph.a(pr_cv, 1)
            ph.w(ve, g_pe, 2)
            ve.tensor_copy(out=sgrow[:, 0:512], in_=ssum[0][0:8, :])
            ve.tensor_copy(out=sgrow[:, 512:1024],
                           in_=ssum[1][0:8, :]).then_inc(g_ve, 1); ph.a(g_ve, 1)
            gv = 1
            io["_p7_a"] = a_marks = []
            io["_p7_b"] = b_marks = []
            io["_p7_h"] = h_marks = []
            io["_p7_l"] = l_marks = []
            for tt in range(8):
                ph.w(ve, g_pe, 2 + tt + 1)
                ve.tensor_copy(out=sgt[:, tt * 8:(tt + 1) * 8],
                               in_=yb[0][:, 0:8]).then_inc(g_ve, 1); ph.a(g_ve, 1)
                gv += 1; a_marks.append(gv)
                gts = sgt[:, tt * 8:(tt + 1) * 8]
                ph.w(ve, g_ve, gv)
                if tt >= 1:
                    ph.w(ve, g_ac, tt)
                ve.reduce_max(out=svec[:, 0:1], in_=gts, axis=AX,
                              negate=True).then_inc(g_ve, 1); ph.a(g_ve, 1)
                gv += 1; b_marks.append(gv)
                ph.w(ve, g_ve, gv)
                ve.tensor_scalar(out=svec[:, 1:2], in0=svec[:, 0:1],
                                 scalar1=-1.0, scalar2=None,
                                 op0=ALU.mult).then_inc(g_ve, 1); ph.a(g_ve, 1)
                gv += 1
                ph.w(ve, g_ve, gv)
                ve.tensor_scalar(out=swork[:, 0:8], in0=gts,
                                 scalar1=svec[:, 1:2], scalar2=None,
                                 op0=ALU.is_ge).then_inc(g_ve, 1); ph.a(g_ve, 1)
                gv += 1
                ph.w(ve, g_ve, gv)
                ve.scalar_tensor_tensor(out=swork[:, 8:16], in0=swork[:, 0:8],
                                        scalar=1e30, in1=gts,
                                        op0=ALU.mult,
                                        op1=ALU.subtract).then_inc(g_ve, 1); ph.a(g_ve, 1)
                gv += 1
                ph.w(ve, g_ve, gv)
                ve.tensor_reduce(out=svec[:, 2:3], in_=swork[:, 8:16], axis=AX,
                                 op=ALU.min, negate=True).then_inc(g_ve, 1); ph.a(g_ve, 1)
                gv += 1
                ph.w(ve, g_ve, gv)
                ve.tensor_scalar(out=swork[:, 24:32], in0=gts,
                                 scalar1=svec[:, 2:3], scalar2=None,
                                 op0=ALU.is_ge).then_inc(g_ve, 1); ph.a(g_ve, 1)
                gv += 1
                ph.w(ve, g_ac, tt + 1)
                ph.w(ve, g_ve, gv)
                ve.tensor_tensor(out=swork[:, 16:24], in0=swork[:, 16:24],
                                 in1=swork[:, 24:32],
                                 op=ALU.mult).then_inc(g_ve, 1); ph.a(g_ve, 1)
                gv += 1; h_marks.append(gv)
                ph.w(ve, g_ve, gv)
                ve.reduce_sum(out=svec[:, 3:4], in_=swork[:, 16:24],
                              axis=AX).then_inc(g_ve, 1); ph.a(g_ve, 1)
                gv += 1
                ph.w(ve, g_ve, gv)
                ve.reciprocal(out=svec[:, 4:5], in_=svec[:, 3:4]).then_inc(g_ve, 1); ph.a(g_ve, 1)
                gv += 1
                ph.w(ve, g_ve, gv)
                ve.tensor_scalar(out=svec[:, 5:6], in0=svec[:, 4:5],
                                 scalar1=2.0, scalar2=None,
                                 op0=ALU.mult).then_inc(g_ve, 1); ph.a(g_ve, 1)
                gv += 1
                ph.w(ve, g_ve, gv)
                ve.tensor_scalar(out=swf[:, tt * 8:(tt + 1) * 8],
                                 in0=swork[:, 16:24],
                                 scalar1=svec[:, 5:6], scalar2=None,
                                 op0=ALU.mult).then_inc(g_ve, 1); ph.a(g_ve, 1)
                gv += 1; l_marks.append(gv)
            t_marks = []
            for tt in range(8):
                ph.w(ve, g_pe, 10 + 2 * (tt + 1))
                ve.tensor_copy(out=swfT[:, tt * 128:(tt + 1) * 128],
                               in_=yb[1][0:8, 0:128]).then_inc(g_ve, 1); ph.a(g_ve, 1)
                gv += 1; t_marks.append(gv)
            io["_p7_t"] = t_marks
            ph.w(ve, g_pe, 28)
            ve.tensor_copy(out=swfe[:, 0:512], in_=yb[2][:, :]).then_inc(g_ve, 1); ph.a(g_ve, 1)
            gv += 1
            ve.tensor_copy(out=swfe[:, 512:1024],
                           in_=yb[3][:, :]).then_inc(g_ve, 1); ph.a(g_ve, 1)
            gv += 1
            swfe_mark = gv
            ph.w(ve, g_pe, 30)
            ph.w(ve, g_ve, swfe_mark)
            ve.tensor_tensor(out=slw[:, 0:512], in0=yb[0][:, :],
                             in1=swfe[:, 0:512], op=ALU.mult).then_inc(g_ve, 1); ph.a(g_ve, 1)
            gv += 1
            ve.tensor_tensor(out=slw[:, 512:1024], in0=yb[1][:, :],
                             in1=swfe[:, 512:1024], op=ALU.mult).then_inc(g_ve, 1); ph.a(g_ve, 1)
            gv += 1
            io["_p7_done"] = gv
            for ob in range(NOB):
                for hh in range(2):
                    ph.w(ve, w_ld[ob % 2], 32 * (ob // 2) + 16 * (hh + 1))
                    if ob >= 2 and hh == 0:
                        ph.w(ve, p_mm, 2 * (ob - 1))
                    ve.tensor_copy(
                        out=swb[ob % 2][:, hh * 2048:(hh + 1) * 2048].rearrange(
                            "p (i c) -> p i c", i=16),
                        in_=landw[ob % 2][:, :, :]).then_inc(w_cv, 1); ph.a(w_cv, 1)
                for half in range(2):
                    hs = slice(half * 512, (half + 1) * 512)
                    ph.w(ve, a_ev, 2 * ob + half + 1)
                    ph.w(ve, x_ld[ob % 2], 16 * (ob // 2 + 1))
                    if ob >= 2:
                        ph.w(ve, p_pre[ob % 2], 16 * (ob // 2))
                        ph.w(ve, a_sq, ob - 1)
                        ph.w(ve, p_sd, ob - 1)
                    ve.tensor_tensor(out=st_t[ob % 2][:, hs],
                                     in0=sy1[ob % 2][:, hs],
                                     in1=sx[ob % 2][:, hs],
                                     op=ALU.add).then_inc(v_t, 1); ph.a(v_t, 1)
                    ph.w(ve, v_t, 2 * ob + half + 1)
                    ins = ve.tensor_copy(out=stb[ob % 2][:, hs],
                                         in_=st_t[ob % 2][:, hs])
                    if half == 1:
                        ins.then_inc(v_st, 1); ph.a(v_st, 1)
            ph.w(ve, p_sd, NOB)
            ve.tensor_copy(out=srow[:, 0:512], in_=ssum[0][0:1, :]).then_inc(cch, 1); ph.a(cch, 1)
            ve.tensor_copy(out=srow[:, 512:1024],
                           in_=ssum[1][0:1, :]).then_inc(cch, 1); ph.a(cch, 1)
            ve.tensor_copy(out=srow[:, 1024:1536], in_=ssq[0][:, :]).then_inc(cch, 1); ph.a(cch, 1)
            ve.tensor_copy(out=srow[:, 1536:2048], in_=ssq[1][:, :]).then_inc(cch, 1); ph.a(cch, 1)
            ph.w(ve, cch, 4)
            ve.tensor_scalar(out=srow[:, 2048:3072], in0=srow[:, 0:1024],
                             scalar1=1.0 / C, scalar2=None,
                             op0=ALU.mult).then_inc(cch, 1); ph.a(cch, 1)
            ve.tensor_scalar(out=srow[:, 3072:4096], in0=srow[:, 1024:2048],
                             scalar1=1.0 / C, scalar2=None,
                             op0=ALU.mult).then_inc(cch, 1); ph.a(cch, 1)
            ph.w(ve, cch, 6)
            ve.tensor_tensor(out=srow[:, 0:1024], in0=srow[:, 2048:3072],
                             in1=srow[:, 2048:3072], op=ALU.mult).then_inc(cch, 1); ph.a(cch, 1)
            ph.w(ve, cch, 7)
            ve.tensor_tensor(out=srow[:, 1024:2048], in0=srow[:, 3072:4096],
                             in1=srow[:, 0:1024],
                             op=ALU.subtract).then_inc(v_row, 1); ph.a(v_row, 1)
            ph.w(ve, a_row, 1)
            ve.reciprocal(out=srow[:, 3072:4096],
                          in_=srow[:, 0:1024]).then_inc(cch, 1); ph.a(cch, 1)
            ph.w(ve, cch, 8)
            ve.tensor_tensor(out=srow[:, 1024:2048], in0=srow[:, 2048:3072],
                             in1=srow[:, 3072:4096], op=ALU.mult).then_inc(cch, 1); ph.a(cch, 1)
            ph.w(ve, cch, 9)
            ve.tensor_scalar(out=srow[:, 1024:2048], in0=srow[:, 1024:2048],
                             scalar1=-1.0, scalar2=None,
                             op0=ALU.mult).then_inc(v_row, 1); ph.a(v_row, 1)
            ph.w(ve, p_bc, 1)
            ve.tensor_copy(out=sra[:, 0:512], in_=yb[0][:, :]).then_inc(cch, 1); ph.a(cch, 1)
            ve.tensor_copy(out=sra[:, 512:1024], in_=yb[1][:, :]).then_inc(cch, 1); ph.a(cch, 1)
            ve.tensor_copy(out=srb2[:, 0:512], in_=yb[2][:, :]).then_inc(cch, 1); ph.a(cch, 1)
            ve.tensor_copy(out=srb2[:, 512:1024],
                           in_=yb[3][:, :]).then_inc(v_bc, 1); ph.a(v_bc, 1)
            for ch in range(NOB):
                ph.w(ve, x2_ld[ch % 2], 16 * (ch // 2 + 1))
                ph.w(ve, v_bc, 1)
                ph.w(ve, cch, 12)
                if ch >= 2:
                    ph.w(ve, p_out[ch % 2], 16 * (ch // 2))
                ve.tensor_tensor(out=stage2[ch % 2][:, :],
                                 in0=land2[ch % 2][:, :],
                                 in1=sra[:, :], op=ALU.mult).then_inc(v_p2, 1); ph.a(v_p2, 1)
                ph.w(ve, v_p2, ch + 1)
                ve.tensor_tensor(out=stage2[ch % 2][:, :],
                                 in0=stage2[ch % 2][:, :],
                                 in1=srb2[:, :], op=ALU.add).then_inc(v_p3, 1); ph.a(v_p3, 1)

        @blk.scalar
        def _(sc):
            for tt in range(8):
                ph.w(sc, g_ve, io["_p7_b"][tt])
                if tt >= 1:
                    ph.w(sc, g_ve, io["_p7_h"][tt - 1])
                sc.activation(out=swork[:, 16:24],
                              in_=sgt[:, tt * 8:(tt + 1) * 8], func=AF.Exp,
                              bias=svec[:, 0:1]).then_inc(g_ac, 1); ph.a(g_ac, 1)
            for ob in range(NOB):
                for half in range(2):
                    hs = slice(half * 512, (half + 1) * 512)
                    ph.w(sc, p_mm, 2 * ob + half + 1)
                    if ob >= 2:
                        ph.w(sc, v_t, 2 * (ob - 1))
                    sc.activation(out=sy1[ob % 2][:, hs],
                                  in_=yb[(ob % 2) * 2 + half][:, :],
                                  func=AF.Identity, scale=1.0,
                                  bias=sbb[:, ob:ob + 1]).then_inc(a_ev, 1); ph.a(a_ev, 1)
                    ph.w(sc, v_t, 2 * ob + half + 1)
                    if ob >= 2:
                        ph.w(sc, p_sd, ob - 1)
                    ins = sc.activation(out=stsq[ob % 2][:, hs],
                                        in_=st_t[ob % 2][:, hs], func=AF.Square)
                    if half == 1:
                        ins.then_inc(a_sq, 1); ph.a(a_sq, 1)
            ph.w(sc, v_row, 1)
            ph.w(sc, ini, 1)
            sc.activation(out=srow[:, 0:1024], in_=srow[:, 1024:2048],
                          func=AF.Sqrt, bias=eps_t[:, 0:1]).then_inc(a_row, 1); ph.a(a_row, 1)

        def _stats7(te, o):
            ph.w(te, v_st, o + 1)
            ph.w(te, a_sq, o + 1)
            for half in range(2):
                hs = slice(half * 512, (half + 1) * 512)
                te.matmul(out=ssum[half][0:1, :], lhsT=sonesc[:, :],
                          rhs=stb[o % 2][:, hs], start=(o == 0), stop=(o == NOB - 1),
                          skip_group_check=True)
                ins = te.matmul(out=ssq[half][:, :], lhsT=sonesc[:, :],
                                rhs=stsq[o % 2][:, hs], start=(o == 0),
                                stop=(o == NOB - 1), skip_group_check=True)
            ins.then_inc(p_sd, 1); ph.a(p_sd, 1)

        @blk.tensor
        def _(te):
            ph.w(te, c_ld, 96)
            ph.w(te, pr_cv, 37)
            for half in range(2):
                for i in range(32):
                    ins = te.matmul(out=ssum[half][0:8, :],
                                    lhsT=sgw[:, i, :],
                                    rhs=sxb[:, i, half * 512:(half + 1) * 512],
                                    start=(i == 0), stop=(i == 31),
                                    skip_group_check=True)
            ins.then_inc(g_pe, 2); ph.a(g_pe, 2)
            for tt in range(8):
                ph.w(te, g_ve, 1)
                if tt >= 1:
                    ph.w(te, g_ve, io["_p7_a"][tt - 1])
                te.transpose(out=yb[0][:, 0:8],
                             in_=sgrow[0:8, tt * 128:(tt + 1) * 128],
                             identity=sid[0:8, 0:8]).then_inc(g_pe, 1); ph.a(g_pe, 1)
            for tt in range(8):
                ph.w(te, g_ve, io["_p7_l"][tt])
                if tt >= 1:
                    ph.w(te, g_ve, io["_p7_t"][tt - 1])
                te.transpose(out=yb[1][0:8, 0:128],
                             in_=swf[:, tt * 8:(tt + 1) * 8],
                             identity=sid[:, :]).then_inc(g_pe, 2); ph.a(g_pe, 2)
            ph.w(te, g_ve, io["_p7_t"][7])
            te.matmul(out=yb[2][:, :], lhsT=sexp8[:, :], rhs=swfT[0:8, 0:512],
                      start=True, stop=True)
            te.matmul(out=yb[3][:, :], lhsT=sexp8[:, :], rhs=swfT[0:8, 512:1024],
                      start=True, stop=True).then_inc(g_pe, 2); ph.a(g_pe, 2)
            for half in range(2):
                for i in range(32):
                    ins = te.matmul(out=yb[half][:, :], lhsT=sat[:, i, :],
                                    rhs=sxb[:, i, half * 512:(half + 1) * 512],
                                    start=(i == 0), stop=(i == 31))
            ins.then_inc(g_pe, 2); ph.a(g_pe, 2)
            ph.w(te, pr_cv, 41)
            ph.w(te, g_ve, io["_p7_done"])
            for ob in range(NOB):
                ph.w(te, w_cv, 2 * ob + 2)
                if ob >= 2:
                    ph.w(te, a_ev, 2 * (ob - 1))
                for half in range(2):
                    hs = slice(half * 512, (half + 1) * 512)
                    for cc in range(32):
                        te.matmul(
                            out=yb[(ob % 2) * 2 + half][:, :],
                            lhsT=swb[ob % 2][:, cc * 128:(cc + 1) * 128],
                            rhs=sxb[:, cc, hs], start=(cc == 0), stop=False)
                    te.matmul(out=yb[(ob % 2) * 2 + half][:, :],
                              lhsT=sbt[:, ob * 128:(ob + 1) * 128],
                              rhs=slw[:, hs],
                              start=False, stop=True).then_inc(p_mm, 1); ph.a(p_mm, 1)
                if ob >= 1:
                    _stats7(te, ob - 1)
            _stats7(te, NOB - 1)
            ph.w(te, v_row, 2)
            ph.w(te, a_ev, 2 * NOB)
            te.matmul(out=yb[0][:, :], lhsT=sonesr[:, :],
                      rhs=srow[:, 3072:3584], start=True, stop=True)
            te.matmul(out=yb[1][:, :], lhsT=sonesr[:, :],
                      rhs=srow[:, 3584:4096], start=True, stop=True)
            te.matmul(out=yb[2][:, :], lhsT=sonesr[:, :],
                      rhs=srow[:, 1024:1536], start=True, stop=True)
            te.matmul(out=yb[3][:, :], lhsT=sonesr[:, :],
                      rhs=srow[:, 1536:2048], start=True, stop=True).then_inc(p_bc, 1); ph.a(p_bc, 1)

        @blk.gpsimd
        def _(gp):
            for ob in range(NOB):
                ph.w(gp, v_t, 2 * ob + 2)
                ph.w(gp, a_sq, ob + 1)
                gp.dma_start(out=OUTPRE[ob * 128:(ob + 1) * 128, :],
                             in_=st_t[ob % 2][:, :]).then_inc(p_pre[ob % 2], 16); ph.a(p_pre[ob % 2], 16)
            for ch in range(NOB):
                ph.w(gp, v_p3, ch + 1)
                gp.dma_start(out=OUT[ch * 128:(ch + 1) * 128, :],
                             in_=stage2[ch % 2][:, :]).then_inc(p_out[ch % 2], 16); ph.a(p_out[ch % 2], 16)
            ph.w(gp, p_out[0], 16 * 16)
            ph.w(gp, p_out[1], 16 * 16)

    ph.close()


def build_launch2():
    nc = bass.Bass("TRN2", target_bir_lowering=False)
    io = {}
    io["ones_col"] = _dram(nc, "ones_col", [128, 1], BF16, "ExternalInput")
    io["ones_row"] = _dram(nc, "ones_row", [1, 128], F32, "ExternalInput")
    OT = _dram(nc, "OT", [C, TPC], BF16, "ExternalInput")
    wo_r = _dram(nc, "wo_r", [32, 32, 128, 128], F32, "ExternalInput")
    xres = _dram(nc, "xres", [C, TPC], F32, "ExternalInput")
    qw_r = _dram(nc, "qw_r", [32, 32, 128, 128], I32, "ExternalInput")
    qb_r = _dram(nc, "qb_r", [128, 32], I32, "ExternalInput")
    ws = _dram(nc, "ws", [1, 1], F32, "ExternalInput")
    wzp = _dram(nc, "wzp", [1, 1], I32, "ExternalInput")
    bs = _dram(nc, "bs", [1, 1], F32, "ExternalInput")
    bzp = _dram(nc, "bzp", [1, 1], I32, "ExternalInput")
    wq1_r = _dram(nc, "wq1_r", [32, 32, 128, 128], F32, "ExternalInput")
    wk1 = _dram(nc, "wk1", [C, 128], F32, "ExternalInput")
    wv1 = _dram(nc, "wv1", [C, 128], F32, "ExternalInput")
    X1 = _dram(nc, "X1", [C, TPC], F32, "Internal")
    X1p = _dram(nc, "X1p", [C, TPC], F32, "Internal")
    X2p = _dram(nc, "X2p", [C, TPC], F32, "Internal")
    X2 = _dram(nc, "X2", [C, TPC], F32, "ExternalOutput")
    QT1 = _dram(nc, "QT1", [C, TPC], BF16, "ExternalOutput")
    KT1c = _dram(nc, "KT1c", [128, TPC], BF16, "ExternalOutput")
    V1c = _dram(nc, "V1c", [TPC, 128], BF16, "ExternalOutput")
    emit_woln(nc, io, "p2", OT, wo_r, xres, X1, X1p)
    nc.all_engine_barrier(sem_only=True)
    emit_ffn0(nc, io, "p3", X1, qw_r, qb_r, ws, wzp, bs, bzp, X2, X2p)
    nc.all_engine_barrier(sem_only=True)
    emit_p4(nc, io, "p4", X2, wq1_r, wk1, wv1, QT1, KT1c, V1c)
    return nc


def build_launch3():
    nc = bass.Bass("TRN2", target_bir_lowering=False)
    io = {}
    io["ones_col"] = _dram(nc, "ones_col", [128, 1], BF16, "ExternalInput")
    io["ones_row"] = _dram(nc, "ones_row", [1, 128], F32, "ExternalInput")
    io["iota"] = _dram(nc, "iota", [128, 512], F32, "ExternalInput")
    io["ident"] = _dram(nc, "ident", [128, 128], F32, "ExternalInput")
    io["exp8"] = _dram(nc, "exp8", [8, 128], BF16, "ExternalInput")
    X2 = _dram(nc, "X2", [C, TPC], F32, "ExternalInput")
    QT1 = _dram(nc, "QT1", [C, TPC], BF16, "ExternalInput")
    KG1 = _dram(nc, "KG1", [128, T], BF16, "ExternalInput")
    VG1 = _dram(nc, "VG1", [T, 128], BF16, "ExternalInput")
    tthr = _dram(nc, "tthr", [128, 48], F32, "ExternalInput")
    wo1_r = _dram(nc, "wo1_r", [32, 32, 128, 128], F32, "ExternalInput")
    bw_r = _dram(nc, "bw_r", [32, 32, 128, 128], F32, "ExternalInput")
    gwT = _dram(nc, "gwT", [C, 8], F32, "ExternalInput")
    atall = _dram(nc, "atall", [C, 128], F32, "ExternalInput")
    btall = _dram(nc, "btall", [128, C], F32, "ExternalInput")
    bb_r = _dram(nc, "bb_r", [128, 32], F32, "ExternalInput")
    OT1 = _dram(nc, "OT1", [C, TPC], BF16, "Internal")
    X3 = _dram(nc, "X3", [C, TPC], F32, "Internal")
    X3p = _dram(nc, "X3p", [C, TPC], F32, "Internal")
    OUTp = _dram(nc, "OUTp", [C, TPC], F32, "Internal")
    OUT = _dram(nc, "OUT", [C, TPC], F32, "ExternalOutput")
    emit_p5(nc, io, "p5", QT1, KG1, VG1, tthr, OT1)
    nc.all_engine_barrier(sem_only=True)
    emit_woln(nc, io, "p6", OT1, wo1_r, X2, X3, X3p)
    nc.all_engine_barrier(sem_only=True)
    emit_ffn1(nc, io, "p7", X3, bw_r, gwT, atall, btall, bb_r, OUT, OUTp)
    return nc


_NC_CACHE = {}


def _get_nc(which):
    if which not in _NC_CACHE:
        _NC_CACHE[which] = {1: build_launch1, 2: build_launch2,
                            3: build_launch3, 9: build_fused}[which]()
    return _NC_CACHE[which]


def _prep_all(inputs):
    cst = _consts()
    cosT, sin2 = _rope_tables()
    x = np.asarray(inputs["L0_attn_x"])
    xT = [np.ascontiguousarray(x[b].T) for b in range(B)]
    wq_all = _wr_blocks(np.ascontiguousarray(np.asarray(inputs["L0_attn_wq"]).T))
    wk_all = _wr_blocks(np.ascontiguousarray(np.asarray(inputs["L0_attn_wk"]).T))
    wvT = np.ascontiguousarray(np.asarray(inputs["L0_attn_wv"]).T)
    wo_r = _wr_blocks(np.ascontiguousarray(np.asarray(inputs["L0_attn_wo"]).T))
    qw_r = np.ascontiguousarray(np.asarray(
        inputs["L0_ffn_quantized_weight"]).T.reshape(
            32, 128, 32, 128).transpose(2, 0, 1, 3))
    qb_r = np.ascontiguousarray(np.asarray(
        inputs["L0_ffn_quantized_bias"]).reshape(32, 128).T)
    wq1_r = np.ascontiguousarray(np.asarray(
        inputs["L1_attn_wq"]).reshape(32, 128, 32, 128).transpose(2, 0, 1, 3))
    wo1_r = _wr_blocks(np.ascontiguousarray(np.asarray(inputs["L1_attn_wo"])))
    bw_r = _wr_blocks(np.ascontiguousarray(
        np.asarray(inputs["L1_ffn_base_w"]).T))
    in_maps = []
    for c in range(NCORE):
        b, j = c // 4, c % 4
        tok = _tok_idx(j)
        in_maps.append({
            "xTb": xT[b],
            "wq_r": np.ascontiguousarray(wq_all[j * 8:(j + 1) * 8]),
            "wk_r": np.ascontiguousarray(wk_all[j * 8:(j + 1) * 8]),
            "wv_r": np.ascontiguousarray(wvT[:, j * 1024:(j + 1) * 1024]),
            "ropec": cosT, "ropes2": sin2,
            "tri": cst["tri"], "ones_col": cst["ones_col"],
            "ones_row": cst["ones_row_f32"], "iota": cst["iota"],
            "ident": cst["ident"], "exp8": cst["exp8"],
            "wo_r": wo_r,
            "xres": np.ascontiguousarray(x[b][tok, :].T),
            "qw_r": qw_r, "qb_r": qb_r,
            "ws": np.asarray(inputs["L0_ffn_weight_scale"],
                             dtype=np.float32).reshape(1, 1),
            "wzp": np.asarray(inputs["L0_ffn_weight_zero_point"],
                              dtype=np.int32).reshape(1, 1),
            "bs": np.asarray(inputs["L0_ffn_bias_scale"],
                             dtype=np.float32).reshape(1, 1),
            "bzp": np.asarray(inputs["L0_ffn_bias_zero_point"],
                              dtype=np.int32).reshape(1, 1),
            "wq1_r": wq1_r,
            "wk1": np.ascontiguousarray(np.asarray(inputs["L1_attn_wk"])),
            "wv1": np.ascontiguousarray(np.asarray(inputs["L1_attn_wv"])),
            "tthr": _tthr(j),
            "wo1_r": wo1_r, "bw_r": bw_r,
            "gwT": np.ascontiguousarray(np.asarray(inputs["L1_ffn_gate_w"]).T),
            "atall": np.ascontiguousarray(np.asarray(
                inputs["L1_ffn_lora_A"]).transpose(2, 0, 1).reshape(C, 128)),
            "btall": np.ascontiguousarray(np.asarray(
                inputs["L1_ffn_lora_B"]).transpose(0, 2, 1).reshape(128, C)),
            "bb_r": np.ascontiguousarray(np.asarray(
                inputs["L1_ffn_base_b"]).reshape(32, 128).T),
            "myj": np.array([[j]], dtype=np.uint32),
        })
    return in_maps


def kernel(**inputs):
    in_maps = _prep_all(inputs)
    res = run_bass_kernel_spmd(_get_nc(9), in_maps,
                               core_ids=list(range(NCORE))).results
    out = np.empty((B, T, C), dtype=np.float32)
    for c in range(NCORE):
        b, j = c // 4, c % 4
        out[b, _tok_idx(j), :] = np.asarray(res[c]["OUT"]).T
    return out


def kernel_unfused(**inputs):
    cst = _consts()
    cosT, sin2 = _rope_tables()
    x = np.asarray(inputs["L0_attn_x"])

    in_maps1 = prep_launch1(inputs)
    res1 = run_bass_kernel_spmd(_get_nc(1), in_maps1,
                                core_ids=list(range(NCORE))).results

    wo_r = _wr_blocks(np.ascontiguousarray(
        np.asarray(inputs["L0_attn_wo"]).T))
    qw_r = np.ascontiguousarray(np.asarray(
        inputs["L0_ffn_quantized_weight"]).T.reshape(
            32, 128, 32, 128).transpose(2, 0, 1, 3))
    qb_r = np.ascontiguousarray(np.asarray(
        inputs["L0_ffn_quantized_bias"]).reshape(32, 128).T)
    wq1_r = np.ascontiguousarray(np.asarray(
        inputs["L1_attn_wq"]).reshape(32, 128, 32, 128).transpose(2, 0, 1, 3))
    wk1 = np.ascontiguousarray(np.asarray(inputs["L1_attn_wk"]))
    wv1 = np.ascontiguousarray(np.asarray(inputs["L1_attn_wv"]))
    in_maps2 = []
    for c in range(NCORE):
        b, j = c // 4, c % 4
        tok = _tok_idx(j)
        OT = np.concatenate(
            [np.asarray(res1[b * 4 + g]["OTh"])[:, tok] for g in range(4)],
            axis=0)
        in_maps2.append({
            "ones_col": cst["ones_col"], "ones_row": cst["ones_row_f32"],
            "OT": np.ascontiguousarray(OT),
            "wo_r": wo_r,
            "xres": np.ascontiguousarray(x[b][tok, :].T),
            "qw_r": qw_r, "qb_r": qb_r,
            "ws": np.asarray(inputs["L0_ffn_weight_scale"],
                             dtype=np.float32).reshape(1, 1),
            "wzp": np.asarray(inputs["L0_ffn_weight_zero_point"],
                              dtype=np.int32).reshape(1, 1),
            "bs": np.asarray(inputs["L0_ffn_bias_scale"],
                             dtype=np.float32).reshape(1, 1),
            "bzp": np.asarray(inputs["L0_ffn_bias_zero_point"],
                              dtype=np.int32).reshape(1, 1),
            "wq1_r": wq1_r, "wk1": wk1, "wv1": wv1,
        })
    res2 = run_bass_kernel_spmd(_get_nc(2), in_maps2,
                                core_ids=list(range(NCORE))).results

    wo1_r = _wr_blocks(np.ascontiguousarray(np.asarray(inputs["L1_attn_wo"])))
    bw_r = _wr_blocks(np.ascontiguousarray(
        np.asarray(inputs["L1_ffn_base_w"]).T))
    gwT = np.ascontiguousarray(np.asarray(inputs["L1_ffn_gate_w"]).T)
    atall = np.ascontiguousarray(np.asarray(
        inputs["L1_ffn_lora_A"]).transpose(2, 0, 1).reshape(C, 128))
    btall = np.ascontiguousarray(np.asarray(
        inputs["L1_ffn_lora_B"]).transpose(0, 2, 1).reshape(128, C))
    bb_r = np.ascontiguousarray(np.asarray(
        inputs["L1_ffn_base_b"]).reshape(32, 128).T)
    KG = []
    VG = []
    for b in range(B):
        kg = np.empty((128, T), dtype=BF)
        vg = np.empty((T, 128), dtype=BF)
        for g in range(8):
            j = g if g < 4 else 7 - g
            half = 0 if g < 4 else 1
            kt = np.asarray(res2[b * 4 + j]["KT1c"])
            vt = np.asarray(res2[b * 4 + j]["V1c"])
            kg[:, g * 512:(g + 1) * 512] = kt[:, half * 512:(half + 1) * 512]
            vg[g * 512:(g + 1) * 512, :] = vt[half * 512:(half + 1) * 512, :]
        KG.append(kg)
        VG.append(vg)
    in_maps3 = []
    for c in range(NCORE):
        b, j = c // 4, c % 4
        in_maps3.append({
            "ones_col": cst["ones_col"], "ones_row": cst["ones_row_f32"],
            "iota": cst["iota"], "ident": cst["ident"], "exp8": cst["exp8"],
            "X2": np.asarray(res2[c]["X2"]),
            "QT1": np.asarray(res2[c]["QT1"]),
            "KG1": KG[b], "VG1": VG[b],
            "tthr": _tthr(j),
            "wo1_r": wo1_r, "bw_r": bw_r, "gwT": gwT,
            "atall": atall, "btall": btall, "bb_r": bb_r,
        })
    res3 = run_bass_kernel_spmd(_get_nc(3), in_maps3,
                                core_ids=list(range(NCORE))).results

    out = np.empty((B, T, C), dtype=np.float32)
    for c in range(NCORE):
        b, j = c // 4, c % 4
        out[b, _tok_idx(j), :] = np.asarray(res3[c]["OUT"]).T
    return out


RG4 = [[0, 1, 2, 3], [4, 5, 6, 7]]


def _phase_break(nc, pool):
    nc.all_engine_barrier(sem_only=True)
    pool.reset()


USE_POOL = [True]


def build_fused(upto=11):
    nc = bass.Bass("TRN2", target_bir_lowering=False, num_devices=8)
    pool = SemPool(nc, 60)
    io = {"pool": pool if USE_POOL[0] else None, "fused": True}
    io["xTb"] = _dram(nc, "xTb", [C, T], F32, "ExternalInput")
    io["wq_r"] = _dram(nc, "wq_r", [8, 32, 128, 128], F32, "ExternalInput")
    io["wk_r"] = _dram(nc, "wk_r", [8, 32, 128, 128], F32, "ExternalInput")
    io["wv_r"] = _dram(nc, "wv_r", [C, 1024], F32, "ExternalInput")
    io["ropec"] = _dram(nc, "ropec", [128, T], F32, "ExternalInput")
    io["ropes2"] = _dram(nc, "ropes2", [128, T], F32, "ExternalInput")
    io["tri"] = _dram(nc, "tri", [4, 128, 512], BF16, "ExternalInput")
    io["ones_col"] = _dram(nc, "ones_col", [128, 1], BF16, "ExternalInput")
    io["ones_row"] = _dram(nc, "ones_row", [1, 128], F32, "ExternalInput")
    io["iota"] = _dram(nc, "iota", [128, 512], F32, "ExternalInput")
    io["ident"] = _dram(nc, "ident", [128, 128], F32, "ExternalInput")
    io["exp8"] = _dram(nc, "exp8", [8, 128], BF16, "ExternalInput")
    wo_r = _dram(nc, "wo_r", [32, 32, 128, 128], F32, "ExternalInput")
    xres = _dram(nc, "xres", [C, TPC], F32, "ExternalInput")
    qw_r = _dram(nc, "qw_r", [32, 32, 128, 128], I32, "ExternalInput")
    qb_r = _dram(nc, "qb_r", [128, 32], I32, "ExternalInput")
    ws = _dram(nc, "ws", [1, 1], F32, "ExternalInput")
    wzp = _dram(nc, "wzp", [1, 1], I32, "ExternalInput")
    bs = _dram(nc, "bs", [1, 1], F32, "ExternalInput")
    bzp = _dram(nc, "bzp", [1, 1], I32, "ExternalInput")
    wq1_r = _dram(nc, "wq1_r", [32, 32, 128, 128], F32, "ExternalInput")
    wk1 = _dram(nc, "wk1", [C, 128], F32, "ExternalInput")
    wv1 = _dram(nc, "wv1", [C, 128], F32, "ExternalInput")
    tthr = _dram(nc, "tthr", [128, 48], F32, "ExternalInput")
    wo1_r = _dram(nc, "wo1_r", [32, 32, 128, 128], F32, "ExternalInput")
    bw_r = _dram(nc, "bw_r", [32, 32, 128, 128], F32, "ExternalInput")
    gwT = _dram(nc, "gwT", [C, 8], F32, "ExternalInput")
    atall = _dram(nc, "atall", [C, 128], F32, "ExternalInput")
    btall = _dram(nc, "btall", [128, C], F32, "ExternalInput")
    bb_r = _dram(nc, "bb_r", [128, 32], F32, "ExternalInput")
    myj = _dram(nc, "myj", [1, 1], mybir.dt.uint32, "ExternalInput")
    io["xbd"] = _dram(nc, "xbd", [C, T], BF16, "Internal")
    io["QT0"] = _dram(nc, "QT0", [8, 128, T], BF16, "Internal")
    io["KT0"] = _dram(nc, "KT0", [8, 128, T], BF16, "Internal")
    io["V0"] = _dram(nc, "V0", [8, T, 128], BF16, "Internal")
    OTC = _dram(nc, "OTC", [4, 1024, TPC], BF16, "Internal")
    io["OTh"] = OTC
    OTG = _dram(nc, "OTG", [4, 4, 1024, TPC], BF16, "Internal")
    X1 = _dram(nc, "X1", [C, TPC], F32, "Internal")
    X1p = _dram(nc, "X1p", [C, TPC], F32, "Internal")
    X2 = _dram(nc, "X2", [C, TPC], F32, "Internal")
    X2p = _dram(nc, "X2p", [C, TPC], F32, "Internal")
    QT1 = _dram(nc, "QT1", [C, TPC], BF16, "Internal")
    KT1c = _dram(nc, "KT1c", [128, TPC], BF16, "Internal")
    V1c = _dram(nc, "V1c", [TPC, 128], BF16, "Internal")
    KGg = _dram(nc, "KGg", [4, 128, TPC], BF16, "Internal")
    VGg = _dram(nc, "VGg", [4, TPC, 128], BF16, "Internal")
    OT1 = _dram(nc, "OT1", [C, TPC], BF16, "Internal")
    X3 = _dram(nc, "X3", [C, TPC], F32, "Internal")
    X3p = _dram(nc, "X3p", [C, TPC], F32, "Internal")
    OUTp = _dram(nc, "OUTp", [C, TPC], F32, "Internal")
    OUT = _dram(nc, "OUT", [C, TPC], F32, "ExternalOutput")
    OTme = _dram(nc, "OTme", [C, TPC], BF16, "Internal")

    emit_p0(nc, io)
    if upto <= 1:
        return nc
    _phase_break(nc, pool)
    emit_p0v(nc, io)
    if upto <= 2:
        return nc
    _phase_break(nc, pool)
    if upto == 31:
        io["fused"] = False
        io["OTh"] = _dram(nc, "OThf", [1024, T], BF16, "Internal")
        emit_p1(nc, io)
        io["fused"] = True
        io["OTh"] = OTC
        return nc
    emit_p1(nc, io)
    if upto <= 3:
        return nc
    _phase_break(nc, pool)

    from concourse.bass import ts as _ts
    ccs = pool.take()
    cb = pool.acc.get(ccs.num, 0)
    pool.acc[ccs.num] = cb + 65
    with nc.Block() as blk:
        @blk.gpsimd
        def _(gp):
            gp.collective_compute(
                "AllGather", ALU.bypass, replica_groups=RG4,
                ins=[OTC.rearrange("a b c -> (a b) c")],
                outs=[OTG.rearrange("r a b c -> (r a b) c")]).then_inc(ccs, 1)
            gp.wait_ge(ccs, cb + 1)
            pid = nc.partition_id(engines=[mybir.EngineType.Pool])
            jv = pid % 4
            for pc in range(4):
                gp.dma_start(out=OTme[pc * 1024:(pc + 1) * 1024, :],
                             in_=OTG[pc, _ts(jv, 1), :, :]).then_inc(ccs, 16)
            gp.wait_ge(ccs, cb + 1 + 64)
    if upto <= 4:
        return nc
    _phase_break(nc, pool)

    emit_woln(nc, io, "p2", OTme, wo_r, xres, X1, X1p)
    if upto <= 5:
        return nc
    _phase_break(nc, pool)
    emit_ffn0(nc, io, "p3", X1, qw_r, qb_r, ws, wzp, bs, bzp, X2, X2p)
    if upto <= 6:
        return nc
    _phase_break(nc, pool)
    emit_p4(nc, io, "p4", X2, wq1_r, wk1, wv1, QT1, KT1c, V1c)
    if upto <= 7:
        return nc
    _phase_break(nc, pool)

    ccs2 = pool.take()
    cb2 = pool.acc.get(ccs2.num, 0)
    pool.acc[ccs2.num] = cb2 + 2
    with nc.Block() as blk:
        @blk.gpsimd
        def _(gp):
            gp.collective_compute(
                "AllGather", ALU.bypass, replica_groups=RG4,
                ins=[KT1c[:, :]],
                outs=[KGg.rearrange("a b c -> (a b) c")]).then_inc(ccs2, 1)
            gp.collective_compute(
                "AllGather", ALU.bypass, replica_groups=RG4,
                ins=[V1c[:, :]],
                outs=[VGg.rearrange("a b c -> (a b) c")]).then_inc(ccs2, 1)
            gp.wait_ge(ccs2, cb2 + 2)
    if upto <= 8:
        return nc
    _phase_break(nc, pool)

    emit_p5(nc, io, "p5", QT1, KGg, VGg, tthr, OT1)
    if upto <= 9:
        return nc
    _phase_break(nc, pool)
    emit_woln(nc, io, "p6", OT1, wo1_r, X2, X3, X3p)
    if upto <= 10:
        return nc
    _phase_break(nc, pool)
    emit_ffn1(nc, io, "p7", X3, bw_r, gwT, atall, btall, bb_r, OUT, OUTp)
    return nc
